# version 02
# baseline (speedup 1.0000x reference)
import math
from contextlib import ExitStack
import numpy as np
import concourse.bass as bass
import concourse.mybir as mybir
from concourse.bass_utils import run_bass_kernel_spmd




F32 = mybir.dt.float32
BF16 = mybir.dt.bfloat16
AF = mybir.ActivationFunctionType
ALU = mybir.AluOpType
AX = mybir.AxisListType

ENGINES = ("tensor", "vector", "scalar", "gpsimd", "sync")
U8 = mybir.dt.uint8


def _dtsize(dt):
    return mybir.dt.size(dt)


class Arena:

    def __init__(self, bk, name, nbytes):
        self.bk = bk
        self.name = name
        self.base_t = bk.sb(name, [128, nbytes], U8)
        self.addr = bk.nc.lookup_mloc(self.base_t).addr
        self.nbytes = nbytes
        self.cur = 0
        self.n = 0

    def reset(self):
        self.cur = 0

    def alloc(self, name, shape, dtype):
        per = 1
        for d in shape[1:]:
            per *= d
        per *= _dtsize(dtype)
        off = (self.cur + 31) // 32 * 32
        assert off + per <= self.nbytes, (
            f"arena {self.name} overflow at {name}: {off}+{per} > {self.nbytes}")
        self.cur = off + per
        self.n += 1
        t = self.bk.nc.alloc_sbuf_tensor_at(
            f"{self.name}_{self.n}_{name}", shape, dtype, offset=self.addr + off)
        return t


class Slot:
    __slots__ = ("name", "writer", "readers")

    def __init__(self, name=""):
        self.name = name
        self.writer = None
        self.readers = []


class Sem:
    __slots__ = ("name", "handle", "count")

    def __init__(self, name, handle):
        self.name = name
        self.handle = handle
        self.count = 0


class BK:
    def __init__(self, nc):
        self.nc = nc
        self.stack = ExitStack()
        self.streams = {e: [] for e in ENGINES}
        self.watermark = {e: {} for e in ENGINES}
        self.sems = {}
        self.eng_sem = {}
        self.phase = 0
        self._pe_pending_reads = []
        self._pe_pending_writes = []
        self.new_phase()

    def sem(self, name):
        s = self.sems.get(name)
        if s is None:
            h = self.stack.enter_context(self.nc.semaphore(name))
            s = Sem(name, h)
            self.sems[name] = s
        return s

    def new_phase(self):
        self.phase += 1
        for e in ENGINES:
            self.eng_sem[e] = self.sem(f"ph{self.phase}_{e}")

    def sb(self, name, shape, dtype):
        t = self.stack.enter_context(self.nc.sbuf_tensor(name, shape, dtype))
        return t

    def ps(self, name, shape, dtype=F32):
        t = self.stack.enter_context(self.nc.psum_tensor(name, shape, dtype))
        return t

    def _collect(self, engine, reads, writes):
        need = {}

        def add(dep):
            if dep is None:
                return
            s, c, peng = dep
            if engine == "tensor" and peng == "tensor":
                return
            if need.get(s, 0) < c:
                need[s] = c

        for sl in reads:
            add(sl.writer)
        for sl in writes:
            add(sl.writer)
            for r in sl.readers:
                add(r)
        wm = self.watermark[engine]
        waits = []
        for s, c in need.items():
            if wm.get(s, 0) < c:
                waits.append((s, c))
                wm[s] = c
        return waits

    def emit(self, engine, fn, reads=(), writes=(), dma=None, tag=""):
        waits = self._collect(engine, reads, writes)
        if dma is not None:
            msem, inc = dma, 16
        else:
            msem, inc = self.eng_sem[engine], 1
        msem.count += inc
        marker = (msem, msem.count, engine)

        def op(eng, fn=fn, waits=tuple(waits), msem=msem, inc=inc, tag=tag):
            for s, c in waits:
                eng.wait_ge(s.handle, c)
            inst = fn(eng)
            inst.then_inc(msem.handle, inc)

        self.streams[engine].append(op)
        for sl in writes:
            sl.writer = marker
            sl.readers = []
        for sl in reads:
            sl.readers.append(marker)
        return marker

    def mm(self, fn, reads=(), psum=None, first=True, last=True, tag=""):
        eff_writes = (psum,) if (first and psum is not None) else ()
        waits = self._collect("tensor", reads, eff_writes)
        msem = self.eng_sem["tensor"]
        msem.count += 1
        marker = (msem, msem.count, "tensor")

        def op(eng, fn=fn, waits=tuple(waits), msem=msem, tag=tag):
            for s, c in waits:
                eng.wait_ge(s.handle, c)
            inst = fn(eng)
            inst.then_inc(msem.handle, 1)

        self.streams["tensor"].append(op)
        for sl in reads:
            sl.readers.append(marker)
        if psum is not None:
            if first:
                psum.readers = []
            if last:
                psum.writer = marker
        return marker

    def barrier(self):
        pairs = [(s, s.count) for s in self.sems.values() if s.count > 0]
        bsem = self.eng_sem["sync"]
        bsem.count += 1
        bcount = bsem.count

        def op_sync(eng, pairs=tuple(pairs), bsem=bsem):
            for s, c in pairs:
                eng.wait_ge(s.handle, c)
            eng.nop().then_inc(bsem.handle, 1)

        self.streams["sync"].append(op_sync)
        wm = self.watermark["sync"]
        for s, c in pairs:
            wm[s] = max(wm.get(s, 0), c)
        for e in ENGINES:
            if e == "sync":
                continue

            def op(eng, bsem=bsem, bcount=bcount):
                eng.wait_ge(bsem.handle, bcount)

            self.streams[e].append(op)
            wme = self.watermark[e]
            wme[bsem] = max(wme.get(bsem, 0), bcount)
            for s, c in pairs:
                wme[s] = max(wme.get(s, 0), c)

    def seal(self, sem, slots):
        for sl in slots:
            if sl.writer is not None and sl.writer[0] is sem:
                sl.writer = (sem, sem.count, sl.writer[2])
            sl.readers = [(s, sem.count if s is sem else c, e2)
                          for (s, c, e2) in sl.readers]

    def wait_all(self, engine, sems):
        pairs = [(s, s.count) for s in sems]

        def op(eng, pairs=tuple(pairs)):
            for s, c in pairs:
                eng.wait_ge(s.handle, c)

        self.streams[engine].append(op)

    def barrier_wait(self, engine, slots):
        waits = self._collect(engine, slots, ())

        def op(eng, waits=tuple(waits)):
            for s, c in waits:
                eng.wait_ge(s.handle, c)

        self.streams[engine].append(op)

    def finalize(self):
        nc = self.nc
        with nc.Block() as blk:
            for ename in ENGINES:
                ops = self.streams[ename]
                getattr(blk, ename)(lambda eng, ops=ops: [op(eng) for op in ops])
        self.stack.close()
        return nc



T = 2048
C = 2048
NT = 16
NCH = 16
HF = 8192
LNEPS = 1e-5
TCW = 1024
NTCOL = T // TCW


def configure(t=2048, c=2048, hf=8192, tcw=1024):
    global T, C, NT, NCH, HF, TCW, NTCOL
    T, C, HF, TCW = t, c, hf, tcw
    NT, NCH, NTCOL = T // 128, C // 128, T // TCW


def S(n=""):
    return Slot(n)


class Common:

    def __init__(self, nc, bk):
        self.nc, self.bk = nc, bk
        sb = bk.sb
        self.eps = sb("eps", [128, 1], F32)
        self.s_eps = S()
        bk.emit("vector", lambda e: e.memset(self.eps[:], LNEPS), writes=[self.s_eps])
        self.onesr = sb("onesr", [1, 128], BF16)
        self.s_onesr = S()
        bk.emit("vector", lambda e: e.memset(self.onesr[:], 1.0), writes=[self.s_onesr])
        self.onesrf = sb("onesrf", [1, 128], F32)
        self.s_onesrf = S()
        bk.emit("vector", lambda e: e.memset(self.onesrf[:], 1.0), writes=[self.s_onesrf])
        self.onecol = sb("onecol", [128, 1], BF16)
        self.s_onecol = S()
        bk.emit("vector", lambda e: e.memset(self.onecol[:], 1.0), writes=[self.s_onecol])
        self.identf = sb("identf", [128, 128], F32)
        self.s_identf = S()
        bk.emit("vector", lambda e: e.memset(self.identf[:], 1.0), writes=[self.s_identf])
        bk.emit("gpsimd", lambda e: e.affine_select(
            out=self.identf[:], in_=self.identf[:], pattern=[[1, 128]],
            compare_op=ALU.is_equal, fill=0.0, base=0, channel_multiplier=-1),
            reads=[self.s_identf], writes=[self.s_identf])
        self.lx = [sb(f"lnx{j}", [128, C], F32) for j in range(2)]
        self.s_lx = [S() for _ in range(2)]
        self.zt = [sb(f"lnz{j}", [128, C], BF16) for j in range(2)]
        self.s_zt = [S() for _ in range(2)]
        self.stat = [[sb(f"lnst{j}_{k}", [128, 1], F32) for k in range(5)] for j in range(2)]
        self.s_stat = [[S() for _ in range(5)] for _ in range(2)]
        self.d_lx = [bk.sem(f"c_lx{j}") for j in range(2)]
        self.d_zs = [bk.sem(f"c_zs{j}") for j in range(2)]
        self.xa = [sb(f"xa{i}", [128, C], F32) for i in range(TCW // 128)]
        self.s_xa = [S() for _ in range(TCW // 128)]
        self.d_xa = [bk.sem(f"c_xa{j}") for j in range(2)]
        self.d_xo = [bk.sem(f"c_xo{j}") for j in range(2)]

    def col_table(self, name, src_1d, n, dsem):
        bk = self.bk
        assert n <= 128
        row = bk.sb(name + "_r", [n, 128], F32)
        out = bk.sb(name, [128, n], F32)
        pst = bk.ps(name + "_ps", [128, n], F32)
        s_row, s_out, s_ps = S(), S(), S()
        bk.emit("sync", lambda e: e.dma_start(
            out=row[:], in_=src_1d.rearrange("(a b) -> a b", b=128)),
            writes=[s_row], dma=dsem)
        bk.seal(dsem, [s_row])
        bk.mm(lambda e: e.transpose(pst[:], row[:], self.identf[0:n, 0:n]),
              reads=[s_row, self.s_identf], psum=s_ps)
        bk.emit("vector", lambda e: e.tensor_copy(out=out[:], in_=pst[:]),
                reads=[s_ps], writes=[s_out])
        return out, s_out

    def ln_pass(self, x_d, zscr_d, s_zscr):
        bk = self.bk
        for i in range(NT):
            j = i % 2
            st, s_st = self.stat[j], self.s_stat[j]
            x, s_x = self.lx[j], self.s_lx[j]
            bk.emit("sync", lambda e, i=i, x=x: e.dma_start(
                out=x[:], in_=x_d[i * 128:(i + 1) * 128, :]),
                writes=[s_x], dma=self.d_lx[j])
            bk.emit("vector", lambda e, x=x, st=st: e.reduce_sum(out=st[0][:], in_=x[:], axis=AX.X),
                    reads=[s_x], writes=[s_st[0]])
            bk.emit("scalar", lambda e, x=x, st=st, j=j: e.activation(
                out=self.zt[j][:], in_=x[:], func=AF.Square, accum_out=st[1][:]),
                reads=[s_x], writes=[self.s_zt[j], s_st[1]])
            bk.emit("vector", lambda e, st=st: e.tensor_scalar(out=st[2][:], in0=st[0][:],
                    scalar1=-1.0 / C, scalar2=None, op0=ALU.mult),
                    reads=[s_st[0]], writes=[s_st[2]])
            bk.emit("vector", lambda e, st=st: e.tensor_tensor(out=st[3][:], in0=st[2][:],
                    in1=st[2][:], op=ALU.mult), reads=[s_st[2]], writes=[s_st[3]])
            bk.emit("vector", lambda e, st=st: e.scalar_tensor_tensor(out=st[3][:], in0=st[1][:],
                    scalar=1.0 / C, in1=st[3][:], op0=ALU.mult, op1=ALU.subtract),
                    reads=[s_st[1], s_st[3]], writes=[s_st[3]])
            bk.emit("scalar", lambda e, st=st: e.activation(out=st[3][:], in_=st[3][:],
                    func=AF.Sqrt, bias=self.eps[:, 0:1]),
                    reads=[s_st[3], self.s_eps], writes=[s_st[3]])
            bk.emit("vector", lambda e, st=st: e.reciprocal(out=st[4][:], in_=st[3][:]),
                    reads=[s_st[3]], writes=[s_st[4]])
            bk.emit("vector", lambda e, x=x, st=st, j=j: e.tensor_scalar(out=self.zt[j][:],
                    in0=x[:], scalar1=st[2][:, 0:1], scalar2=st[4][:, 0:1],
                    op0=ALU.add, op1=ALU.mult),
                    reads=[s_x, s_st[2], s_st[4]], writes=[self.s_zt[j]])
            bk.emit("scalar", lambda e, i=i, j=j: e.dma_start(
                out=zscr_d[i * 128:(i + 1) * 128, :], in_=self.zt[j][:]),
                reads=[self.s_zt[j]], writes=[s_zscr[j]], dma=self.d_zs[j])
        for j in range(2):
            bk.seal(self.d_zs[j], [s_zscr[j]])

    def load_xacc(self, x_d, tc):
        bk = self.bk
        sem = self.d_xa[tc % 2]
        for k in range(TCW // 128):
            i = tc * (TCW // 128) + k
            bk.emit("sync", lambda e, i=i, k=k: e.dma_start(
                out=self.xa[k][:], in_=x_d[i * 128:(i + 1) * 128, :]),
                writes=[self.s_xa[k]], dma=sem)
        bk.seal(sem, self.s_xa)

    def store_xacc(self, xo_d, tc):
        bk = self.bk
        for k in range(TCW // 128):
            i = tc * (TCW // 128) + k
            bk.emit("sync", lambda e, i=i, k=k: e.dma_start(
                out=xo_d[i * 128:(i + 1) * 128, :], in_=self.xa[k][:]),
                reads=[self.s_xa[k]], dma=self.d_xo[tc % 2])
        bk.seal(self.d_xo[tc % 2], self.s_xa)

    def finish_x(self):
        self.bk.wait_all("sync", self.d_xo)



def emit_l0ffn(nc, bk, cm, W, zscr_d, x_d, xo_d, dbg=None):
    sb, ps = bk.sb, bk.ps
    s_zscr = [S("zscr0"), S("zscr1")]

    s_upb = [S() for _ in range(4)]
    s_dnb = [S() for _ in range(4)]
    for p in range(4):
        w = C // 4
        bk.emit("gpsimd", lambda e, p=p, w=w: e.dma_start(
            out=W["upb"][:, p * w:(p + 1) * w], in_=W["up"][:, p * w:(p + 1) * w]),
            writes=[s_upb[p]], dma=bk.sem(f"f_cup{p}"))
    for p in range(4):
        w = HF // 4
        bk.emit("gpsimd", lambda e, p=p, w=w: e.dma_start(
            out=W["dnb"][:, p * w:(p + 1) * w], in_=W["dn"][:, p * w:(p + 1) * w]),
            writes=[s_dnb[p]], dma=bk.sem(f"f_cdn{p}"))

    cm.ln_pass(x_d, zscr_d, s_zscr)

    usc, s_usc = cm.col_table("usc", W["us"], HF // 128, bk.sem("f_usc"))

    dsrow = sb("dsrow", [1, C], F32)
    dsrow_b = sb("dsrow_b", [1, C], BF16)
    osct = sb("osct", [1, 1], F32)
    dsb = sb("dsb", [128, C], BF16)
    ps_b = ps("ps_b", [128, 512], F32)
    s_dsrow, s_dsrow_b, s_osc, s_dsb, s_psb = S(), S(), S(), S(), S()
    d_ds = bk.sem("f_ds")
    bk.emit("sync", lambda e: e.dma_start(out=dsrow[:],
            in_=W["ds"].rearrange("(b a) -> b a", b=1)), writes=[s_dsrow], dma=d_ds)
    bk.emit("sync", lambda e: e.dma_start(out=osct[:],
            in_=W["os"].rearrange("(b a) -> b a", b=1)), writes=[s_osc], dma=d_ds)
    bk.seal(d_ds, [s_dsrow, s_osc])
    bk.emit("vector", lambda e: e.tensor_scalar(out=dsrow[:], in0=dsrow[:],
            scalar1=osct[0:1, 0:1], scalar2=0.5, op0=ALU.mult, op1=ALU.mult),
            reads=[s_dsrow, s_osc], writes=[s_dsrow])
    bk.emit("vector", lambda e: e.tensor_copy(out=dsrow_b[:], in_=dsrow[:]),
            reads=[s_dsrow], writes=[s_dsrow_b])
    for q in range(C // 512 if C >= 512 else 1):
        bk.mm(lambda e, q=q: e.matmul(ps_b[:], cm.onesr[:], dsrow_b[:, q * 512:(q + 1) * 512] if C >= 512 else dsrow_b[:],
              start=True, stop=True), reads=[cm.s_onesr, s_dsrow_b], psum=s_psb)
        bk.emit("vector", lambda e, q=q: e.tensor_copy(out=dsb[:, q * 512:(q + 1) * 512],
                in_=ps_b[:]), reads=[s_psb], writes=[s_dsb])

    upT = [sb(f"fupT{c}", [128, 512], BF16) for c in range(NCH)]
    s_upT = [S() for _ in range(NCH)]
    dnT = [sb(f"fdnT{h}", [128, C], BF16) for h in range(4)]
    s_dnT = [S() for _ in range(4)]
    zT = [sb(f"fzT{c}", [128, TCW], BF16) for c in range(NCH)]
    s_zT = [S() for _ in range(NCH)]
    hidT = [[sb(f"fhid{p}_{h}", [128, TCW], BF16) for h in range(4)] for p in range(2)]
    s_hid = [[S() for _ in range(4)] for _ in range(2)]
    psh = [ps(f"fpsh{h}", [128, 512], F32) for h in range(4)]
    s_psh = [S() for _ in range(4)]
    pso = [ps(f"fpso{o}", [128, 512], F32) for o in range(2)]
    s_pso = [S() for _ in range(2)]
    d_up = [bk.sem(f"f_up{j}") for j in range(2)]
    d_dn = [bk.sem(f"f_dn{j}") for j in range(2)]
    d_zt = bk.sem("f_zt")

    if dbg is not None:
        d_dbg = bk.sem("f_dbg")
        bk.emit("gpsimd", lambda e: e.dma_start(out=dbg["z"], in_=zscr_d),
                reads=s_zscr, dma=d_dbg)
        bk.emit("sync", lambda e: e.dma_start(out=dbg["usc"], in_=usc[:]),
                reads=[s_usc], dma=d_dbg)
        bk.emit("sync", lambda e: e.dma_start(out=dbg["dsb"], in_=dsb[:]),
                reads=[s_dsb], dma=d_dbg)
    NHP = HF // 512
    for tc in range(NTCOL):
        cm.load_xacc(x_d, tc)
        for c in range(NCH):
            bk.emit("scalar", lambda e, c=c, tc=tc: e.dma_start_transpose(
                out=zT[c][:], in_=zscr_d[tc * TCW:(tc + 1) * TCW, c * 128:(c + 1) * 128]),
                reads=s_zscr, writes=[s_zT[c]], dma=d_zt)
        bk.seal(d_zt, s_zT)
        for hp in range(NHP):
            par = hp % 2
            sem = d_up[par]
            for c in range(NCH):
                cp = s_upb[(c * 128) // (C // 4)]
                bk.emit("scalar", lambda e, c=c, hp=hp: e.dma_start_transpose(
                    out=upT[c][:], in_=W["upb"][hp * 512:(hp + 1) * 512, c * 128:(c + 1) * 128]),
                    reads=[cp], writes=[s_upT[c]], dma=sem)
            bk.seal(sem, s_upT)
            for c in range(NCH):
                bk.emit("scalar", lambda e, c=c: e.sign(out=upT[c][:], in_=upT[c][:]),
                        reads=[s_upT[c]], writes=[s_upT[c]])
            sem = d_dn[par]
            for hc in range(4):
                col = hp * 4 + hc
                cp = s_dnb[(col * 128) // (HF // 4)]
                bk.emit("scalar", lambda e, hc=hc, col=col: e.dma_start_transpose(
                    out=dnT[hc][:], in_=W["dnb"][:, col * 128:(col + 1) * 128]),
                    reads=[cp], writes=[s_dnT[hc]], dma=sem)
            bk.seal(sem, s_dnT)
            for hc in range(4):
                bk.emit("scalar", lambda e, hc=hc: e.sign(out=dnT[hc][:], in_=dnT[hc][:]),
                        reads=[s_dnT[hc]], writes=[s_dnT[hc]])
                bk.emit("vector", lambda e, hc=hc: e.tensor_tensor(out=dnT[hc][:],
                        in0=dnT[hc][:], in1=dsb[:], op=ALU.mult),
                        reads=[s_dnT[hc], s_dsb], writes=[s_dnT[hc]])
            for hb in range(4):
                for th in range(TCW // 512):
                    pp = (hb * (TCW // 512) + th) % 4
                    for c in range(NCH):
                        bk.mm(lambda e, hb=hb, th=th, c=c, pp=pp: e.matmul(
                            psh[pp][:], upT[c][:, hb * 128:(hb + 1) * 128],
                            zT[c][:, th * 512:(th + 1) * 512],
                            start=(c == 0), stop=(c == NCH - 1)),
                            reads=[s_upT[c], s_zT[c]], psum=s_psh[pp],
                            first=(c == 0), last=(c == NCH - 1))
                    hcol = hp * 4 + hb
                    bk.emit("scalar", lambda e, hb=hb, th=th, par=par, hcol=hcol, pp=pp:
                            e.activation(out=hidT[par][hb][:, th * 512:(th + 1) * 512],
                            in_=psh[pp][:], func=AF.Relu, scale=usc[:, hcol:hcol + 1]),
                            reads=[s_psh[pp], s_usc], writes=[s_hid[par][hb]])
            if dbg is not None and tc == 0 and hp == 0:
                for c in range(NCH):
                    bk.emit("sync", lambda e, c=c: e.dma_start(
                        out=dbg["upT"][c * 128:(c + 1) * 128, :], in_=upT[c][:]),
                        reads=[s_upT[c]], dma=d_dbg)
                for hb in range(4):
                    bk.emit("sync", lambda e, hb=hb: e.dma_start(
                        out=dbg["hid"][hb * 128:(hb + 1) * 128, :], in_=hidT[0][hb][:]),
                        reads=[s_hid[0][hb]], dma=d_dbg)
            for tt in range(TCW // 128):
                for ccol in range(C // 512):
                    o = (tt * 4 + ccol) % 2
                    for hc in range(4):
                        bk.mm(lambda e, hc=hc, tt=tt, ccol=ccol, par=par, o=o: e.matmul(
                            pso[o][:], hidT[par][hc][:, tt * 128:(tt + 1) * 128],
                            dnT[hc][:, ccol * 512:(ccol + 1) * 512],
                            start=(hc == 0), stop=(hc == 3)),
                            reads=[s_hid[par][hc], s_dnT[hc]], psum=s_pso[o],
                            first=(hc == 0), last=(hc == 3))
                    bk.emit("vector", lambda e, tt=tt, ccol=ccol, o=o: e.tensor_add(
                        out=cm.xa[tt][:, ccol * 512:(ccol + 1) * 512],
                        in0=cm.xa[tt][:, ccol * 512:(ccol + 1) * 512], in1=pso[o][:]),
                        reads=[s_pso[o], cm.s_xa[tt]], writes=[cm.s_xa[tt]])
        cm.store_xacc(xo_d, tc)
    cm.finish_x()





def S(n=""):
    return Slot(n)


def emit_l1moe(nc, bk, cm, W, zscr_d, x_d, xo_d):
    sb, ps = bk.sb, bk.ps
    s_zscr = [S("mz0"), S("mz1")]
    NE = 8
    NTT = TCW // 128

    s_cast = [S(f"cast{e}") for e in range(NE)]
    for e in range(NE):
        sem = bk.sem(f"m_cast{e}")
        srcs = ([W["W1"][e], W["Wg"][e], W["W2"][e]] if e < 7
                else [W["Ws1"], W["Wsg"], W["Ws2"]])
        for k, nm in enumerate(("W1b", "Wgb", "W2b")):
            bk.emit("gpsimd", lambda e2, e=e, k=k, nm=nm, srcs=srcs: e2.dma_start(
                out=W[nm][e], in_=srcs[k]), writes=[s_cast[e]], dma=sem)
        bk.seal(sem, [s_cast[e]])

    cm.ln_pass(x_d, zscr_d, s_zscr)

    gw = [sb(f"m_gw{c}", [128, 7], F32) for c in range(NCH)]
    gwb = [sb(f"m_gwb{c}", [128, 7], BF16) for c in range(NCH)]
    s_gw = [S() for _ in range(NCH)]
    s_gwb = [S() for _ in range(NCH)]
    d_gw = bk.sem("m_gw")
    gw3 = W["gate_w"].rearrange("(c p) g -> c p g", p=128)
    for c in range(NCH):
        bk.emit("sync", lambda e, c=c: e.dma_start(out=gw[c][:], in_=gw3[c]),
                writes=[s_gw[c]], dma=d_gw)
    bk.seal(d_gw, s_gw)
    for c in range(NCH):
        bk.emit("vector", lambda e, c=c: e.tensor_copy(out=gwb[c][:], in_=gw[c][:]),
                reads=[s_gw[c]], writes=[s_gwb[c]])
    gbrow = sb("m_gbrow", [1, 7], F32)
    birow = sb("m_birow", [1, 7], F32)
    gbb = sb("m_gbb", [128, 7], F32)
    bib = sb("m_bib", [128, 7], F32)
    ps_g = ps("m_psg", [128, 8], F32)
    s_gbrow, s_birow, s_gbb, s_bib, s_psg = S(), S(), S(), S(), S()
    d_gb = bk.sem("m_gb")
    bk.emit("sync", lambda e: e.dma_start(out=gbrow[:],
            in_=W["gate_b"].rearrange("(b a) -> b a", b=1)), writes=[s_gbrow], dma=d_gb)
    bk.emit("sync", lambda e: e.dma_start(out=birow[:],
            in_=W["biases"].rearrange("(b a) -> b a", b=1)), writes=[s_birow], dma=d_gb)
    bk.seal(d_gb, [s_gbrow, s_birow])
    bk.mm(lambda e: e.matmul(ps_g[:, 0:7], cm.onesrf[:], gbrow[:], start=True, stop=True),
          reads=[cm.s_onesrf, s_gbrow], psum=s_psg)
    bk.emit("vector", lambda e: e.tensor_copy(out=gbb[:], in_=ps_g[:, 0:7]),
            reads=[s_psg], writes=[s_gbb])
    bk.mm(lambda e: e.matmul(ps_g[:, 0:7], cm.onesrf[:], birow[:], start=True, stop=True),
          reads=[cm.s_onesrf, s_birow], psum=s_psg)
    bk.emit("vector", lambda e: e.tensor_copy(out=bib[:], in_=ps_g[:, 0:7]),
            reads=[s_psg], writes=[s_bib])

    zT = [sb(f"m_zT{c}", [128, TCW], BF16) for c in range(NCH)]
    s_zT = [S() for _ in range(NCH)]
    slab1 = [sb(f"m_sl1_{c}", [128, 512], BF16) for c in range(NCH)]
    slabg = [sb(f"m_slg_{c}", [128, 512], BF16) for c in range(NCH)]
    slab2 = [sb(f"m_sl2_{h}", [128, C], BF16) for h in range(4)]
    s_slab1 = [S() for _ in range(NCH)]
    s_slabg = [S() for _ in range(NCH)]
    s_slab2 = [S() for _ in range(4)]
    s1t = [sb(f"m_s1t{h}", [128, TCW], BF16) for h in range(4)]
    hh = [sb(f"m_hh{h}", [128, TCW], BF16) for h in range(4)]
    s_s1t = [S() for _ in range(4)]
    s_hh = [S() for _ in range(4)]
    wcol = [sb(f"m_w{t}", [128, 8], F32) for t in range(NTT)]
    s_wcol = [S() for _ in range(NTT)]
    glog = sb("m_glog", [128, 8], F32)
    gtmp = sb("m_gtmp", [128, 8], F32)
    gsel = sb("m_gsel", [128, 8], F32)
    gt2 = sb("m_gt2", [128, 8], F32)
    gcol = [sb(f"m_gc{k}", [128, 1], F32) for k in range(3)]
    s_glog, s_gtmp, s_gsel, s_gt2 = S(), S(), S(), S()
    s_gcol = [S() for _ in range(3)]
    psh = [ps(f"m_psh{h}", [128, 512], F32) for h in range(4)]
    s_psh = [S() for _ in range(4)]
    pso = [ps(f"m_pso{o}", [128, 512], F32) for o in range(2)]
    s_pso = [S() for _ in range(2)]
    d_zt = bk.sem("m_zt")
    d_s1 = [bk.sem(f"m_dsl1{j}") for j in range(2)]
    d_sg = [bk.sem(f"m_dslg{j}") for j in range(2)]
    d_s2 = [bk.sem(f"m_dsl2{j}") for j in range(2)]
    NTH = TCW // 512

    w1b4 = {nm: W[nm].rearrange("e (c p) h -> e c p h", p=128) for nm in ("W1b", "Wgb")}
    w2b4 = W["W2b"].rearrange("e (h p) c -> e h p c", p=128)

    for tc in range(NTCOL):
        cm.load_xacc(x_d, tc)
        for c in range(NCH):
            bk.emit("scalar", lambda e, c=c, tc=tc: e.dma_start_transpose(
                out=zT[c][:], in_=zscr_d[tc * TCW:(tc + 1) * TCW, c * 128:(c + 1) * 128]),
                reads=s_zscr, writes=[s_zT[c]], dma=d_zt)
        bk.seal(d_zt, s_zT)
        for tt in range(NTT):
            for c in range(NCH):
                bk.mm(lambda e, tt=tt, c=c: e.matmul(
                    ps_g[:, 0:7], zT[c][:, tt * 128:(tt + 1) * 128], gwb[c][:],
                    start=(c == 0), stop=(c == NCH - 1)),
                    reads=[s_zT[c], s_gwb[c]], psum=s_psg,
                    first=(c == 0), last=(c == NCH - 1))
            bk.emit("vector", lambda e: e.tensor_tensor(out=glog[:, 0:7], in0=ps_g[:, 0:7],
                    in1=gbb[:, 0:7], op=ALU.add), reads=[s_psg, s_gbb], writes=[s_glog])
            bk.emit("vector", lambda e: e.reduce_max(out=gcol[0][:], in_=glog[:, 0:7],
                    axis=AX.X, negate=True), reads=[s_glog], writes=[s_gcol[0]])
            bk.emit("scalar", lambda e: e.activation(out=glog[:, 0:7], in_=glog[:, 0:7],
                    func=AF.Exp, bias=gcol[0][:, 0:1]),
                    reads=[s_glog, s_gcol[0]], writes=[s_glog])
            bk.emit("vector", lambda e: e.reduce_sum(out=gcol[1][:], in_=glog[:, 0:7],
                    axis=AX.X), reads=[s_glog], writes=[s_gcol[1]])
            bk.emit("vector", lambda e: e.reciprocal(out=gcol[2][:], in_=gcol[1][:]),
                    reads=[s_gcol[1]], writes=[s_gcol[2]])
            bk.emit("vector", lambda e: e.tensor_scalar(out=glog[:, 0:7], in0=glog[:, 0:7],
                    scalar1=gcol[2][:, 0:1], scalar2=None, op0=ALU.mult),
                    reads=[s_glog, s_gcol[2]], writes=[s_glog])
            bk.emit("vector", lambda e: e.tensor_tensor(out=gtmp[:, 0:7], in0=glog[:, 0:7],
                    in1=bib[:, 0:7], op=ALU.add), reads=[s_glog, s_bib], writes=[s_gtmp])
            bk.emit("vector", lambda e: e.reduce_max(out=gcol[0][:], in_=gtmp[:, 0:7],
                    axis=AX.X), reads=[s_gtmp], writes=[s_gcol[0]])
            bk.emit("vector", lambda e: e.tensor_scalar(out=gsel[:, 0:7], in0=gtmp[:, 0:7],
                    scalar1=gcol[0][:, 0:1], scalar2=None, op0=ALU.is_ge),
                    reads=[s_gtmp, s_gcol[0]], writes=[s_gsel])
            bk.emit("vector", lambda e: e.scalar_tensor_tensor(out=gt2[:, 0:7],
                    in0=gsel[:, 0:7], scalar=-1e9, in1=gtmp[:, 0:7],
                    op0=ALU.mult, op1=ALU.add), reads=[s_gsel, s_gtmp], writes=[s_gt2])
            bk.emit("vector", lambda e: e.reduce_max(out=gcol[1][:], in_=gt2[:, 0:7],
                    axis=AX.X), reads=[s_gt2], writes=[s_gcol[1]])
            bk.emit("vector", lambda e: e.tensor_scalar(out=gsel[:, 0:7], in0=gt2[:, 0:7],
                    scalar1=gcol[1][:, 0:1], scalar2=None, op0=ALU.is_ge),
                    reads=[s_gt2, s_gcol[1]], writes=[s_gsel])
            bk.emit("vector", lambda e: e.scalar_tensor_tensor(out=gt2[:, 0:7],
                    in0=gsel[:, 0:7], scalar=-1e9, in1=gt2[:, 0:7],
                    op0=ALU.mult, op1=ALU.add), reads=[s_gsel, s_gt2], writes=[s_gt2])
            bk.emit("vector", lambda e: e.reduce_max(out=gcol[2][:], in_=gt2[:, 0:7],
                    axis=AX.X), reads=[s_gt2], writes=[s_gcol[2]])
            bk.emit("vector", lambda e: e.tensor_scalar(out=gsel[:, 0:7], in0=gtmp[:, 0:7],
                    scalar1=gcol[2][:, 0:1], scalar2=None, op0=ALU.is_ge),
                    reads=[s_gtmp, s_gcol[2]], writes=[s_gsel])
            bk.emit("vector", lambda e, tt=tt: e.tensor_tensor(out=wcol[tt][:, 0:7],
                    in0=glog[:, 0:7], in1=gsel[:, 0:7], op=ALU.mult),
                    reads=[s_glog, s_gsel], writes=[s_wcol[tt]])
            bk.emit("vector", lambda e, tt=tt: e.reduce_sum(out=gcol[0][:],
                    in_=wcol[tt][:, 0:7], axis=AX.X), reads=[s_wcol[tt]], writes=[s_gcol[0]])
            bk.emit("vector", lambda e: e.reciprocal(out=gcol[1][:], in_=gcol[0][:]),
                    reads=[s_gcol[0]], writes=[s_gcol[1]])
            bk.emit("vector", lambda e, tt=tt: e.tensor_scalar(out=wcol[tt][:, 0:7],
                    in0=wcol[tt][:, 0:7], scalar1=gcol[1][:, 0:1], scalar2=0.5,
                    op0=ALU.mult, op1=ALU.mult), reads=[s_wcol[tt], s_gcol[1]],
                    writes=[s_wcol[tt]])
        for e_i in range(NE):
            for hp in range(C // 512):
                par = hp % 2
                sem1 = d_s1[par]
                for c in range(NCH):
                    bk.emit("sync", lambda e, e_i=e_i, hp=hp, c=c: e.dma_start(
                        out=slab1[c][:], in_=w1b4["W1b"][e_i, c, :, hp * 512:(hp + 1) * 512]),
                        reads=[s_cast[e_i]], writes=[s_slab1[c]], dma=sem1)
                bk.seal(sem1, s_slab1)
                semg = d_sg[par]
                for c in range(NCH):
                    bk.emit("sync", lambda e, e_i=e_i, hp=hp, c=c: e.dma_start(
                        out=slabg[c][:], in_=w1b4["Wgb"][e_i, c, :, hp * 512:(hp + 1) * 512]),
                        reads=[s_cast[e_i]], writes=[s_slabg[c]], dma=semg)
                bk.seal(semg, s_slabg)
                sem2 = d_s2[par]
                for hc in range(4):
                    bk.emit("sync", lambda e, e_i=e_i, hp=hp, hc=hc: e.dma_start(
                        out=slab2[hc][:], in_=w2b4[e_i, hp * 4 + hc]),
                        reads=[s_cast[e_i]], writes=[s_slab2[hc]], dma=sem2)
                bk.seal(sem2, s_slab2)
                for hb in range(4):
                    for th in range(NTH):
                        pp = (hb * NTH + th) % 2
                        for c in range(NCH):
                            bk.mm(lambda e, hb=hb, th=th, c=c, pp=pp: e.matmul(
                                psh[pp][:], slab1[c][:, hb * 128:(hb + 1) * 128],
                                zT[c][:, th * 512:(th + 1) * 512],
                                start=(c == 0), stop=(c == NCH - 1)),
                                reads=[s_slab1[c], s_zT[c]], psum=s_psh[pp],
                                first=(c == 0), last=(c == NCH - 1))
                        bk.emit("scalar", lambda e, hb=hb, th=th, pp=pp: e.activation(
                            out=s1t[hb][:, th * 512:(th + 1) * 512], in_=psh[pp][:],
                            func=AF.Silu), reads=[s_psh[pp]], writes=[s_s1t[hb]])
                        pp2 = 2 + (hb * NTH + th) % 2
                        for c in range(NCH):
                            bk.mm(lambda e, hb=hb, th=th, c=c, pp2=pp2: e.matmul(
                                psh[pp2][:], slabg[c][:, hb * 128:(hb + 1) * 128],
                                zT[c][:, th * 512:(th + 1) * 512],
                                start=(c == 0), stop=(c == NCH - 1)),
                                reads=[s_slabg[c], s_zT[c]], psum=s_psh[pp2],
                                first=(c == 0), last=(c == NCH - 1))
                        bk.emit("vector", lambda e, hb=hb, th=th, pp2=pp2: e.tensor_tensor(
                            out=hh[hb][:, th * 512:(th + 1) * 512], in0=psh[pp2][:],
                            in1=s1t[hb][:, th * 512:(th + 1) * 512], op=ALU.mult),
                            reads=[s_psh[pp2], s_s1t[hb]], writes=[s_hh[hb]])
                for tt in range(NTT):
                    gt = tc * NTT + tt
                    for ccol in range(C // 512):
                        o = (tt * (C // 512) + ccol) % 2
                        for hc in range(4):
                            bk.mm(lambda e, hc=hc, tt=tt, ccol=ccol, o=o: e.matmul(
                                pso[o][:], hh[hc][:, tt * 128:(tt + 1) * 128],
                                slab2[hc][:, ccol * 512:(ccol + 1) * 512],
                                start=(hc == 0), stop=(hc == 3)),
                                reads=[s_hh[hc], s_slab2[hc]], psum=s_pso[o],
                                first=(hc == 0), last=(hc == 3))
                        if e_i == 7:
                            bk.emit("vector", lambda e, tt=tt, ccol=ccol, o=o:
                                    e.scalar_tensor_tensor(
                                    out=cm.xa[tt][:, ccol * 512:(ccol + 1) * 512],
                                    in0=pso[o][:], scalar=0.5,
                                    in1=cm.xa[tt][:, ccol * 512:(ccol + 1) * 512],
                                    op0=ALU.mult, op1=ALU.add),
                                    reads=[s_pso[o], cm.s_xa[tt]], writes=[cm.s_xa[tt]])
                        else:
                            bk.emit("vector", lambda e, tt=tt, ccol=ccol, o=o, e_i=e_i, gt=gt:
                                    e.scalar_tensor_tensor(
                                    out=cm.xa[tt][:, ccol * 512:(ccol + 1) * 512],
                                    in0=pso[o][:], scalar=wcol[tt][:, e_i:e_i + 1],
                                    in1=cm.xa[tt][:, ccol * 512:(ccol + 1) * 512],
                                    op0=ALU.mult, op1=ALU.add),
                                    reads=[s_pso[o], cm.s_xa[tt], s_wcol[tt]],
                                    writes=[cm.s_xa[tt]])
        cm.store_xacc(xo_d, tc)
    cm.finish_x()





def S(n=""):
    return Slot(n)


EW = 256


def emit_l1attn(nc, bk, cm, W, zscr_d, x_d, xo_d):
    sb, ps = bk.sb, bk.ps
    s_zscr = [S(), S()]
    NTT = TCW // 128
    NTH = TCW // 512
    NEP = C // EW
    NEB = EW // 128
    denom = float(C) ** 0.5 / float(T)

    s_cast = {}
    for nm in ("q", "k", "v", "o"):
        sem = bk.sem(f"a1_c{nm}")
        s_cast[nm] = S()
        bk.emit("gpsimd", lambda e, nm=nm: e.dma_start(
            out=W[nm + "b"], in_=W["W" + nm]), writes=[s_cast[nm]], dma=sem)
        bk.seal(sem, [s_cast[nm]])

    cm.ln_pass(x_d, zscr_d, s_zscr)

    zT = [sb(f"a1_zT{c}", [128, TCW], BF16) for c in range(NCH)]
    s_zT = [S() for _ in range(NCH)]
    wq = [sb(f"a1_wq{c}", [128, EW], BF16) for c in range(NCH)]
    wk = [sb(f"a1_wk{c}", [128, EW], BF16) for c in range(NCH)]
    wv = [sb(f"a1_wv{c}", [128, EW], BF16) for c in range(NCH)]
    s_wq = [S() for _ in range(NCH)]
    s_wk = [S() for _ in range(NCH)]
    s_wv = [S() for _ in range(NCH)]
    wo = [sb(f"a1_wo{b}", [128, C], BF16) for b in range(NEB)]
    s_wo = [S() for _ in range(NEB)]
    qf = [sb(f"a1_qf{b}", [128, TCW], F32) for b in range(NEB)]
    kf = [sb(f"a1_kf{b}", [128, TCW], F32) for b in range(NEB)]
    vf = [sb(f"a1_vf{b}", [128, TCW], F32) for b in range(NEB)]
    kvt = [sb(f"a1_kvt{b}", [128, TCW], F32) for b in range(NEB)]
    vhb = [sb(f"a1_vhb{b}", [128, TCW], BF16) for b in range(NEB)]
    s_qf = [S() for _ in range(NEB)]
    s_kf = [S() for _ in range(NEB)]
    s_vf = [S() for _ in range(NEB)]
    s_kvt = [S() for _ in range(NEB)]
    s_vhb = [S() for _ in range(NEB)]
    zeros = sb("a1_zeros", [128, TCW], F32)
    s_zeros = S()
    state = sb("a1_state", [128, NCH], F32)
    s_state = [S() for _ in range(NCH)]
    bk.emit("vector", lambda e: e.memset(zeros[:], 0.0), writes=[s_zeros])
    st_init = bk.emit("vector", lambda e: e.memset(state[:], 0.0), writes=s_state)
    psq = [ps(f"a1_psq{j}", [128, 512], F32) for j in range(3)]
    s_psq = [S() for _ in range(3)]
    pso = [ps(f"a1_pso{o}", [128, 512], F32) for o in range(2)]
    s_pso = [S() for _ in range(2)]
    d_zt = bk.sem("a1_zt")
    d_w = [bk.sem(f"a1_dw{j}") for j in range(2)]
    d_wo = [bk.sem(f"a1_dwo{j}") for j in range(2)]

    wqbT = {"q": W["qb"], "k": W["kb"], "v": W["vb"]}

    for tc in range(NTCOL):
        cm.load_xacc(x_d, tc)
        for c in range(NCH):
            bk.emit("scalar", lambda e, c=c, tc=tc: e.dma_start_transpose(
                out=zT[c][:], in_=zscr_d[tc * TCW:(tc + 1) * TCW, c * 128:(c + 1) * 128]),
                reads=s_zscr, writes=[s_zT[c]], dma=d_zt)
        bk.seal(d_zt, s_zT)
        for ep in range(NEP):
            par = ep % 2
            semw = d_w[par]
            for c in range(NCH):
                for nm, buf, slot in (("q", wq, s_wq), ("k", wk, s_wk), ("v", wv, s_wv)):
                    bk.emit("scalar", lambda e, c=c, ep=ep, nm=nm, buf=buf: e.dma_start_transpose(
                        out=buf[c][:], in_=wqbT[nm][ep * EW:(ep + 1) * EW, c * 128:(c + 1) * 128]),
                        reads=[s_cast[nm]], writes=[slot[c]], dma=semw)
            bk.seal(semw, s_wq + s_wk + s_wv)
            semo = d_wo[par]
            for b in range(NEB):
                bk.emit("scalar", lambda e, b=b, ep=ep: e.dma_start_transpose(
                    out=wo[b][:], in_=W["ob"][:, (ep * NEB + b) * 128:(ep * NEB + b + 1) * 128]),
                    reads=[s_cast["o"]], writes=[s_wo[b]], dma=semo)
            bk.seal(semo, s_wo)
            for b in range(NEB):
                for proj, (buf, slot, fbuf, fslot) in enumerate(
                        ((wq, s_wq, qf, s_qf), (wk, s_wk, kf, s_kf), (wv, s_wv, vf, s_vf))):
                    pp = proj
                    for th in range(NTH):
                        for c in range(NCH):
                            bk.mm(lambda e, b=b, th=th, c=c, pp=pp, buf=buf: e.matmul(
                                psq[pp][:], buf[c][:, b * 128:(b + 1) * 128],
                                zT[c][:, th * 512:(th + 1) * 512],
                                start=(c == 0), stop=(c == NCH - 1)),
                                reads=[slot[c], s_zT[c]], psum=s_psq[pp],
                                first=(c == 0), last=(c == NCH - 1))
                        bk.emit("vector", lambda e, b=b, th=th, pp=pp, fbuf=fbuf: e.tensor_copy(
                            out=fbuf[b][:, th * 512:(th + 1) * 512], in_=psq[pp][:]),
                            reads=[s_psq[pp]], writes=[fslot[b]])
                gc = ep * NEB + b
                bk.emit("vector", lambda e, b=b: e.tensor_tensor(out=kf[b][:], in0=kf[b][:],
                        in1=vf[b][:], op=ALU.mult), reads=[s_kf[b], s_vf[b]], writes=[s_kf[b]])
                bk.emit("vector", lambda e, b=b, gc=gc: e.tensor_tensor_scan(
                        out=kvt[b][:], data0=kf[b][:], data1=zeros[:],
                        initial=state[:, gc:gc + 1], op0=ALU.add, op1=ALU.add),
                        reads=[s_kf[b], s_zeros, s_state[gc]], writes=[s_kvt[b]])
                bk.emit("vector", lambda e, b=b, gc=gc: e.tensor_copy(
                        out=state[:, gc:gc + 1], in_=kvt[b][:, TCW - 1:TCW]),
                        reads=[s_kvt[b]], writes=[s_state[gc]])
                bk.emit("vector", lambda e, b=b: e.scalar_tensor_tensor(
                        out=vhb[b][:], in0=kvt[b][:], scalar=denom, in1=qf[b][:],
                        op0=ALU.mult, op1=ALU.mult),
                        reads=[s_kvt[b], s_qf[b]], writes=[s_vhb[b]])
            for tt in range(NTT):
                for ccol in range(C // 512):
                    o = (tt * (C // 512) + ccol) % 2
                    for b in range(NEB):
                        bk.mm(lambda e, tt=tt, ccol=ccol, o=o, b=b: e.matmul(
                            pso[o][:], vhb[b][:, tt * 128:(tt + 1) * 128],
                            wo[b][:, ccol * 512:(ccol + 1) * 512],
                            start=(b == 0), stop=(b == NEB - 1)),
                            reads=[s_vhb[b], s_wo[b]], psum=s_pso[o],
                            first=(b == 0), last=(b == NEB - 1))
                    bk.emit("vector", lambda e, tt=tt, ccol=ccol, o=o: e.tensor_add(
                        out=cm.xa[tt][:, ccol * 512:(ccol + 1) * 512],
                        in0=cm.xa[tt][:, ccol * 512:(ccol + 1) * 512], in1=pso[o][:]),
                        reads=[s_pso[o], cm.s_xa[tt]], writes=[cm.s_xa[tt]])
        cm.store_xacc(xo_d, tc)
    cm.finish_x()





def S(n=""):
    return Slot(n)


GATE_OFFSET = 6.906768
NKV = 8
NQH = 16
LN_C = math.log(1.0 / math.sqrt(128.0))


def emit_l0attn(nc, bk, cm, W, zscr_d, x_d, xo_d, arena=None):
    sb, ps = bk.sb, bk.ps
    s_zscr = [S(), S()]
    NTT = TCW // 128
    NTH = TCW // 512

    s_cast = {}
    for nm, src in (("qb", "Wq"), ("kb", "Wk"), ("vb", "Wv")):
        sem = bk.sem(f"a0_c{nm}")
        s_cast[nm] = S()
        bk.emit("gpsimd", lambda e, nm=nm, src=src: e.dma_start(
            out=W[nm], in_=W[src]), writes=[s_cast[nm]], dma=sem)
        bk.seal(sem, [s_cast[nm]])

    cm.ln_pass(x_d, zscr_d, s_zscr)

    pp = [ps(f"a0_ps{i}", [128, 512], F32) for i in range(6)]
    s_pp = [S() for _ in range(6)]
    ps_g = ps("a0_psg", [128, 8], F32)
    s_psg = S()
    ps_n = ps("a0_psn", [1, 512], F32)
    s_psn = S()

    onecolf = sb("a0_one", [128, 1], F32)
    s_onecolf = S()
    bk.emit("vector", lambda e: e.memset(onecolf[:], 1.0), writes=[s_onecolf])
    lncol = sb("a0_lnc", [128, 1], F32)
    s_lncol = S()
    bk.emit("vector", lambda e: e.memset(lncol[:], LN_C), writes=[s_lncol])
    row1 = sb("a0_row1", [1, T], F32)
    s_row1 = S()
    logg_tm = [sb(f"a0_lg{t}", [128, 8], F32) for t in range(NT)]
    s_logg_tm = [S() for _ in range(NT)]
    sqes_tm = [sb(f"a0_se{t}", [128, 8], F32) for t in range(NT)]
    s_sqes_tm = [S() for _ in range(NT)]
    gu = sb("a0_gu", [128, 8], F32)
    ga = sb("a0_ga", [128, 8], F32)
    gl = sb("a0_gl", [128, 8], F32)
    s_gu, s_ga, s_gl = S(), S(), S()
    lgpre = sb("a0_lgpre", [8, T], F32)
    lgcum = sb("a0_lgcum", [8, T], F32)
    s_lgpre, s_lgcum = S(), S()
    ebuf = [sb(f"a0_ebuf{j}", [128, 512], BF16) for j in range(4)]
    s_ebuf = [S() for _ in range(4)]
    d_st = [bk.sem(f"a0_dst{j}") for j in range(4)]
    s_qscr, s_kscr, s_vscr = S(), S(), S()
    ecnt = [0]

    wgf = [sb(f"a0_wgf{c}", [128, 8], F32) for c in range(NCH)]
    wgb = [sb(f"a0_wgb{c}", [128, 8], BF16) for c in range(NCH)]
    s_wgf = [S() for _ in range(NCH)]
    s_wgb = [S() for _ in range(NCH)]
    d_wg = bk.sem("a0_wg")
    wg3 = W["Wg"].rearrange("(c p) g -> c p g", p=128)
    for c in range(NCH):
        bk.emit("sync", lambda e, c=c: e.dma_start(out=wgf[c][:, 0:8], in_=wg3[c]),
                writes=[s_wgf[c]], dma=d_wg)
    bk.seal(d_wg, s_wgf)
    for c in range(NCH):
        bk.emit("vector", lambda e, c=c: e.tensor_copy(out=wgb[c][:], in_=wgf[c][:]),
                reads=[s_wgf[c]], writes=[s_wgb[c]])

    if arena is None:
        arena = Arena(bk, "a0_ar", 66 * 1024)

    def evac_store(u, dst_ap, wslot):
        j = ecnt[0] % 4
        ecnt[0] += 1
        bk.emit("vector", lambda e, u=u, j=j: e.tensor_copy(out=ebuf[j][:], in_=pp[u][:]),
                reads=[s_pp[u]], writes=[s_ebuf[j]])
        bk.emit("sync", lambda e, j=j, dst_ap=dst_ap: e.dma_start(out=dst_ap, in_=ebuf[j][:]),
                reads=[s_ebuf[j]], writes=[wslot], dma=d_st[j])

    zT = [arena.alloc(f"zT{c}", [128, TCW], BF16) for c in range(NCH)]
    s_zT = [S() for _ in range(NCH)]
    wqsl = [arena.alloc(f"wqsl{j}", [128, 2048], BF16) for j in range(2)]
    s_wqsl = [S() for _ in range(2)]
    wksl = [arena.alloc(f"wksl{j}", [128, 2048], BF16) for j in range(2)]
    s_wksl = [S() for _ in range(2)]
    wvsl = [arena.alloc(f"wvsl{c}", [128, 512], BF16) for c in range(NCH)]
    s_wvsl = [S() for _ in range(NCH)]
    d_zt = bk.sem("a0_zt")
    d_wq = [bk.sem(f"a0_dwq{j}") for j in range(2)]
    d_wv = bk.sem("a0_dwv")
    wqb3 = W["qb"].rearrange("(c p) e -> c p e", p=128)
    wkb3 = W["kb"].rearrange("(c p) e -> c p e", p=128)
    wvb3 = W["vb"].rearrange("(c p) e -> c p e", p=128)

    for tc in range(NTCOL):
        for c in range(NCH):
            bk.emit("scalar", lambda e, c=c, tc=tc: e.dma_start_transpose(
                out=zT[c][:], in_=zscr_d[tc * TCW:(tc + 1) * TCW, c * 128:(c + 1) * 128]),
                reads=s_zscr, writes=[s_zT[c]], dma=d_zt)
        bk.seal(d_zt, s_zT)
        for qb in range(16):
            j = qb % 2
            bk.emit("sync", lambda e, qb=qb, j=j: e.dma_start(
                out=wqsl[j].rearrange("p (c e) -> p c e", e=128),
                in_=wqb3[:, :, qb * 128:(qb + 1) * 128].rearrange("c p e -> p c e")),
                reads=[s_cast["qb"]], writes=[s_wqsl[j]], dma=d_wq[j])
            bk.seal(d_wq[j], [s_wqsl[j]])
            for th in range(NTH):
                u = (qb * NTH + th) % 2
                for c in range(NCH):
                    bk.mm(lambda e, th=th, c=c, u=u, j=j: e.matmul(
                        pp[u][:], wqsl[j][:, c * 128:(c + 1) * 128],
                        zT[c][:, th * 512:(th + 1) * 512],
                        start=(c == 0), stop=(c == NCH - 1)),
                        reads=[s_wqsl[j], s_zT[c]], psum=s_pp[u],
                        first=(c == 0), last=(c == NCH - 1))
                evac_store(u, W["qscr"][qb * 128:(qb + 1) * 128,
                           tc * TCW + th * 512:tc * TCW + (th + 1) * 512], s_qscr)
        for kb in range(8):
            j = kb % 2
            bk.emit("sync", lambda e, kb=kb, j=j: e.dma_start(
                out=wksl[j].rearrange("p (c e) -> p c e", e=128),
                in_=wkb3[:, :, kb * 128:(kb + 1) * 128].rearrange("c p e -> p c e")),
                reads=[s_cast["kb"]], writes=[s_wksl[j]], dma=d_wq[j])
            bk.seal(d_wq[j], [s_wksl[j]])
            for th in range(NTH):
                u = (kb * NTH + th) % 2
                for c in range(NCH):
                    bk.mm(lambda e, th=th, c=c, u=u, j=j: e.matmul(
                        pp[u][:], wksl[j][:, c * 128:(c + 1) * 128],
                        zT[c][:, th * 512:(th + 1) * 512],
                        start=(c == 0), stop=(c == NCH - 1)),
                        reads=[s_wksl[j], s_zT[c]], psum=s_pp[u],
                        first=(c == 0), last=(c == NCH - 1))
                evac_store(u, W["kscr"][kb * 128:(kb + 1) * 128,
                           tc * TCW + th * 512:tc * TCW + (th + 1) * 512], s_kscr)
        for vh in range(2):
            for c in range(NCH):
                bk.emit("sync", lambda e, vh=vh, c=c: e.dma_start(
                    out=wvsl[c][:], in_=wvb3[c][:, vh * 512:(vh + 1) * 512]),
                    reads=[s_cast["vb"]], writes=[s_wvsl[c]], dma=d_wv)
            bk.seal(d_wv, s_wvsl)
            for tt in range(NTT):
                gt = tc * NTT + tt
                u = (vh * NTT + tt) % 2 + 2
                for c in range(NCH):
                    bk.mm(lambda e, tt=tt, c=c, u=u: e.matmul(
                        pp[u][:], zT[c][:, tt * 128:(tt + 1) * 128], wvsl[c][:],
                        start=(c == 0), stop=(c == NCH - 1)),
                        reads=[s_wvsl[c], s_zT[c]], psum=s_pp[u],
                        first=(c == 0), last=(c == NCH - 1))
                evac_store(u, W["vscr"][gt * 128:(gt + 1) * 128,
                           vh * 512:(vh + 1) * 512], s_vscr)
        for tt in range(NTT):
            gt = tc * NTT + tt
            for c in range(NCH):
                bk.mm(lambda e, tt=tt, c=c: e.matmul(
                    ps_g[:], zT[c][:, tt * 128:(tt + 1) * 128], wgb[c][:],
                    start=(c == 0), stop=(c == NCH - 1)),
                    reads=[s_wgb[c], s_zT[c]], psum=s_psg,
                    first=(c == 0), last=(c == NCH - 1))
            bk.emit("vector", lambda e: e.tensor_copy(out=gu[:], in_=ps_g[:]),
                    reads=[s_psg], writes=[s_gu])
            bk.emit("vector", lambda e: e.tensor_scalar(out=gu[:], in0=gu[:],
                    scalar1=GATE_OFFSET, scalar2=None, op0=ALU.add),
                    reads=[s_gu], writes=[s_gu])
            bk.emit("scalar", lambda e: e.activation(out=ga[:], in_=gu[:],
                    func=AF.Abs), reads=[s_gu], writes=[s_ga])
            bk.emit("scalar", lambda e: e.activation(out=ga[:], in_=ga[:],
                    func=AF.Exp, scale=-1.0), reads=[s_ga], writes=[s_ga])
            bk.emit("scalar", lambda e: e.activation(out=ga[:], in_=ga[:],
                    func=AF.Ln, bias=onecolf[:, 0:1]),
                    reads=[s_ga, s_onecolf], writes=[s_ga])
            bk.emit("vector", lambda e: e.tensor_scalar(out=gl[:], in0=gu[:],
                    scalar1=-1.0, scalar2=0.0, op0=ALU.mult, op1=ALU.max),
                    reads=[s_gu], writes=[s_gl])
            bk.emit("vector", lambda e, gt=gt: e.scalar_tensor_tensor(
                    out=logg_tm[gt][:], in0=gl[:], scalar=-1.0, in1=ga[:],
                    op0=ALU.mult, op1=ALU.subtract),
                    reads=[s_gl, s_ga], writes=[s_logg_tm[gt]])

    for t in range(NT):
        u = t % 2
        bk.mm(lambda e, t=t, u=u: e.transpose(pp[u][0:8, 0:128], logg_tm[t][:],
              cm.identf[:]), reads=[s_logg_tm[t], cm.s_identf], psum=s_pp[u])
        bk.emit("vector", lambda e, t=t, u=u: e.tensor_copy(
            out=lgpre[:, t * 128:(t + 1) * 128], in_=pp[u][0:8, 0:128]),
            reads=[s_pp[u]], writes=[s_lgpre])
    bk.emit("vector", lambda e: e.tensor_tensor_scan(
        out=lgcum[:], data0=lgpre[:], data1=lgpre[:], initial=0.0,
        op0=ALU.add, op1=ALU.bypass), reads=[s_lgpre], writes=[s_lgcum])
    d_lg = bk.sem("a0_dlg")
    s_loggscr = S()
    bk.emit("sync", lambda e: e.dma_start(out=W["logg"], in_=lgcum[:]),
            reads=[s_lgcum], writes=[s_loggscr], dma=d_lg)
    bk.seal(d_lg, [s_loggscr])
    for t in range(NT):
        u = t % 2
        bk.mm(lambda e, t=t, u=u: e.transpose(pp[u][0:128, 0:8],
              lgcum[:, t * 128:(t + 1) * 128], cm.identf[0:8, 0:8]),
              reads=[s_lgcum, cm.s_identf], psum=s_pp[u])
        bk.emit("scalar", lambda e, t=t, u=u: e.activation(out=sqes_tm[t][:],
                in_=pp[u][0:128, 0:8], func=AF.Exp, scale=-0.5, bias=lncol[:, 0:1]),
                reads=[s_pp[u], s_lncol], writes=[s_sqes_tm[t]])

    bk.barrier()
    arena.reset()
    qh = arena.alloc("qh", [128, T], BF16)
    kh = arena.alloc("kh", [128, T], BF16)
    vh = [arena.alloc(f"vh{t}", [128, 128], BF16) for t in range(NT)]
    yh = arena.alloc("yh", [128, T], BF16)
    eb = arena.alloc("eb", [128, T], BF16)
    sq = [arena.alloc(f"sq{j}", [128, 512], BF16) for j in range(2)]
    at = [arena.alloc(f"at{j}", [128, 512], BF16) for j in range(2)]
    erowb = arena.alloc("erowb", [1, T], BF16)
    nrow = arena.alloc("nrow", [1, 512], F32)
    nrowb = arena.alloc("nrowb", [1, 512], BF16)
    rbb = arena.alloc("rbb", [128, 512], BF16)
    s_qh, s_kh, s_yh, s_eb = S(), S(), S(), S()
    s_vh = [S() for _ in range(NT)]
    s_sq = [S(), S()]
    s_at = [S(), S()]
    s_erowb, s_nrow, s_nrowb, s_rbb = S(), S(), S(), S()
    d_qh = [bk.sem(f"a0_dqh{j}") for j in range(2)]
    d_kh = bk.sem("a0_dkh")
    d_vh = bk.sem("a0_dvh")
    d_er = bk.sem("a0_der")
    d_ys = [bk.sem(f"a0_dys{j}") for j in range(2)]
    s_yscr = S()

    for h in range(NQH):
        hk = h // 2
        if h % 2 == 0:
            bk.emit("sync", lambda e, hk=hk: e.dma_start(
                out=kh[:], in_=W["kscr"][hk * 128:(hk + 1) * 128, :]),
                reads=[s_kscr], writes=[s_kh], dma=d_kh)
            bk.seal(d_kh, [s_kh])
            for t in range(NT):
                bk.emit("sync", lambda e, t=t, hk=hk: e.dma_start(
                    out=vh[t][:], in_=W["vscr"][t * 128:(t + 1) * 128,
                                                hk * 128:(hk + 1) * 128]),
                    reads=[s_vscr], writes=[s_vh[t]], dma=d_vh)
            bk.seal(d_vh, s_vh)
            bk.emit("sync", lambda e, hk=hk: e.dma_start(
                out=row1[:], in_=W["logg"][hk:hk + 1, :]),
                reads=[s_loggscr], writes=[s_row1], dma=d_er)
            bk.seal(d_er, [s_row1])
            bk.emit("scalar", lambda e: e.activation(out=row1[:], in_=row1[:],
                    func=AF.Exp), reads=[s_row1], writes=[s_row1])
            bk.emit("vector", lambda e: e.tensor_copy(out=erowb[:], in_=row1[:]),
                    reads=[s_row1], writes=[s_erowb])
            for q4 in range(4):
                bk.mm(lambda e, q4=q4: e.matmul(pp[4][:], cm.onesr[:],
                      erowb[:, q4 * 512:(q4 + 1) * 512], start=True, stop=True),
                      reads=[cm.s_onesr, s_erowb], psum=s_pp[4])
                bk.emit("vector", lambda e, q4=q4: e.tensor_copy(
                        out=eb[:, q4 * 512:(q4 + 1) * 512], in_=pp[4][:]),
                        reads=[s_pp[4]], writes=[s_eb])
        bk.emit("sync", lambda e, h=h: e.dma_start(
            out=qh[:], in_=W["qscr"][h * 128:(h + 1) * 128, :]),
            reads=[s_qscr], writes=[s_qh], dma=d_qh[h % 2])
        bk.seal(d_qh[h % 2], [s_qh])
        for tq in range(4):
            nsb = 4 * tq + 4
            for sbk in range(nsb):
                u = sbk % 2
                bk.mm(lambda e, sbk=sbk, tq=tq, u=u: e.matmul(
                    pp[u][:], kh[:, sbk * 128:(sbk + 1) * 128],
                    qh[:, tq * 512:(tq + 1) * 512], start=True, stop=True),
                    reads=[s_kh, s_qh], psum=s_pp[u])
                bk.emit("scalar", lambda e, sbk=sbk, u=u, hk=hk: e.activation(
                    out=sq[u][:], in_=pp[u][:], func=AF.Square,
                    scale=sqes_tm[sbk][:, hk:hk + 1]),
                    reads=[s_pp[u], s_sqes_tm[sbk]], writes=[s_sq[u]])
                bk.emit("vector", lambda e, tq=tq, u=u: e.tensor_tensor(
                    out=at[u][:], in0=sq[u][:], in1=eb[:, tq * 512:(tq + 1) * 512],
                    op=ALU.mult), reads=[s_sq[u], s_eb], writes=[s_at[u]])
                if sbk >= 4 * tq:
                    bk.emit("gpsimd", lambda e, sbk=sbk, tq=tq, u=u: e.affine_select(
                        out=at[u][:], in_=at[u][:], pattern=[[1, 512]],
                        compare_op=ALU.is_ge, fill=0.0,
                        base=tq * 512 - sbk * 128, channel_multiplier=-1),
                        reads=[s_at[u]], writes=[s_at[u]])
                bk.mm(lambda e, sbk=sbk, u=u, nsb=nsb: e.matmul(
                    ps_n[:], cm.onecol[:], at[u][:],
                    start=(sbk == 0), stop=(sbk == nsb - 1)),
                    reads=[cm.s_onecol, s_at[u]], psum=s_psn,
                    first=(sbk == 0), last=(sbk == nsb - 1))
                bk.mm(lambda e, sbk=sbk, u=u, nsb=nsb: e.matmul(
                    pp[2][:], vh[sbk][:], at[u][:],
                    start=(sbk == 0), stop=(sbk == nsb - 1)),
                    reads=[s_vh[sbk], s_at[u]], psum=s_pp[2],
                    first=(sbk == 0), last=(sbk == nsb - 1))
            bk.emit("vector", lambda e: e.tensor_scalar(out=nrow[:], in0=ps_n[:],
                    scalar1=1e-6, scalar2=None, op0=ALU.add),
                    reads=[s_psn], writes=[s_nrow])
            bk.emit("scalar", lambda e: e.activation(out=nrow[:], in_=nrow[:],
                    func=AF.Ln), reads=[s_nrow], writes=[s_nrow])
            bk.emit("scalar", lambda e: e.activation(out=nrow[:], in_=nrow[:],
                    func=AF.Exp, scale=-1.0), reads=[s_nrow], writes=[s_nrow])
            bk.emit("vector", lambda e: e.tensor_copy(out=nrowb[:], in_=nrow[:]),
                    reads=[s_nrow], writes=[s_nrowb])
            bk.mm(lambda e: e.matmul(pp[3][:], cm.onesr[:], nrowb[:],
                  start=True, stop=True), reads=[cm.s_onesr, s_nrowb], psum=s_pp[3])
            bk.emit("vector", lambda e: e.tensor_copy(out=rbb[:], in_=pp[3][:]),
                    reads=[s_pp[3]], writes=[s_rbb])
            bk.emit("vector", lambda e, tq=tq: e.tensor_tensor(
                    out=yh[:, tq * 512:(tq + 1) * 512], in0=pp[2][:], in1=rbb[:],
                    op=ALU.mult), reads=[s_pp[2], s_rbb], writes=[s_yh])
        bk.emit("sync", lambda e, h=h: e.dma_start(
            out=W["yscr"][h * 128:(h + 1) * 128, :], in_=yh[:]),
            reads=[s_yh], writes=[s_yscr], dma=d_ys[h % 2])
        bk.seal(d_ys[h % 2], [s_yscr])

    bk.barrier()
    arena.reset()
    ysl = [arena.alloc(f"ysl{q}", [128, TCW], BF16) for q in range(16)]
    s_ysl = [S() for _ in range(16)]
    wcs = [arena.alloc(f"wcs{q}", [128, 512], BF16) for q in range(16)]
    s_wcs = [S() for _ in range(16)]
    bcb = arena.alloc("bcb", [128, C], F32)
    s_bcb = S()
    d_bc = bk.sem("a0_dbc")
    d_ysl = [bk.sem(f"a0_dysl{j}") for j in range(2)]
    d_wcs = [bk.sem(f"a0_dwcs{j}") for j in range(2)]
    bk.emit("sync", lambda e: e.dma_start(out=row1[:],
            in_=W["bc"].rearrange("(b a) -> b a", b=1)), writes=[s_row1], dma=d_bc)
    bk.seal(d_bc, [s_row1])
    for q4 in range(4):
        bk.mm(lambda e, q4=q4: e.matmul(pp[5][:], cm.onesrf[:],
              row1[:, q4 * 512:(q4 + 1) * 512], start=True, stop=True),
              reads=[cm.s_onesrf, s_row1], psum=s_pp[5])
        bk.emit("vector", lambda e, q4=q4: e.tensor_copy(
                out=bcb[:, q4 * 512:(q4 + 1) * 512], in_=pp[5][:]),
                reads=[s_pp[5]], writes=[s_bcb])
    wc3 = W["Wc"].rearrange("(q p) c -> q p c", p=128)
    for tc in range(NTCOL):
        cm.load_xacc(x_d, tc)
        for tt in range(NTT):
            bk.emit("vector", lambda e, tt=tt: e.tensor_add(out=cm.xa[tt][:],
                    in0=cm.xa[tt][:], in1=bcb[:]),
                    reads=[s_bcb, cm.s_xa[tt]], writes=[cm.s_xa[tt]])
        sem = d_ysl[tc % 2]
        for q in range(16):
            bk.emit("sync", lambda e, q=q, tc=tc: e.dma_start(
                out=ysl[q][:], in_=W["yscr"][q * 128:(q + 1) * 128,
                                             tc * TCW:(tc + 1) * TCW]),
                reads=[s_yscr], writes=[s_ysl[q]], dma=sem)
        bk.seal(sem, s_ysl)
        for ccol in range(C // 512):
            semw = d_wcs[ccol % 2]
            for q in range(16):
                bk.emit("gpsimd", lambda e, q=q, ccol=ccol: e.dma_start(
                    out=wcs[q][:], in_=wc3[q][:, ccol * 512:(ccol + 1) * 512]),
                    writes=[s_wcs[q]], dma=semw)
            bk.seal(semw, s_wcs)
            for tt in range(NTT):
                o = tt % 2
                for q in range(16):
                    bk.mm(lambda e, q=q, tt=tt, o=o: e.matmul(
                        pp[o][:], ysl[q][:, tt * 128:(tt + 1) * 128], wcs[q][:],
                        start=(q == 0), stop=(q == 15)),
                        reads=[s_ysl[q], s_wcs[q]], psum=s_pp[o],
                        first=(q == 0), last=(q == 15))
                bk.emit("vector", lambda e, tt=tt, ccol=ccol, o=o: e.tensor_add(
                    out=cm.xa[tt][:, ccol * 512:(ccol + 1) * 512],
                    in0=cm.xa[tt][:, ccol * 512:(ccol + 1) * 512], in1=pp[o][:]),
                    reads=[s_pp[o], cm.s_xa[tt]], writes=[cm.s_xa[tt]])
        cm.store_xacc(xo_d, tc)
    cm.finish_x()






def _decl(nc, name, shape, dtype, kind):
    return nc.dram_tensor(name, shape, dtype, kind=kind).ap()


def build_prog_l0ffn():
    nc = bass.Bass("TRN2", target_bir_lowering=False)
    x_d = _decl(nc, "x", [T, C], F32, "ExternalInput")
    W = {"up": _decl(nc, "up_w", [HF, C], F32, "ExternalInput"),
         "dn": _decl(nc, "dn_w", [C, HF], F32, "ExternalInput"),
         "us": _decl(nc, "up_scale", [HF], F32, "ExternalInput"),
         "ds": _decl(nc, "dn_scale", [C], F32, "ExternalInput"),
         "os": _decl(nc, "out_scale", [1], F32, "ExternalInput"),
         "upb": _decl(nc, "upb", [HF, C], BF16, "Internal"),
         "dnb": _decl(nc, "dnb", [C, HF], BF16, "Internal")}
    xo_d = _decl(nc, "xo", [T, C], F32, "ExternalOutput")
    zscr_d = _decl(nc, "zscr", [T, C], BF16, "Internal")
    bk = BK(nc)
    cm = Common(nc, bk)
    emit_l0ffn(nc, bk, cm, W, zscr_d, x_d, xo_d)
    bk.finalize()
    return nc


def build_prog_l0attn():
    nc = bass.Bass("TRN2", target_bir_lowering=False)
    x_d = _decl(nc, "x", [T, C], F32, "ExternalInput")
    W = {"Wq": _decl(nc, "Wq", [C, 2048], F32, "ExternalInput"),
         "Wk": _decl(nc, "Wk", [C, 1024], F32, "ExternalInput"),
         "Wv": _decl(nc, "Wv", [C, 1024], F32, "ExternalInput"),
         "Wg": _decl(nc, "Wg", [C, 8], F32, "ExternalInput"),
         "Wc": _decl(nc, "Wc", [2048, C], F32, "ExternalInput"),
         "bc": _decl(nc, "bc", [C], F32, "ExternalInput"),
         "qb": _decl(nc, "qb", [C, 2048], BF16, "Internal"),
         "kb": _decl(nc, "kb", [C, 1024], BF16, "Internal"),
         "vb": _decl(nc, "vb", [C, 1024], BF16, "Internal"),
         "qscr": _decl(nc, "qscr", [2048, T], BF16, "Internal"),
         "kscr": _decl(nc, "kscr", [1024, T], BF16, "Internal"),
         "vscr": _decl(nc, "vscr", [T, 1024], BF16, "Internal"),
         "yscr": _decl(nc, "yscr", [2048, T], BF16, "Internal"),
         "logg": _decl(nc, "logg", [8, T], F32, "Internal")}
    xo_d = _decl(nc, "xo", [T, C], F32, "ExternalOutput")
    zscr_d = _decl(nc, "zscr", [T, C], BF16, "Internal")
    bk = BK(nc)
    cm = Common(nc, bk)
    emit_l0attn(nc, bk, cm, W, zscr_d, x_d, xo_d)
    bk.finalize()
    return nc


def build_prog_l1moe():
    nc = bass.Bass("TRN2", target_bir_lowering=False)
    x_d = _decl(nc, "x", [T, C], F32, "ExternalInput")
    W = {"W1": _decl(nc, "W1", [7, C, C], F32, "ExternalInput"),
         "Wg": _decl(nc, "Wg", [7, C, C], F32, "ExternalInput"),
         "W2": _decl(nc, "W2", [7, C, C], F32, "ExternalInput"),
         "Ws1": _decl(nc, "Ws1", [C, C], F32, "ExternalInput"),
         "Wsg": _decl(nc, "Wsg", [C, C], F32, "ExternalInput"),
         "Ws2": _decl(nc, "Ws2", [C, C], F32, "ExternalInput"),
         "gate_w": _decl(nc, "gate_w", [C, 7], F32, "ExternalInput"),
         "gate_b": _decl(nc, "gate_b", [7], F32, "ExternalInput"),
         "biases": _decl(nc, "biases", [7], F32, "ExternalInput"),
         "W1b": _decl(nc, "W1b", [8, C, C], BF16, "Internal"),
         "Wgb": _decl(nc, "Wgb", [8, C, C], BF16, "Internal"),
         "W2b": _decl(nc, "W2b", [8, C, C], BF16, "Internal")}
    xo_d = _decl(nc, "xo", [T, C], F32, "ExternalOutput")
    zscr_d = _decl(nc, "zscr", [T, C], BF16, "Internal")
    bk = BK(nc)
    cm = Common(nc, bk)
    emit_l1moe(nc, bk, cm, W, zscr_d, x_d, xo_d)
    bk.finalize()
    return nc


def build_prog_l1attn():
    nc = bass.Bass("TRN2", target_bir_lowering=False)
    x_d = _decl(nc, "x", [T, C], F32, "ExternalInput")
    W = {}
    for nm in ("Wq", "Wk", "Wv", "Wo"):
        W[nm] = _decl(nc, nm, [C, C], F32, "ExternalInput")
    for nm in ("q", "k", "v", "o"):
        W[nm + "b"] = _decl(nc, nm + "b", [C, C], BF16, "Internal")
    xo_d = _decl(nc, "xo", [T, C], F32, "ExternalOutput")
    zscr_d = _decl(nc, "zscr", [T, C], BF16, "Internal")
    bk = BK(nc)
    cm = Common(nc, bk)
    emit_l1attn(nc, bk, cm, W, zscr_d, x_d, xo_d)
    bk.finalize()
    return nc


_PROGS = {}


def _prog(name):
    if name not in _PROGS:
        _PROGS[name] = {"l0ffn": build_prog_l0ffn, "l0attn": build_prog_l0attn,
                        "l1moe": build_prog_l1moe, "l1attn": build_prog_l1attn}[name]()
    return _PROGS[name]


def _launch(name, wmap, xs):
    nc = _prog(name)
    in_maps = [{**wmap, "x": xs[b]} for b in range(8)]
    res = run_bass_kernel_spmd(nc, in_maps, core_ids=list(range(8)))
    return [np.asarray(res.results[b]["xo"]) for b in range(8)]


def kernel(L0_attn_hidden_states, L0_attn_Wq, L0_attn_Wk, L0_attn_Wv, L0_attn_Wg,
           L0_attn_Wc, L0_attn_bc, L0_ffn_up_weight, L0_ffn_down_weight,
           L0_ffn_up_scale, L0_ffn_down_scale, L0_ffn_output_scale,
           L1_attn_Wq, L1_attn_Wk, L1_attn_Wv, L1_attn_Wo,
           L1_ffn_W1, L1_ffn_Wg, L1_ffn_W2, L1_ffn_Ws1, L1_ffn_Wsg, L1_ffn_Ws2,
           L1_ffn_gate_w, L1_ffn_gate_b, L1_ffn_biases):
    f32 = lambda a: np.ascontiguousarray(np.asarray(a), dtype=np.float32)
    x = f32(L0_attn_hidden_states)
    xs = [np.ascontiguousarray(x[b]) for b in range(8)]
    w_ffn = {"up_w": f32(L0_ffn_up_weight), "dn_w": f32(L0_ffn_down_weight),
             "up_scale": f32(L0_ffn_up_scale), "dn_scale": f32(L0_ffn_down_scale),
             "out_scale": f32(L0_ffn_output_scale)}
    w_a0 = {"Wq": f32(L0_attn_Wq), "Wk": f32(L0_attn_Wk), "Wv": f32(L0_attn_Wv),
            "Wg": f32(L0_attn_Wg), "Wc": f32(L0_attn_Wc), "bc": f32(L0_attn_bc)}
    w_moe = {"W1": f32(L1_ffn_W1), "Wg": f32(L1_ffn_Wg), "W2": f32(L1_ffn_W2),
             "Ws1": f32(L1_ffn_Ws1), "Wsg": f32(L1_ffn_Wsg), "Ws2": f32(L1_ffn_Ws2),
             "gate_w": f32(L1_ffn_gate_w), "gate_b": f32(L1_ffn_gate_b),
             "biases": f32(L1_ffn_biases)}
    w_a1 = {"Wq": f32(L1_attn_Wq), "Wk": f32(L1_attn_Wk), "Wv": f32(L1_attn_Wv),
            "Wo": f32(L1_attn_Wo)}
    xs = _launch("l0ffn", w_ffn, xs)
    xs = _launch("l0attn", w_a0, xs)
    xs = _launch("l0ffn", w_ffn, xs)
    xs = _launch("l1moe", w_moe, xs)
    xs = _launch("l1attn", w_a1, xs)
    xs = _launch("l1moe", w_moe, xs)
    return np.stack(xs, axis=0).astype(np.float32)


# version 04
# speedup vs baseline: 1.1145x; 1.1145x over previous
import math
from contextlib import ExitStack
import numpy as np
import concourse.bass as bass
import concourse.mybir as mybir
from concourse.bass_utils import run_bass_kernel_spmd




F32 = mybir.dt.float32
BF16 = mybir.dt.bfloat16
AF = mybir.ActivationFunctionType
ALU = mybir.AluOpType
AX = mybir.AxisListType

ENGINES = ("tensor", "vector", "scalar", "gpsimd", "sync")
U8 = mybir.dt.uint8


def _dtsize(dt):
    return mybir.dt.size(dt)


class Arena:

    def __init__(self, bk, name, nbytes, addr=None):
        self.bk = bk
        self.name = name
        if addr is None:
            self.base_t = bk.sb(name, [128, nbytes], U8)
            self.addr = bk.nc.lookup_mloc(self.base_t).addr
        else:
            self.addr = addr
        self.nbytes = nbytes
        self.cur = 0
        self.n = 0

    def child(self, name):
        off = (self.cur + 31) // 32 * 32
        return Arena(self.bk, name, self.nbytes - off, addr=self.addr + off)

    def reset(self):
        self.cur = 0

    def alloc(self, name, shape, dtype):
        per = 1
        for d in shape[1:]:
            per *= d
        per *= _dtsize(dtype)
        off = (self.cur + 31) // 32 * 32
        assert off + per <= self.nbytes, (
            f"arena {self.name} overflow at {name}: {off}+{per} > {self.nbytes}")
        self.cur = off + per
        self.n += 1
        t = self.bk.nc.alloc_sbuf_tensor_at(
            f"{self.name}_{self.n}_{name}", shape, dtype, offset=self.addr + off)
        return t


class Slot:
    __slots__ = ("name", "writer", "readers")

    def __init__(self, name=""):
        self.name = name
        self.writer = None
        self.readers = []


class Sem:
    __slots__ = ("name", "handle", "count")

    def __init__(self, name, handle):
        self.name = name
        self.handle = handle
        self.count = 0


class BK:
    def __init__(self, nc):
        self.nc = nc
        self.stack = ExitStack()
        self.streams = {e: [] for e in ENGINES}
        self.watermark = {e: {} for e in ENGINES}
        self.sems = {}
        self.eng_sem = {}
        self.phase = 0
        self._pe_pending_reads = []
        self._pe_pending_writes = []
        self.new_phase()

    def sem(self, name):
        s = self.sems.get(name)
        if s is None:
            h = self.stack.enter_context(self.nc.semaphore(name))
            s = Sem(name, h)
            self.sems[name] = s
        return s

    def new_phase(self):
        self.phase += 1
        for e in ENGINES:
            self.eng_sem[e] = self.sem(f"ph{self.phase}_{e}")

    def sb(self, name, shape, dtype):
        t = self.stack.enter_context(self.nc.sbuf_tensor(name, shape, dtype))
        return t

    def ps(self, name, shape, dtype=F32):
        t = self.stack.enter_context(self.nc.psum_tensor(name, shape, dtype))
        return t

    def _collect(self, engine, reads, writes):
        need = {}

        def add(dep):
            if dep is None:
                return
            s, c, peng = dep
            if engine == "tensor" and peng == "tensor":
                return
            if need.get(s, 0) < c:
                need[s] = c

        for sl in reads:
            add(sl.writer)
        for sl in writes:
            add(sl.writer)
            for r in sl.readers:
                add(r)
        wm = self.watermark[engine]
        waits = []
        for s, c in need.items():
            if wm.get(s, 0) < c:
                waits.append((s, c))
                wm[s] = c
        return waits

    def emit(self, engine, fn, reads=(), writes=(), dma=None, tag=""):
        waits = self._collect(engine, reads, writes)
        if dma is not None:
            msem, inc = dma, 16
        else:
            msem, inc = self.eng_sem[engine], 1
        msem.count += inc
        marker = (msem, msem.count, engine)

        def op(eng, fn=fn, waits=tuple(waits), msem=msem, inc=inc, tag=tag):
            for s, c in waits:
                eng.wait_ge(s.handle, c)
            inst = fn(eng)
            inst.then_inc(msem.handle, inc)

        self.streams[engine].append(op)
        for sl in writes:
            sl.writer = marker
            sl.readers = []
        for sl in reads:
            sl.readers.append(marker)
        return marker

    def mm(self, fn, reads=(), psum=None, first=True, last=True, tag=""):
        eff_writes = (psum,) if (first and psum is not None) else ()
        waits = self._collect("tensor", reads, eff_writes)
        msem = self.eng_sem["tensor"]
        msem.count += 1
        marker = (msem, msem.count, "tensor")

        def op(eng, fn=fn, waits=tuple(waits), msem=msem, tag=tag):
            for s, c in waits:
                eng.wait_ge(s.handle, c)
            inst = fn(eng)
            inst.then_inc(msem.handle, 1)

        self.streams["tensor"].append(op)
        for sl in reads:
            sl.readers.append(marker)
        if psum is not None:
            if first:
                psum.readers = []
            if last:
                psum.writer = marker
        return marker

    def barrier(self, exclude=()):
        pairs = [(s, s.count) for s in self.sems.values()
                 if s.count > 0 and s not in exclude]
        bsem = self.eng_sem["sync"]
        bsem.count += 1
        bcount = bsem.count

        def op_sync(eng, pairs=tuple(pairs), bsem=bsem):
            for s, c in pairs:
                eng.wait_ge(s.handle, c)
            eng.nop().then_inc(bsem.handle, 1)

        self.streams["sync"].append(op_sync)
        wm = self.watermark["sync"]
        for s, c in pairs:
            wm[s] = max(wm.get(s, 0), c)
        for e in ENGINES:
            if e == "sync":
                continue

            def op(eng, bsem=bsem, bcount=bcount):
                eng.wait_ge(bsem.handle, bcount)

            self.streams[e].append(op)
            wme = self.watermark[e]
            wme[bsem] = max(wme.get(bsem, 0), bcount)
            for s, c in pairs:
                wme[s] = max(wme.get(s, 0), c)

    def seal(self, sem, slots):
        for sl in slots:
            if sl.writer is not None and sl.writer[0] is sem:
                sl.writer = (sem, sem.count, sl.writer[2])
            sl.readers = [(s, sem.count if s is sem else c, e2)
                          for (s, c, e2) in sl.readers]

    def wait_all(self, engine, sems):
        pairs = [(s, s.count) for s in sems]

        def op(eng, pairs=tuple(pairs)):
            for s, c in pairs:
                eng.wait_ge(s.handle, c)

        self.streams[engine].append(op)

    def barrier_wait(self, engine, slots):
        waits = self._collect(engine, slots, ())

        def op(eng, waits=tuple(waits)):
            for s, c in waits:
                eng.wait_ge(s.handle, c)

        self.streams[engine].append(op)

    def scope_mark(self, name):
        for e in ENGINES:
            self.streams[e].append(name)

    def finalize(self):
        nc = self.nc

        def replay(eng, ops):
            cur = None
            for op in ops:
                if isinstance(op, str):
                    if cur is not None:
                        nc.leave_named_scope(cur[0], cur[1], False)
                    cur = (op, nc.enter_named_scope(op, False)[0])
                else:
                    op(eng)
            if cur is not None:
                nc.leave_named_scope(cur[0], cur[1], False)

        with nc.Block() as blk:
            for ename in ENGINES:
                ops = self.streams[ename]
                getattr(blk, ename)(lambda eng, ops=ops: replay(eng, ops))
        self.stack.close()
        return nc



T = 2048
C = 2048
NT = 16
NCH = 16
HF = 8192
LNEPS = 1e-5
TCW = 1024
NTCOL = T // TCW


def configure(t=2048, c=2048, hf=8192, tcw=1024):
    global T, C, NT, NCH, HF, TCW, NTCOL
    T, C, HF, TCW = t, c, hf, tcw
    NT, NCH, NTCOL = T // 128, C // 128, T // TCW


def S(n=""):
    return Slot(n)


class Common:

    def __init__(self, nc, bk):
        self.nc, self.bk = nc, bk
        sb = bk.sb
        self.eps = sb("eps", [128, 1], F32)
        self.s_eps = S()
        bk.emit("vector", lambda e: e.memset(self.eps[:], LNEPS), writes=[self.s_eps])
        self.onesr = sb("onesr", [1, 128], BF16)
        self.s_onesr = S()
        bk.emit("vector", lambda e: e.memset(self.onesr[:], 1.0), writes=[self.s_onesr])
        self.onesrf = sb("onesrf", [1, 128], F32)
        self.s_onesrf = S()
        bk.emit("vector", lambda e: e.memset(self.onesrf[:], 1.0), writes=[self.s_onesrf])
        self.onecol = sb("onecol", [128, 1], BF16)
        self.s_onecol = S()
        bk.emit("vector", lambda e: e.memset(self.onecol[:], 1.0), writes=[self.s_onecol])
        self.identf = sb("identf", [128, 128], F32)
        self.s_identf = S()
        bk.emit("vector", lambda e: e.memset(self.identf[:], 1.0), writes=[self.s_identf])
        bk.emit("gpsimd", lambda e: e.affine_select(
            out=self.identf[:], in_=self.identf[:], pattern=[[1, 128]],
            compare_op=ALU.is_equal, fill=0.0, base=0, channel_multiplier=-1),
            reads=[self.s_identf], writes=[self.s_identf])
        self.lx = [sb(f"lnx{j}", [128, C], F32) for j in range(2)]
        self.s_lx = [S() for _ in range(2)]
        self.zt = [sb(f"lnz{j}", [128, C], BF16) for j in range(2)]
        self.s_zt = [S() for _ in range(2)]
        self.stat = [[sb(f"lnst{j}_{k}", [128, 1], F32) for k in range(5)] for j in range(2)]
        self.s_stat = [[S() for _ in range(5)] for _ in range(2)]
        self.d_lx = [bk.sem(f"c_lx{j}") for j in range(2)]
        self.d_zs = [bk.sem(f"c_zs{j}") for j in range(2)]
        self.xa = [sb(f"xa{i}", [128, C], F32) for i in range(TCW // 128)]
        self.s_xa = [S() for _ in range(TCW // 128)]
        self.d_xa = [bk.sem(f"c_xa{j}") for j in range(2)]
        self.d_xo = [bk.sem(f"c_xo{j}") for j in range(2)]
        self.pp = [bk.ps(f"cpp{i}", [128, 512]) for i in range(8)]
        self.s_pp = [S() for _ in range(8)]

    def col_table(self, name, src_1d, n, dsem, arena):
        bk = self.bk
        assert n <= 128
        row = arena.alloc(name + "_r", [n, 128], F32)
        out = arena.alloc(name, [128, n], F32)
        pst = self.pp[7][:, 0:n]
        s_row, s_out, s_ps = S(), S(), self.s_pp[7]
        bk.emit("sync", lambda e: e.dma_start(
            out=row[:], in_=src_1d.rearrange("(a b) -> a b", b=128)),
            writes=[s_row], dma=dsem)
        bk.seal(dsem, [s_row])
        bk.mm(lambda e: e.transpose(pst, row[:], self.identf[0:n, 0:n]),
              reads=[s_row, self.s_identf], psum=s_ps)
        bk.emit("vector", lambda e: e.tensor_copy(out=out[:], in_=pst),
                reads=[s_ps], writes=[s_out])
        return out, s_out

    def ln_pass(self, x_d, zscr_d, s_zscr):
        bk = self.bk
        for i in range(NT):
            j = i % 2
            st, s_st = self.stat[j], self.s_stat[j]
            x, s_x = self.lx[j], self.s_lx[j]
            bk.emit("sync", lambda e, i=i, x=x: e.dma_start(
                out=x[:], in_=x_d[i * 128:(i + 1) * 128, :]),
                writes=[s_x], dma=self.d_lx[j])
            bk.emit("vector", lambda e, x=x, st=st: e.reduce_sum(out=st[0][:], in_=x[:], axis=AX.X),
                    reads=[s_x], writes=[s_st[0]])
            bk.emit("scalar", lambda e, x=x, st=st, j=j: e.activation(
                out=self.zt[j][:], in_=x[:], func=AF.Square, accum_out=st[1][:]),
                reads=[s_x], writes=[self.s_zt[j], s_st[1]])
            bk.emit("vector", lambda e, st=st: e.tensor_scalar(out=st[2][:], in0=st[0][:],
                    scalar1=-1.0 / C, scalar2=None, op0=ALU.mult),
                    reads=[s_st[0]], writes=[s_st[2]])
            bk.emit("vector", lambda e, st=st: e.tensor_tensor(out=st[3][:], in0=st[2][:],
                    in1=st[2][:], op=ALU.mult), reads=[s_st[2]], writes=[s_st[3]])
            bk.emit("vector", lambda e, st=st: e.scalar_tensor_tensor(out=st[3][:], in0=st[1][:],
                    scalar=1.0 / C, in1=st[3][:], op0=ALU.mult, op1=ALU.subtract),
                    reads=[s_st[1], s_st[3]], writes=[s_st[3]])
            bk.emit("scalar", lambda e, st=st: e.activation(out=st[3][:], in_=st[3][:],
                    func=AF.Sqrt, bias=self.eps[:, 0:1]),
                    reads=[s_st[3], self.s_eps], writes=[s_st[3]])
            bk.emit("vector", lambda e, st=st: e.reciprocal(out=st[4][:], in_=st[3][:]),
                    reads=[s_st[3]], writes=[s_st[4]])
            bk.emit("vector", lambda e, x=x, st=st, j=j: e.tensor_scalar(out=self.zt[j][:],
                    in0=x[:], scalar1=st[2][:, 0:1], scalar2=st[4][:, 0:1],
                    op0=ALU.add, op1=ALU.mult),
                    reads=[s_x, s_st[2], s_st[4]], writes=[self.s_zt[j]])
            bk.emit("scalar", lambda e, i=i, j=j: e.dma_start(
                out=zscr_d[i * 128:(i + 1) * 128, :], in_=self.zt[j][:]),
                reads=[self.s_zt[j]], writes=[s_zscr[j]], dma=self.d_zs[j])
        for j in range(2):
            bk.seal(self.d_zs[j], [s_zscr[j]])

    def load_xacc(self, x_d, tc):
        bk = self.bk
        sem = self.d_xa[tc % 2]
        for k in range(TCW // 128):
            i = tc * (TCW // 128) + k
            bk.emit("sync", lambda e, i=i, k=k: e.dma_start(
                out=self.xa[k][:], in_=x_d[i * 128:(i + 1) * 128, :]),
                writes=[self.s_xa[k]], dma=sem)
        bk.seal(sem, self.s_xa)

    def store_xacc(self, xo_d, tc):
        bk = self.bk
        for k in range(TCW // 128):
            i = tc * (TCW // 128) + k
            bk.emit("sync", lambda e, i=i, k=k: e.dma_start(
                out=xo_d[i * 128:(i + 1) * 128, :], in_=self.xa[k][:]),
                reads=[self.s_xa[k]], dma=self.d_xo[tc % 2])
        bk.seal(self.d_xo[tc % 2], self.s_xa)

    def finish_x(self):
        self.bk.wait_all("sync", self.d_xo)



def emit_l0ffn_casts(bk, W):
    s_upb = [S() for _ in range(4)]
    s_dnb = [S() for _ in range(4)]
    for p in range(4):
        w = C // 4
        bk.emit("gpsimd", lambda e, p=p, w=w: e.dma_start(
            out=W["upb"][:, p * w:(p + 1) * w], in_=W["up"][:, p * w:(p + 1) * w]),
            writes=[s_upb[p]], dma=bk.sem(f"f_cup{p}"))
    for p in range(4):
        w = HF // 4
        bk.emit("gpsimd", lambda e, p=p, w=w: e.dma_start(
            out=W["dnb"][:, p * w:(p + 1) * w], in_=W["dn"][:, p * w:(p + 1) * w]),
            writes=[s_dnb[p]], dma=bk.sem(f"f_cdn{p}"))
    return s_upb, s_dnb


def emit_l0ffn(nc, bk, cm, W, zscr_d, x_d, xo_d, arena, s_upb, s_dnb, app=0):
    sb, ps = bk.sb, bk.ps
    s_zscr = [S("zscr0"), S("zscr1")]

    cm.ln_pass(x_d, zscr_d, s_zscr)

    usc, s_usc = cm.col_table("usc", W["us"], HF // 128, bk.sem("f_usc"), arena)

    dsrow = arena.alloc("dsrow", [1, C], F32)
    dsrow_b = arena.alloc("dsrow_b", [1, C], BF16)
    osct = arena.alloc("osct", [1, 1], F32)
    dsb = arena.alloc("dsb", [128, C], BF16)
    ps_b = cm.pp[6]
    s_dsrow, s_dsrow_b, s_osc, s_dsb, s_psb = S(), S(), S(), S(), cm.s_pp[6]
    d_ds = bk.sem("f_ds")
    bk.emit("sync", lambda e: e.dma_start(out=dsrow[:],
            in_=W["ds"].rearrange("(b a) -> b a", b=1)), writes=[s_dsrow], dma=d_ds)
    bk.emit("sync", lambda e: e.dma_start(out=osct[:],
            in_=W["os"].rearrange("(b a) -> b a", b=1)), writes=[s_osc], dma=d_ds)
    bk.seal(d_ds, [s_dsrow, s_osc])
    bk.emit("vector", lambda e: e.tensor_scalar(out=dsrow[:], in0=dsrow[:],
            scalar1=osct[0:1, 0:1], scalar2=0.5, op0=ALU.mult, op1=ALU.mult),
            reads=[s_dsrow, s_osc], writes=[s_dsrow])
    bk.emit("vector", lambda e: e.tensor_copy(out=dsrow_b[:], in_=dsrow[:]),
            reads=[s_dsrow], writes=[s_dsrow_b])
    for q in range(C // 512 if C >= 512 else 1):
        bk.mm(lambda e, q=q: e.matmul(ps_b[:], cm.onesr[:], dsrow_b[:, q * 512:(q + 1) * 512] if C >= 512 else dsrow_b[:],
              start=True, stop=True), reads=[cm.s_onesr, s_dsrow_b], psum=s_psb)
        bk.emit("vector", lambda e, q=q: e.tensor_copy(out=dsb[:, q * 512:(q + 1) * 512],
                in_=ps_b[:]), reads=[s_psb], writes=[s_dsb])

    upT = [arena.alloc(f"fupT{c}", [128, 512], BF16) for c in range(NCH)]
    s_upT = [S() for _ in range(NCH)]
    dnT = [arena.alloc(f"fdnT{h}", [128, C], BF16) for h in range(4)]
    s_dnT = [S() for _ in range(4)]
    zT = [arena.alloc(f"fzT{c}", [128, TCW], BF16) for c in range(NCH)]
    s_zT = [S() for _ in range(NCH)]
    hidT = [[arena.alloc(f"fhid{p}_{h}", [128, TCW], BF16) for h in range(4)] for p in range(2)]
    s_hid = [[S() for _ in range(4)] for _ in range(2)]
    psh = cm.pp[0:4]
    s_psh = cm.s_pp[0:4]
    pso = cm.pp[4:6]
    s_pso = cm.s_pp[4:6]
    d_up = [bk.sem(f"f_up{j}") for j in range(2)]
    d_dn = [bk.sem(f"f_dn{j}") for j in range(2)]
    d_zt = bk.sem("f_zt")

    NHP = HF // 512
    for tc in range(NTCOL):
        cm.load_xacc(x_d, tc)
        for c in range(NCH):
            bk.emit("scalar", lambda e, c=c, tc=tc: e.dma_start_transpose(
                out=zT[c][:], in_=zscr_d[tc * TCW:(tc + 1) * TCW, c * 128:(c + 1) * 128]),
                reads=s_zscr, writes=[s_zT[c]], dma=d_zt)
        bk.seal(d_zt, s_zT)
        for hp in range(NHP):
            par = hp % 2
            sem = d_up[par]
            for c in range(NCH):
                cp = s_upb[(c * 128) // (C // 4)]
                bk.emit("scalar", lambda e, c=c, hp=hp: e.dma_start_transpose(
                    out=upT[c][:], in_=W["upb"][hp * 512:(hp + 1) * 512, c * 128:(c + 1) * 128]),
                    reads=[cp], writes=[s_upT[c]], dma=sem)
            bk.seal(sem, s_upT)
            for c in range(NCH):
                bk.emit("scalar", lambda e, c=c: e.sign(out=upT[c][:], in_=upT[c][:]),
                        reads=[s_upT[c]], writes=[s_upT[c]])
            sem = d_dn[par]
            for hc in range(4):
                col = hp * 4 + hc
                cp = s_dnb[(col * 128) // (HF // 4)]
                bk.emit("scalar", lambda e, hc=hc, col=col: e.dma_start_transpose(
                    out=dnT[hc][:], in_=W["dnb"][:, col * 128:(col + 1) * 128]),
                    reads=[cp], writes=[s_dnT[hc]], dma=sem)
            bk.seal(sem, s_dnT)
            for hc in range(4):
                bk.emit("scalar", lambda e, hc=hc: e.sign(out=dnT[hc][:], in_=dnT[hc][:]),
                        reads=[s_dnT[hc]], writes=[s_dnT[hc]])
                bk.emit("vector", lambda e, hc=hc: e.tensor_tensor(out=dnT[hc][:],
                        in0=dnT[hc][:], in1=dsb[:], op=ALU.mult),
                        reads=[s_dnT[hc], s_dsb], writes=[s_dnT[hc]])
            for hb in range(4):
                for th in range(TCW // 512):
                    pp = (hb * (TCW // 512) + th) % 4
                    for c in range(NCH):
                        bk.mm(lambda e, hb=hb, th=th, c=c, pp=pp: e.matmul(
                            psh[pp][:], upT[c][:, hb * 128:(hb + 1) * 128],
                            zT[c][:, th * 512:(th + 1) * 512],
                            start=(c == 0), stop=(c == NCH - 1)),
                            reads=[s_upT[c], s_zT[c]], psum=s_psh[pp],
                            first=(c == 0), last=(c == NCH - 1))
                    hcol = hp * 4 + hb
                    bk.emit("scalar", lambda e, hb=hb, th=th, par=par, hcol=hcol, pp=pp:
                            e.activation(out=hidT[par][hb][:, th * 512:(th + 1) * 512],
                            in_=psh[pp][:], func=AF.Relu, scale=usc[:, hcol:hcol + 1]),
                            reads=[s_psh[pp], s_usc], writes=[s_hid[par][hb]])
            for tt in range(TCW // 128):
                for ccol in range(C // 512):
                    o = (tt * 4 + ccol) % 2
                    for hc in range(4):
                        bk.mm(lambda e, hc=hc, tt=tt, ccol=ccol, par=par, o=o: e.matmul(
                            pso[o][:], hidT[par][hc][:, tt * 128:(tt + 1) * 128],
                            dnT[hc][:, ccol * 512:(ccol + 1) * 512],
                            start=(hc == 0), stop=(hc == 3)),
                            reads=[s_hid[par][hc], s_dnT[hc]], psum=s_pso[o],
                            first=(hc == 0), last=(hc == 3))
                    bk.emit("vector", lambda e, tt=tt, ccol=ccol, o=o: e.tensor_add(
                        out=cm.xa[tt][:, ccol * 512:(ccol + 1) * 512],
                        in0=cm.xa[tt][:, ccol * 512:(ccol + 1) * 512], in1=pso[o][:]),
                        reads=[s_pso[o], cm.s_xa[tt]], writes=[cm.s_xa[tt]])
        cm.store_xacc(xo_d, tc)
    cm.finish_x()





def S(n=""):
    return Slot(n)


def emit_l1moe_casts(bk, W):
    NE = 8
    s_cast = [S(f"cast{e}") for e in range(NE)]
    for e in range(NE):
        sem = bk.sem(f"m_cast{e}")
        srcs = ([W["W1"][e], W["Wg"][e], W["W2"][e]] if e < 7
                else [W["Ws1"], W["Wsg"], W["Ws2"]])
        for k, nm in enumerate(("W1b", "Wgb", "W2b")):
            bk.emit("gpsimd", lambda e2, e=e, k=k, nm=nm, srcs=srcs: e2.dma_start(
                out=W[nm][e], in_=srcs[k]), writes=[s_cast[e]], dma=sem)
        bk.seal(sem, [s_cast[e]])
    return s_cast


def emit_l1moe(nc, bk, cm, W, zscr_d, x_d, xo_d, arena, s_cast, app=0):
    sb, ps = bk.sb, bk.ps
    s_zscr = [S("mz0"), S("mz1")]
    NE = 8
    NTT = TCW // 128

    cm.ln_pass(x_d, zscr_d, s_zscr)

    gw = [arena.alloc(f"m_gw{c}", [128, 7], F32) for c in range(NCH)]
    gwb = [arena.alloc(f"m_gwb{c}", [128, 7], BF16) for c in range(NCH)]
    s_gw = [S() for _ in range(NCH)]
    s_gwb = [S() for _ in range(NCH)]
    d_gw = bk.sem("m_gw")
    gw3 = W["gate_w"].rearrange("(c p) g -> c p g", p=128)
    for c in range(NCH):
        bk.emit("sync", lambda e, c=c: e.dma_start(out=gw[c][:], in_=gw3[c]),
                writes=[s_gw[c]], dma=d_gw)
    bk.seal(d_gw, s_gw)
    for c in range(NCH):
        bk.emit("vector", lambda e, c=c: e.tensor_copy(out=gwb[c][:], in_=gw[c][:]),
                reads=[s_gw[c]], writes=[s_gwb[c]])
    gbrow = arena.alloc("m_gbrow", [1, 7], F32)
    birow = arena.alloc("m_birow", [1, 7], F32)
    gbb = arena.alloc("m_gbb", [128, 7], F32)
    bib = arena.alloc("m_bib", [128, 7], F32)
    ps_g = cm.pp[6][:, 0:8]
    s_gbrow, s_birow, s_gbb, s_bib, s_psg = S(), S(), S(), S(), cm.s_pp[6]
    d_gb = bk.sem("m_gb")
    bk.emit("sync", lambda e: e.dma_start(out=gbrow[:],
            in_=W["gate_b"].rearrange("(b a) -> b a", b=1)), writes=[s_gbrow], dma=d_gb)
    bk.emit("sync", lambda e: e.dma_start(out=birow[:],
            in_=W["biases"].rearrange("(b a) -> b a", b=1)), writes=[s_birow], dma=d_gb)
    bk.seal(d_gb, [s_gbrow, s_birow])
    bk.mm(lambda e: e.matmul(ps_g[:, 0:7], cm.onesrf[:], gbrow[:], start=True, stop=True),
          reads=[cm.s_onesrf, s_gbrow], psum=s_psg)
    bk.emit("vector", lambda e: e.tensor_copy(out=gbb[:], in_=ps_g[:, 0:7]),
            reads=[s_psg], writes=[s_gbb])
    bk.mm(lambda e: e.matmul(ps_g[:, 0:7], cm.onesrf[:], birow[:], start=True, stop=True),
          reads=[cm.s_onesrf, s_birow], psum=s_psg)
    bk.emit("vector", lambda e: e.tensor_copy(out=bib[:], in_=ps_g[:, 0:7]),
            reads=[s_psg], writes=[s_bib])

    zT = [arena.alloc(f"m_zT{c}", [128, TCW], BF16) for c in range(NCH)]
    s_zT = [S() for _ in range(NCH)]
    slab1a = arena.alloc("m_sl1", [128, NCH * 512], BF16)
    slabga = arena.alloc("m_slg", [128, NCH * 512], BF16)
    slab2a = arena.alloc("m_sl2", [128, 4 * C], BF16)
    slab1 = [slab1a[:, c * 512:(c + 1) * 512] for c in range(NCH)]
    slabg = [slabga[:, c * 512:(c + 1) * 512] for c in range(NCH)]
    slab2 = [slab2a[:, h * C:(h + 1) * C] for h in range(4)]
    s_slab1a, s_slabga, s_slab2a = S(), S(), S()
    s_slab1 = [s_slab1a] * NCH
    s_slabg = [s_slabga] * NCH
    s_slab2 = [s_slab2a] * 4
    s1t = [arena.alloc(f"m_s1t{h}", [128, TCW], BF16) for h in range(4)]
    hh = [arena.alloc(f"m_hh{h}", [128, TCW], BF16) for h in range(4)]
    s_s1t = [S() for _ in range(4)]
    s_hh = [S() for _ in range(4)]
    wcol = [arena.alloc(f"m_w{t}", [128, 8], F32) for t in range(NTT)]
    s_wcol = [S() for _ in range(NTT)]
    glog = arena.alloc("m_glog", [128, 8], F32)
    gtmp = arena.alloc("m_gtmp", [128, 8], F32)
    gsel = arena.alloc("m_gsel", [128, 8], F32)
    gt2 = arena.alloc("m_gt2", [128, 8], F32)
    gcol = [arena.alloc(f"m_gc{k}", [128, 1], F32) for k in range(3)]
    s_glog, s_gtmp, s_gsel, s_gt2 = S(), S(), S(), S()
    s_gcol = [S() for _ in range(3)]
    psh = cm.pp[0:4]
    s_psh = cm.s_pp[0:4]
    pso = cm.pp[4:6]
    s_pso = cm.s_pp[4:6]
    d_zt = bk.sem("m_zt")
    d_s1 = [bk.sem(f"m_dsl1{j}") for j in range(2)]
    d_sg = [bk.sem(f"m_dslg{j}") for j in range(2)]
    d_s2 = [bk.sem(f"m_dsl2{j}") for j in range(2)]
    NTH = TCW // 512

    def src1(nm, e_i):
        w = W[nm][e_i] if e_i < 7 else W["Ws" + nm[1]]
        return w.rearrange("(c p) h -> p c h", p=128)

    def src2(e_i):
        w = W["W2"][e_i] if e_i < 7 else W["Ws2"]
        return w.rearrange("(h p) c -> p h c", p=128)

    for tc in range(NTCOL):
        cm.load_xacc(x_d, tc)
        for c in range(NCH):
            bk.emit("scalar", lambda e, c=c, tc=tc: e.dma_start_transpose(
                out=zT[c][:], in_=zscr_d[tc * TCW:(tc + 1) * TCW, c * 128:(c + 1) * 128]),
                reads=s_zscr, writes=[s_zT[c]], dma=d_zt)
        bk.seal(d_zt, s_zT)
        bk.scope_mark(f"m_gate_tc{tc}_{app}")
        for tt in range(NTT):
            for c in range(NCH):
                bk.mm(lambda e, tt=tt, c=c: e.matmul(
                    ps_g[:, 0:7], zT[c][:, tt * 128:(tt + 1) * 128], gwb[c][:],
                    start=(c == 0), stop=(c == NCH - 1)),
                    reads=[s_zT[c], s_gwb[c]], psum=s_psg,
                    first=(c == 0), last=(c == NCH - 1))
            bk.emit("vector", lambda e: e.tensor_tensor(out=glog[:, 0:7], in0=ps_g[:, 0:7],
                    in1=gbb[:, 0:7], op=ALU.add), reads=[s_psg, s_gbb], writes=[s_glog])
            bk.emit("vector", lambda e: e.reduce_max(out=gcol[0][:], in_=glog[:, 0:7],
                    axis=AX.X, negate=True), reads=[s_glog], writes=[s_gcol[0]])
            bk.emit("scalar", lambda e: e.activation(out=glog[:, 0:7], in_=glog[:, 0:7],
                    func=AF.Exp, bias=gcol[0][:, 0:1]),
                    reads=[s_glog, s_gcol[0]], writes=[s_glog])
            bk.emit("vector", lambda e: e.reduce_sum(out=gcol[1][:], in_=glog[:, 0:7],
                    axis=AX.X), reads=[s_glog], writes=[s_gcol[1]])
            bk.emit("vector", lambda e: e.reciprocal(out=gcol[2][:], in_=gcol[1][:]),
                    reads=[s_gcol[1]], writes=[s_gcol[2]])
            bk.emit("vector", lambda e: e.tensor_scalar(out=glog[:, 0:7], in0=glog[:, 0:7],
                    scalar1=gcol[2][:, 0:1], scalar2=None, op0=ALU.mult),
                    reads=[s_glog, s_gcol[2]], writes=[s_glog])
            bk.emit("vector", lambda e: e.tensor_tensor(out=gtmp[:, 0:7], in0=glog[:, 0:7],
                    in1=bib[:, 0:7], op=ALU.add), reads=[s_glog, s_bib], writes=[s_gtmp])
            bk.emit("vector", lambda e: e.reduce_max(out=gcol[0][:], in_=gtmp[:, 0:7],
                    axis=AX.X), reads=[s_gtmp], writes=[s_gcol[0]])
            bk.emit("vector", lambda e: e.tensor_scalar(out=gsel[:, 0:7], in0=gtmp[:, 0:7],
                    scalar1=gcol[0][:, 0:1], scalar2=None, op0=ALU.is_ge),
                    reads=[s_gtmp, s_gcol[0]], writes=[s_gsel])
            bk.emit("vector", lambda e: e.scalar_tensor_tensor(out=gt2[:, 0:7],
                    in0=gsel[:, 0:7], scalar=-1e9, in1=gtmp[:, 0:7],
                    op0=ALU.mult, op1=ALU.add), reads=[s_gsel, s_gtmp], writes=[s_gt2])
            bk.emit("vector", lambda e: e.reduce_max(out=gcol[1][:], in_=gt2[:, 0:7],
                    axis=AX.X), reads=[s_gt2], writes=[s_gcol[1]])
            bk.emit("vector", lambda e: e.tensor_scalar(out=gsel[:, 0:7], in0=gt2[:, 0:7],
                    scalar1=gcol[1][:, 0:1], scalar2=None, op0=ALU.is_ge),
                    reads=[s_gt2, s_gcol[1]], writes=[s_gsel])
            bk.emit("vector", lambda e: e.scalar_tensor_tensor(out=gt2[:, 0:7],
                    in0=gsel[:, 0:7], scalar=-1e9, in1=gt2[:, 0:7],
                    op0=ALU.mult, op1=ALU.add), reads=[s_gsel, s_gt2], writes=[s_gt2])
            bk.emit("vector", lambda e: e.reduce_max(out=gcol[2][:], in_=gt2[:, 0:7],
                    axis=AX.X), reads=[s_gt2], writes=[s_gcol[2]])
            bk.emit("vector", lambda e: e.tensor_scalar(out=gsel[:, 0:7], in0=gtmp[:, 0:7],
                    scalar1=gcol[2][:, 0:1], scalar2=None, op0=ALU.is_ge),
                    reads=[s_gtmp, s_gcol[2]], writes=[s_gsel])
            bk.emit("vector", lambda e, tt=tt: e.tensor_tensor(out=wcol[tt][:, 0:7],
                    in0=glog[:, 0:7], in1=gsel[:, 0:7], op=ALU.mult),
                    reads=[s_glog, s_gsel], writes=[s_wcol[tt]])
            bk.emit("vector", lambda e, tt=tt: e.reduce_sum(out=gcol[0][:],
                    in_=wcol[tt][:, 0:7], axis=AX.X), reads=[s_wcol[tt]], writes=[s_gcol[0]])
            bk.emit("vector", lambda e: e.reciprocal(out=gcol[1][:], in_=gcol[0][:]),
                    reads=[s_gcol[0]], writes=[s_gcol[1]])
            bk.emit("vector", lambda e, tt=tt: e.tensor_scalar(out=wcol[tt][:, 0:7],
                    in0=wcol[tt][:, 0:7], scalar1=gcol[1][:, 0:1], scalar2=0.5,
                    op0=ALU.mult, op1=ALU.mult), reads=[s_wcol[tt], s_gcol[1]],
                    writes=[s_wcol[tt]])
        bk.scope_mark(f"m_exp_tc{tc}_{app}")
        for e_i in range(NE):
            for hp in range(C // 512):
                par = hp % 2
                sem1 = d_s1[par]
                bk.emit("gpsimd", lambda e, e_i=e_i, hp=hp: e.dma_start(
                    out=slab1a.rearrange("p (c h) -> p c h", h=512),
                    in_=src1("W1", e_i)[:, :, hp * 512:(hp + 1) * 512]),
                    writes=[s_slab1a], dma=sem1)
                bk.seal(sem1, [s_slab1a])
                semg = d_sg[par]
                bk.emit("gpsimd", lambda e, e_i=e_i, hp=hp: e.dma_start(
                    out=slabga.rearrange("p (c h) -> p c h", h=512),
                    in_=src1("Wg", e_i)[:, :, hp * 512:(hp + 1) * 512]),
                    writes=[s_slabga], dma=semg)
                bk.seal(semg, [s_slabga])
                sem2 = d_s2[par]
                bk.emit("gpsimd", lambda e, e_i=e_i, hp=hp: e.dma_start(
                    out=slab2a.rearrange("p (h c) -> p h c", c=C),
                    in_=src2(e_i)[:, hp * 4:(hp + 1) * 4, :]),
                    writes=[s_slab2a], dma=sem2)
                bk.seal(sem2, [s_slab2a])
                for hb in range(4):
                    for th in range(NTH):
                        pp = (hb * NTH + th) % 2
                        for c in range(NCH):
                            bk.mm(lambda e, hb=hb, th=th, c=c, pp=pp: e.matmul(
                                psh[pp][:], slab1[c][:, hb * 128:(hb + 1) * 128],
                                zT[c][:, th * 512:(th + 1) * 512],
                                start=(c == 0), stop=(c == NCH - 1)),
                                reads=[s_slab1[c], s_zT[c]], psum=s_psh[pp],
                                first=(c == 0), last=(c == NCH - 1))
                        bk.emit("scalar", lambda e, hb=hb, th=th, pp=pp: e.activation(
                            out=s1t[hb][:, th * 512:(th + 1) * 512], in_=psh[pp][:],
                            func=AF.Silu), reads=[s_psh[pp]], writes=[s_s1t[hb]])
                        pp2 = 2 + (hb * NTH + th) % 2
                        for c in range(NCH):
                            bk.mm(lambda e, hb=hb, th=th, c=c, pp2=pp2: e.matmul(
                                psh[pp2][:], slabg[c][:, hb * 128:(hb + 1) * 128],
                                zT[c][:, th * 512:(th + 1) * 512],
                                start=(c == 0), stop=(c == NCH - 1)),
                                reads=[s_slabg[c], s_zT[c]], psum=s_psh[pp2],
                                first=(c == 0), last=(c == NCH - 1))
                        bk.emit("vector", lambda e, hb=hb, th=th, pp2=pp2: e.tensor_tensor(
                            out=hh[hb][:, th * 512:(th + 1) * 512], in0=psh[pp2][:],
                            in1=s1t[hb][:, th * 512:(th + 1) * 512], op=ALU.mult),
                            reads=[s_psh[pp2], s_s1t[hb]], writes=[s_hh[hb]])
                for tt in range(NTT):
                    gt = tc * NTT + tt
                    for ccol in range(C // 512):
                        o = (tt * (C // 512) + ccol) % 2
                        for hc in range(4):
                            bk.mm(lambda e, hc=hc, tt=tt, ccol=ccol, o=o: e.matmul(
                                pso[o][:], hh[hc][:, tt * 128:(tt + 1) * 128],
                                slab2[hc][:, ccol * 512:(ccol + 1) * 512],
                                start=(hc == 0), stop=(hc == 3)),
                                reads=[s_hh[hc], s_slab2[hc]], psum=s_pso[o],
                                first=(hc == 0), last=(hc == 3))
                        if e_i == 7:
                            bk.emit("vector", lambda e, tt=tt, ccol=ccol, o=o:
                                    e.scalar_tensor_tensor(
                                    out=cm.xa[tt][:, ccol * 512:(ccol + 1) * 512],
                                    in0=pso[o][:], scalar=0.5,
                                    in1=cm.xa[tt][:, ccol * 512:(ccol + 1) * 512],
                                    op0=ALU.mult, op1=ALU.add),
                                    reads=[s_pso[o], cm.s_xa[tt]], writes=[cm.s_xa[tt]])
                        else:
                            bk.emit("vector", lambda e, tt=tt, ccol=ccol, o=o, e_i=e_i, gt=gt:
                                    e.scalar_tensor_tensor(
                                    out=cm.xa[tt][:, ccol * 512:(ccol + 1) * 512],
                                    in0=pso[o][:], scalar=wcol[tt][:, e_i:e_i + 1],
                                    in1=cm.xa[tt][:, ccol * 512:(ccol + 1) * 512],
                                    op0=ALU.mult, op1=ALU.add),
                                    reads=[s_pso[o], cm.s_xa[tt], s_wcol[tt]],
                                    writes=[cm.s_xa[tt]])
        cm.store_xacc(xo_d, tc)
    cm.finish_x()





def S(n=""):
    return Slot(n)


EW = 256


def emit_l1attn_casts(bk, W):
    s_cast = {}
    for nm in ("q", "k", "v", "o"):
        sem = bk.sem(f"a1_c{nm}")
        s_cast[nm] = S()
        bk.emit("gpsimd", lambda e, nm=nm: e.dma_start(
            out=W[nm + "b"], in_=W["W" + nm]), writes=[s_cast[nm]], dma=sem)
        bk.seal(sem, [s_cast[nm]])
    return s_cast


def emit_l1attn(nc, bk, cm, W, zscr_d, x_d, xo_d, arena, s_cast):
    sb, ps = bk.sb, bk.ps
    s_zscr = [S(), S()]
    NTT = TCW // 128
    NTH = TCW // 512
    NEP = C // EW
    NEB = EW // 128
    denom = float(C) ** 0.5 / float(T)

    cm.ln_pass(x_d, zscr_d, s_zscr)

    zT = [arena.alloc(f"a1_zT{c}", [128, TCW], BF16) for c in range(NCH)]
    s_zT = [S() for _ in range(NCH)]
    wq = [arena.alloc(f"a1_wq{c}", [128, EW], BF16) for c in range(NCH)]
    wk = [arena.alloc(f"a1_wk{c}", [128, EW], BF16) for c in range(NCH)]
    wv = [arena.alloc(f"a1_wv{c}", [128, EW], BF16) for c in range(NCH)]
    s_wq = [S() for _ in range(NCH)]
    s_wk = [S() for _ in range(NCH)]
    s_wv = [S() for _ in range(NCH)]
    wo = [arena.alloc(f"a1_wo{b}", [128, C], BF16) for b in range(NEB)]
    s_wo = [S() for _ in range(NEB)]
    qf = [arena.alloc(f"a1_qf{b}", [128, TCW], F32) for b in range(NEB)]
    kf = [arena.alloc(f"a1_kf{b}", [128, TCW], F32) for b in range(NEB)]
    vf = [arena.alloc(f"a1_vf{b}", [128, TCW], F32) for b in range(NEB)]
    kvt = [arena.alloc(f"a1_kvt{b}", [128, TCW], F32) for b in range(NEB)]
    vhb = [arena.alloc(f"a1_vhb{b}", [128, TCW], BF16) for b in range(NEB)]
    s_qf = [S() for _ in range(NEB)]
    s_kf = [S() for _ in range(NEB)]
    s_vf = [S() for _ in range(NEB)]
    s_kvt = [S() for _ in range(NEB)]
    s_vhb = [S() for _ in range(NEB)]
    zeros = arena.alloc("a1_zeros", [128, TCW], F32)
    s_zeros = S()
    state = sb("a1_state", [128, NCH], F32)
    s_state = [S() for _ in range(NCH)]
    bk.emit("vector", lambda e: e.memset(zeros[:], 0.0), writes=[s_zeros])
    st_init = bk.emit("vector", lambda e: e.memset(state[:], 0.0), writes=s_state)
    psq = cm.pp[0:3]
    s_psq = cm.s_pp[0:3]
    pso = cm.pp[4:6]
    s_pso = cm.s_pp[4:6]
    d_zt = bk.sem("a1_zt")
    d_w = [bk.sem(f"a1_dw{j}") for j in range(2)]
    d_wo = [bk.sem(f"a1_dwo{j}") for j in range(2)]

    wqbT = {"q": W["qb"], "k": W["kb"], "v": W["vb"]}

    for tc in range(NTCOL):
        cm.load_xacc(x_d, tc)
        for c in range(NCH):
            bk.emit("scalar", lambda e, c=c, tc=tc: e.dma_start_transpose(
                out=zT[c][:], in_=zscr_d[tc * TCW:(tc + 1) * TCW, c * 128:(c + 1) * 128]),
                reads=s_zscr, writes=[s_zT[c]], dma=d_zt)
        bk.seal(d_zt, s_zT)
        for ep in range(NEP):
            par = ep % 2
            semw = d_w[par]
            for c in range(NCH):
                for nm, buf, slot in (("q", wq, s_wq), ("k", wk, s_wk), ("v", wv, s_wv)):
                    bk.emit("scalar", lambda e, c=c, ep=ep, nm=nm, buf=buf: e.dma_start_transpose(
                        out=buf[c][:], in_=wqbT[nm][ep * EW:(ep + 1) * EW, c * 128:(c + 1) * 128]),
                        reads=[s_cast[nm]], writes=[slot[c]], dma=semw)
            bk.seal(semw, s_wq + s_wk + s_wv)
            semo = d_wo[par]
            for b in range(NEB):
                bk.emit("scalar", lambda e, b=b, ep=ep: e.dma_start_transpose(
                    out=wo[b][:], in_=W["ob"][:, (ep * NEB + b) * 128:(ep * NEB + b + 1) * 128]),
                    reads=[s_cast["o"]], writes=[s_wo[b]], dma=semo)
            bk.seal(semo, s_wo)
            for b in range(NEB):
                for proj, (buf, slot, fbuf, fslot) in enumerate(
                        ((wq, s_wq, qf, s_qf), (wk, s_wk, kf, s_kf), (wv, s_wv, vf, s_vf))):
                    pp = proj
                    for th in range(NTH):
                        for c in range(NCH):
                            bk.mm(lambda e, b=b, th=th, c=c, pp=pp, buf=buf: e.matmul(
                                psq[pp][:], buf[c][:, b * 128:(b + 1) * 128],
                                zT[c][:, th * 512:(th + 1) * 512],
                                start=(c == 0), stop=(c == NCH - 1)),
                                reads=[slot[c], s_zT[c]], psum=s_psq[pp],
                                first=(c == 0), last=(c == NCH - 1))
                        bk.emit("vector", lambda e, b=b, th=th, pp=pp, fbuf=fbuf: e.tensor_copy(
                            out=fbuf[b][:, th * 512:(th + 1) * 512], in_=psq[pp][:]),
                            reads=[s_psq[pp]], writes=[fslot[b]])
                gc = ep * NEB + b
                bk.emit("vector", lambda e, b=b: e.tensor_tensor(out=kf[b][:], in0=kf[b][:],
                        in1=vf[b][:], op=ALU.mult), reads=[s_kf[b], s_vf[b]], writes=[s_kf[b]])
                bk.emit("vector", lambda e, b=b, gc=gc: e.tensor_tensor_scan(
                        out=kvt[b][:], data0=kf[b][:], data1=zeros[:],
                        initial=state[:, gc:gc + 1], op0=ALU.add, op1=ALU.add),
                        reads=[s_kf[b], s_zeros, s_state[gc]], writes=[s_kvt[b]])
                bk.emit("vector", lambda e, b=b, gc=gc: e.tensor_copy(
                        out=state[:, gc:gc + 1], in_=kvt[b][:, TCW - 1:TCW]),
                        reads=[s_kvt[b]], writes=[s_state[gc]])
                bk.emit("vector", lambda e, b=b: e.scalar_tensor_tensor(
                        out=vhb[b][:], in0=kvt[b][:], scalar=denom, in1=qf[b][:],
                        op0=ALU.mult, op1=ALU.mult),
                        reads=[s_kvt[b], s_qf[b]], writes=[s_vhb[b]])
            for tt in range(NTT):
                for ccol in range(C // 512):
                    o = (tt * (C // 512) + ccol) % 2
                    for b in range(NEB):
                        bk.mm(lambda e, tt=tt, ccol=ccol, o=o, b=b: e.matmul(
                            pso[o][:], vhb[b][:, tt * 128:(tt + 1) * 128],
                            wo[b][:, ccol * 512:(ccol + 1) * 512],
                            start=(b == 0), stop=(b == NEB - 1)),
                            reads=[s_vhb[b], s_wo[b]], psum=s_pso[o],
                            first=(b == 0), last=(b == NEB - 1))
                    bk.emit("vector", lambda e, tt=tt, ccol=ccol, o=o: e.tensor_add(
                        out=cm.xa[tt][:, ccol * 512:(ccol + 1) * 512],
                        in0=cm.xa[tt][:, ccol * 512:(ccol + 1) * 512], in1=pso[o][:]),
                        reads=[s_pso[o], cm.s_xa[tt]], writes=[cm.s_xa[tt]])
        cm.store_xacc(xo_d, tc)
    cm.finish_x()





def S(n=""):
    return Slot(n)


GATE_OFFSET = 6.906768
NKV = 8
NQH = 16
LN_C = math.log(1.0 / math.sqrt(128.0))


def emit_l0attn_casts(bk, W):
    s_cast = {}
    for nm, src in (("qb", "Wq"), ("kb", "Wk"), ("vb", "Wv")):
        sem = bk.sem(f"a0_c{nm}")
        s_cast[nm] = S()
        bk.emit("gpsimd", lambda e, nm=nm, src=src: e.dma_start(
            out=W[nm], in_=W[src]), writes=[s_cast[nm]], dma=sem)
        bk.seal(sem, [s_cast[nm]])
    return s_cast


def emit_l0attn(nc, bk, cm, W, zscr_d, x_d, xo_d, arena, s_cast, bx=()):
    sb, ps = bk.sb, bk.ps
    s_zscr = [S(), S()]
    NTT = TCW // 128
    NTH = TCW // 512

    cm.ln_pass(x_d, zscr_d, s_zscr)

    pp = cm.pp[0:6]
    s_pp = cm.s_pp[0:6]
    ps_g = cm.pp[6][:, 0:8]
    s_psg = cm.s_pp[6]
    ps_n = cm.pp[7][0:1, :]
    s_psn = cm.s_pp[7]

    onecolf = arena.alloc("a0_one", [128, 1], F32)
    s_onecolf = S()
    bk.emit("vector", lambda e: e.memset(onecolf[:], 1.0), writes=[s_onecolf])
    lncol = arena.alloc("a0_lnc", [128, 1], F32)
    s_lncol = S()
    bk.emit("vector", lambda e: e.memset(lncol[:], LN_C), writes=[s_lncol])
    row1 = arena.alloc("a0_row1", [1, T], F32)
    s_row1 = S()
    logg_tm = [arena.alloc(f"a0_lg{t}", [128, 8], F32) for t in range(NT)]
    s_logg_tm = [S() for _ in range(NT)]
    sqes_tm = [arena.alloc(f"a0_se{t}", [128, 8], F32) for t in range(NT)]
    s_sqes_tm = [S() for _ in range(NT)]
    gu = arena.alloc("a0_gu", [128, 8], F32)
    ga = arena.alloc("a0_ga", [128, 8], F32)
    gl = arena.alloc("a0_gl", [128, 8], F32)
    s_gu, s_ga, s_gl = S(), S(), S()
    lgpre = arena.alloc("a0_lgpre", [8, T], F32)
    lgcum = arena.alloc("a0_lgcum", [8, T], F32)
    s_lgpre, s_lgcum = S(), S()
    ebuf = [arena.alloc(f"a0_ebuf{j}", [128, 512], BF16) for j in range(4)]
    s_ebuf = [S() for _ in range(4)]
    d_st = [bk.sem(f"a0_dst{j}") for j in range(4)]
    s_qscr, s_kscr, s_vscr = S(), S(), S()
    ecnt = [0]

    wgf = [arena.alloc(f"a0_wgf{c}", [128, 8], F32) for c in range(NCH)]
    wgb = [arena.alloc(f"a0_wgb{c}", [128, 8], BF16) for c in range(NCH)]
    s_wgf = [S() for _ in range(NCH)]
    s_wgb = [S() for _ in range(NCH)]
    d_wg = bk.sem("a0_wg")
    wg3 = W["Wg"].rearrange("(c p) g -> c p g", p=128)
    for c in range(NCH):
        bk.emit("sync", lambda e, c=c: e.dma_start(out=wgf[c][:, 0:8], in_=wg3[c]),
                writes=[s_wgf[c]], dma=d_wg)
    bk.seal(d_wg, s_wgf)
    for c in range(NCH):
        bk.emit("vector", lambda e, c=c: e.tensor_copy(out=wgb[c][:], in_=wgf[c][:]),
                reads=[s_wgf[c]], writes=[s_wgb[c]])

    arena = arena.child("a0_sub")

    def evac_store(u, dst_ap, wslot):
        j = ecnt[0] % 4
        ecnt[0] += 1
        bk.emit("vector", lambda e, u=u, j=j: e.tensor_copy(out=ebuf[j][:], in_=pp[u][:]),
                reads=[s_pp[u]], writes=[s_ebuf[j]])
        bk.emit("sync", lambda e, j=j, dst_ap=dst_ap: e.dma_start(out=dst_ap, in_=ebuf[j][:]),
                reads=[s_ebuf[j]], writes=[wslot], dma=d_st[j])

    zT = [arena.alloc(f"zT{c}", [128, TCW], BF16) for c in range(NCH)]
    s_zT = [S() for _ in range(NCH)]
    wqsl = [arena.alloc(f"wqsl{j}", [128, 2048], BF16) for j in range(2)]
    s_wqsl = [S() for _ in range(2)]
    wksl = [arena.alloc(f"wksl{j}", [128, 2048], BF16) for j in range(2)]
    s_wksl = [S() for _ in range(2)]
    wvsla = arena.alloc("wvsl", [128, NCH * 512], BF16)
    wvsl = [wvsla[:, c * 512:(c + 1) * 512] for c in range(NCH)]
    s_wvsla = S()
    s_wvsl = [s_wvsla] * NCH
    d_zt = bk.sem("a0_zt")
    d_wq = [bk.sem(f"a0_dwq{j}") for j in range(2)]
    d_wv = bk.sem("a0_dwv")
    wqb3 = W["Wq"].rearrange("(c p) e -> p c e", p=128)
    wkb3 = W["Wk"].rearrange("(c p) e -> p c e", p=128)
    wvb3 = W["Wv"].rearrange("(c p) e -> p c e", p=128)

    for tc in range(NTCOL):
        bk.scope_mark(f"a0_proj{tc}")
        for c in range(NCH):
            bk.emit("scalar", lambda e, c=c, tc=tc: e.dma_start_transpose(
                out=zT[c][:], in_=zscr_d[tc * TCW:(tc + 1) * TCW, c * 128:(c + 1) * 128]),
                reads=s_zscr, writes=[s_zT[c]], dma=d_zt)
        bk.seal(d_zt, s_zT)
        for qb in range(16):
            j = qb % 2
            bk.emit("gpsimd", lambda e, qb=qb, j=j: e.dma_start(
                out=wqsl[j].rearrange("p (c e) -> p c e", e=128),
                in_=wqb3[:, :, qb * 128:(qb + 1) * 128]),
                writes=[s_wqsl[j]], dma=d_wq[j])
            bk.seal(d_wq[j], [s_wqsl[j]])
            for th in range(NTH):
                u = (qb * NTH + th) % 2
                for c in range(NCH):
                    bk.mm(lambda e, th=th, c=c, u=u, j=j: e.matmul(
                        pp[u][:], wqsl[j][:, c * 128:(c + 1) * 128],
                        zT[c][:, th * 512:(th + 1) * 512],
                        start=(c == 0), stop=(c == NCH - 1)),
                        reads=[s_wqsl[j], s_zT[c]], psum=s_pp[u],
                        first=(c == 0), last=(c == NCH - 1))
                evac_store(u, W["qscr"][qb * 128:(qb + 1) * 128,
                           tc * TCW + th * 512:tc * TCW + (th + 1) * 512], s_qscr)
        for kb in range(8):
            j = kb % 2
            bk.emit("gpsimd", lambda e, kb=kb, j=j: e.dma_start(
                out=wksl[j].rearrange("p (c e) -> p c e", e=128),
                in_=wkb3[:, :, kb * 128:(kb + 1) * 128]),
                writes=[s_wksl[j]], dma=d_wq[j])
            bk.seal(d_wq[j], [s_wksl[j]])
            for th in range(NTH):
                u = (kb * NTH + th) % 2
                for c in range(NCH):
                    bk.mm(lambda e, th=th, c=c, u=u, j=j: e.matmul(
                        pp[u][:], wksl[j][:, c * 128:(c + 1) * 128],
                        zT[c][:, th * 512:(th + 1) * 512],
                        start=(c == 0), stop=(c == NCH - 1)),
                        reads=[s_wksl[j], s_zT[c]], psum=s_pp[u],
                        first=(c == 0), last=(c == NCH - 1))
                evac_store(u, W["kscr"][kb * 128:(kb + 1) * 128,
                           tc * TCW + th * 512:tc * TCW + (th + 1) * 512], s_kscr)
        for vh in range(2):
            bk.emit("gpsimd", lambda e, vh=vh: e.dma_start(
                out=wvsla.rearrange("p (c e) -> p c e", e=512),
                in_=wvb3[:, :, vh * 512:(vh + 1) * 512]),
                writes=[s_wvsla], dma=d_wv)
            bk.seal(d_wv, [s_wvsla])
            for tt in range(NTT):
                gt = tc * NTT + tt
                u = (vh * NTT + tt) % 2 + 2
                for c in range(NCH):
                    bk.mm(lambda e, tt=tt, c=c, u=u: e.matmul(
                        pp[u][:], zT[c][:, tt * 128:(tt + 1) * 128], wvsl[c][:],
                        start=(c == 0), stop=(c == NCH - 1)),
                        reads=[s_wvsl[c], s_zT[c]], psum=s_pp[u],
                        first=(c == 0), last=(c == NCH - 1))
                evac_store(u, W["vscr"][gt * 128:(gt + 1) * 128,
                           vh * 512:(vh + 1) * 512], s_vscr)
        for tt in range(NTT):
            gt = tc * NTT + tt
            for c in range(NCH):
                bk.mm(lambda e, tt=tt, c=c: e.matmul(
                    ps_g, zT[c][:, tt * 128:(tt + 1) * 128], wgb[c][:],
                    start=(c == 0), stop=(c == NCH - 1)),
                    reads=[s_wgb[c], s_zT[c]], psum=s_psg,
                    first=(c == 0), last=(c == NCH - 1))
            bk.emit("vector", lambda e: e.tensor_copy(out=gu[:], in_=ps_g),
                    reads=[s_psg], writes=[s_gu])
            bk.emit("vector", lambda e: e.tensor_scalar(out=gu[:], in0=gu[:],
                    scalar1=GATE_OFFSET, scalar2=None, op0=ALU.add),
                    reads=[s_gu], writes=[s_gu])
            bk.emit("scalar", lambda e: e.activation(out=ga[:], in_=gu[:],
                    func=AF.Abs), reads=[s_gu], writes=[s_ga])
            bk.emit("scalar", lambda e: e.activation(out=ga[:], in_=ga[:],
                    func=AF.Exp, scale=-1.0), reads=[s_ga], writes=[s_ga])
            bk.emit("scalar", lambda e: e.activation(out=ga[:], in_=ga[:],
                    func=AF.Ln, bias=onecolf[:, 0:1]),
                    reads=[s_ga, s_onecolf], writes=[s_ga])
            bk.emit("vector", lambda e: e.tensor_scalar(out=gl[:], in0=gu[:],
                    scalar1=-1.0, scalar2=0.0, op0=ALU.mult, op1=ALU.max),
                    reads=[s_gu], writes=[s_gl])
            bk.emit("vector", lambda e, gt=gt: e.scalar_tensor_tensor(
                    out=logg_tm[gt][:], in0=gl[:], scalar=-1.0, in1=ga[:],
                    op0=ALU.mult, op1=ALU.subtract),
                    reads=[s_gl, s_ga], writes=[s_logg_tm[gt]])

    bk.scope_mark("a0_logg")
    for t in range(NT):
        u = t % 2
        bk.mm(lambda e, t=t, u=u: e.transpose(pp[u][0:8, 0:128], logg_tm[t][:],
              cm.identf[:]), reads=[s_logg_tm[t], cm.s_identf], psum=s_pp[u])
        bk.emit("vector", lambda e, t=t, u=u: e.tensor_copy(
            out=lgpre[:, t * 128:(t + 1) * 128], in_=pp[u][0:8, 0:128]),
            reads=[s_pp[u]], writes=[s_lgpre])
    bk.emit("vector", lambda e: e.tensor_tensor_scan(
        out=lgcum[:], data0=lgpre[:], data1=lgpre[:], initial=0.0,
        op0=ALU.add, op1=ALU.bypass), reads=[s_lgpre], writes=[s_lgcum])
    d_lg = bk.sem("a0_dlg")
    s_loggscr = S()
    bk.emit("sync", lambda e: e.dma_start(out=W["logg"], in_=lgcum[:]),
            reads=[s_lgcum], writes=[s_loggscr], dma=d_lg)
    bk.seal(d_lg, [s_loggscr])
    for t in range(NT):
        u = t % 2
        bk.mm(lambda e, t=t, u=u: e.transpose(pp[u][0:128, 0:8],
              lgcum[:, t * 128:(t + 1) * 128], cm.identf[0:8, 0:8]),
              reads=[s_lgcum, cm.s_identf], psum=s_pp[u])
        bk.emit("scalar", lambda e, t=t, u=u: e.activation(out=sqes_tm[t][:],
                in_=pp[u][0:128, 0:8], func=AF.Exp, scale=-0.5, bias=lncol[:, 0:1]),
                reads=[s_pp[u], s_lncol], writes=[s_sqes_tm[t]])

    bk.barrier(exclude=bx)
    arena.reset()
    qh = arena.alloc("qh", [128, T], BF16)
    kh = arena.alloc("kh", [128, T], BF16)
    vh = [arena.alloc(f"vh{t}", [128, 128], BF16) for t in range(NT)]
    yh = arena.alloc("yh", [128, T], BF16)
    eb = arena.alloc("eb", [128, T], BF16)
    sq = [arena.alloc(f"sq{j}", [128, 512], BF16) for j in range(2)]
    at = [arena.alloc(f"at{j}", [128, 512], BF16) for j in range(2)]
    erowb = arena.alloc("erowb", [1, T], BF16)
    nrow = arena.alloc("nrow", [1, 512], F32)
    nrowb = arena.alloc("nrowb", [1, 512], BF16)
    rbb = arena.alloc("rbb", [128, 512], BF16)
    s_qh, s_kh, s_yh, s_eb = S(), S(), S(), S()
    s_vh = [S() for _ in range(NT)]
    s_sq = [S(), S()]
    s_at = [S(), S()]
    s_erowb, s_nrow, s_nrowb, s_rbb = S(), S(), S(), S()
    d_qh = [bk.sem(f"a0_dqh{j}") for j in range(2)]
    d_kh = bk.sem("a0_dkh")
    d_vh = bk.sem("a0_dvh")
    d_er = bk.sem("a0_der")
    d_ys = [bk.sem(f"a0_dys{j}") for j in range(2)]
    s_yscr = S()

    for h in range(NQH):
        bk.scope_mark(f"a0_head{h}")
        hk = h // 2
        if h % 2 == 0:
            bk.emit("sync", lambda e, hk=hk: e.dma_start(
                out=kh[:], in_=W["kscr"][hk * 128:(hk + 1) * 128, :]),
                reads=[s_kscr], writes=[s_kh], dma=d_kh)
            bk.seal(d_kh, [s_kh])
            for t in range(NT):
                bk.emit("sync", lambda e, t=t, hk=hk: e.dma_start(
                    out=vh[t][:], in_=W["vscr"][t * 128:(t + 1) * 128,
                                                hk * 128:(hk + 1) * 128]),
                    reads=[s_vscr], writes=[s_vh[t]], dma=d_vh)
            bk.seal(d_vh, s_vh)
            bk.emit("sync", lambda e, hk=hk: e.dma_start(
                out=row1[:], in_=W["logg"][hk:hk + 1, :]),
                reads=[s_loggscr], writes=[s_row1], dma=d_er)
            bk.seal(d_er, [s_row1])
            bk.emit("scalar", lambda e: e.activation(out=row1[:], in_=row1[:],
                    func=AF.Exp), reads=[s_row1], writes=[s_row1])
            bk.emit("vector", lambda e: e.tensor_copy(out=erowb[:], in_=row1[:]),
                    reads=[s_row1], writes=[s_erowb])
            for q4 in range(4):
                bk.mm(lambda e, q4=q4: e.matmul(pp[4][:], cm.onesr[:],
                      erowb[:, q4 * 512:(q4 + 1) * 512], start=True, stop=True),
                      reads=[cm.s_onesr, s_erowb], psum=s_pp[4])
                bk.emit("vector", lambda e, q4=q4: e.tensor_copy(
                        out=eb[:, q4 * 512:(q4 + 1) * 512], in_=pp[4][:]),
                        reads=[s_pp[4]], writes=[s_eb])
        bk.emit("sync", lambda e, h=h: e.dma_start(
            out=qh[:], in_=W["qscr"][h * 128:(h + 1) * 128, :]),
            reads=[s_qscr], writes=[s_qh], dma=d_qh[h % 2])
        bk.seal(d_qh[h % 2], [s_qh])
        for tq in range(4):
            nsb = 4 * tq + 4
            for sbk in range(nsb):
                u = sbk % 2
                bk.mm(lambda e, sbk=sbk, tq=tq, u=u: e.matmul(
                    pp[u][:], kh[:, sbk * 128:(sbk + 1) * 128],
                    qh[:, tq * 512:(tq + 1) * 512], start=True, stop=True),
                    reads=[s_kh, s_qh], psum=s_pp[u])
                bk.emit("scalar", lambda e, sbk=sbk, u=u, hk=hk: e.activation(
                    out=sq[u][:], in_=pp[u][:], func=AF.Square,
                    scale=sqes_tm[sbk][:, hk:hk + 1]),
                    reads=[s_pp[u], s_sqes_tm[sbk]], writes=[s_sq[u]])
                bk.emit("vector", lambda e, tq=tq, u=u: e.tensor_tensor(
                    out=at[u][:], in0=sq[u][:], in1=eb[:, tq * 512:(tq + 1) * 512],
                    op=ALU.mult), reads=[s_sq[u], s_eb], writes=[s_at[u]])
                if sbk >= 4 * tq:
                    bk.emit("gpsimd", lambda e, sbk=sbk, tq=tq, u=u: e.affine_select(
                        out=at[u][:], in_=at[u][:], pattern=[[1, 512]],
                        compare_op=ALU.is_ge, fill=0.0,
                        base=tq * 512 - sbk * 128, channel_multiplier=-1),
                        reads=[s_at[u]], writes=[s_at[u]])
                bk.mm(lambda e, sbk=sbk, u=u, nsb=nsb: e.matmul(
                    ps_n, cm.onecol[:], at[u][:],
                    start=(sbk == 0), stop=(sbk == nsb - 1)),
                    reads=[cm.s_onecol, s_at[u]], psum=s_psn,
                    first=(sbk == 0), last=(sbk == nsb - 1))
                bk.mm(lambda e, sbk=sbk, u=u, nsb=nsb: e.matmul(
                    pp[2][:], vh[sbk][:], at[u][:],
                    start=(sbk == 0), stop=(sbk == nsb - 1)),
                    reads=[s_vh[sbk], s_at[u]], psum=s_pp[2],
                    first=(sbk == 0), last=(sbk == nsb - 1))
            bk.emit("vector", lambda e: e.tensor_scalar(out=nrow[:], in0=ps_n,
                    scalar1=1e-6, scalar2=None, op0=ALU.add),
                    reads=[s_psn], writes=[s_nrow])
            bk.emit("scalar", lambda e: e.activation(out=nrow[:], in_=nrow[:],
                    func=AF.Ln), reads=[s_nrow], writes=[s_nrow])
            bk.emit("scalar", lambda e: e.activation(out=nrow[:], in_=nrow[:],
                    func=AF.Exp, scale=-1.0), reads=[s_nrow], writes=[s_nrow])
            bk.emit("vector", lambda e: e.tensor_copy(out=nrowb[:], in_=nrow[:]),
                    reads=[s_nrow], writes=[s_nrowb])
            bk.mm(lambda e: e.matmul(pp[3][:], cm.onesr[:], nrowb[:],
                  start=True, stop=True), reads=[cm.s_onesr, s_nrowb], psum=s_pp[3])
            bk.emit("vector", lambda e: e.tensor_copy(out=rbb[:], in_=pp[3][:]),
                    reads=[s_pp[3]], writes=[s_rbb])
            bk.emit("vector", lambda e, tq=tq: e.tensor_tensor(
                    out=yh[:, tq * 512:(tq + 1) * 512], in0=pp[2][:], in1=rbb[:],
                    op=ALU.mult), reads=[s_pp[2], s_rbb], writes=[s_yh])
        bk.emit("sync", lambda e, h=h: e.dma_start(
            out=W["yscr"][h * 128:(h + 1) * 128, :], in_=yh[:]),
            reads=[s_yh], writes=[s_yscr], dma=d_ys[h % 2])
        bk.seal(d_ys[h % 2], [s_yscr])

    bk.barrier(exclude=bx)
    arena.reset()
    ysl = [arena.alloc(f"ysl{q}", [128, TCW], BF16) for q in range(16)]
    s_ysl = [S() for _ in range(16)]
    wcs = [arena.alloc(f"wcs{q}", [128, 512], BF16) for q in range(16)]
    s_wcs = [S() for _ in range(16)]
    bcb = arena.alloc("bcb", [128, C], F32)
    s_bcb = S()
    d_bc = bk.sem("a0_dbc")
    d_ysl = [bk.sem(f"a0_dysl{j}") for j in range(2)]
    d_wcs = [bk.sem(f"a0_dwcs{j}") for j in range(2)]
    bk.emit("sync", lambda e: e.dma_start(out=row1[:],
            in_=W["bc"].rearrange("(b a) -> b a", b=1)), writes=[s_row1], dma=d_bc)
    bk.seal(d_bc, [s_row1])
    for q4 in range(4):
        bk.mm(lambda e, q4=q4: e.matmul(pp[5][:], cm.onesrf[:],
              row1[:, q4 * 512:(q4 + 1) * 512], start=True, stop=True),
              reads=[cm.s_onesrf, s_row1], psum=s_pp[5])
        bk.emit("vector", lambda e, q4=q4: e.tensor_copy(
                out=bcb[:, q4 * 512:(q4 + 1) * 512], in_=pp[5][:]),
                reads=[s_pp[5]], writes=[s_bcb])
    wc3 = W["Wc"].rearrange("(q p) c -> q p c", p=128)
    for tc in range(NTCOL):
        bk.scope_mark(f"a0_wc{tc}")
        cm.load_xacc(x_d, tc)
        for tt in range(NTT):
            bk.emit("vector", lambda e, tt=tt: e.tensor_add(out=cm.xa[tt][:],
                    in0=cm.xa[tt][:], in1=bcb[:]),
                    reads=[s_bcb, cm.s_xa[tt]], writes=[cm.s_xa[tt]])
        sem = d_ysl[tc % 2]
        for q in range(16):
            bk.emit("sync", lambda e, q=q, tc=tc: e.dma_start(
                out=ysl[q][:], in_=W["yscr"][q * 128:(q + 1) * 128,
                                             tc * TCW:(tc + 1) * TCW]),
                reads=[s_yscr], writes=[s_ysl[q]], dma=sem)
        bk.seal(sem, s_ysl)
        for ccol in range(C // 512):
            semw = d_wcs[ccol % 2]
            for q in range(16):
                bk.emit("gpsimd", lambda e, q=q, ccol=ccol: e.dma_start(
                    out=wcs[q][:], in_=wc3[q][:, ccol * 512:(ccol + 1) * 512]),
                    writes=[s_wcs[q]], dma=semw)
            bk.seal(semw, s_wcs)
            for tt in range(NTT):
                o = tt % 2
                for q in range(16):
                    bk.mm(lambda e, q=q, tt=tt, o=o: e.matmul(
                        pp[o][:], ysl[q][:, tt * 128:(tt + 1) * 128], wcs[q][:],
                        start=(q == 0), stop=(q == 15)),
                        reads=[s_ysl[q], s_wcs[q]], psum=s_pp[o],
                        first=(q == 0), last=(q == 15))
                bk.emit("vector", lambda e, tt=tt, ccol=ccol, o=o: e.tensor_add(
                    out=cm.xa[tt][:, ccol * 512:(ccol + 1) * 512],
                    in0=cm.xa[tt][:, ccol * 512:(ccol + 1) * 512], in1=pp[o][:]),
                    reads=[s_pp[o], cm.s_xa[tt]], writes=[cm.s_xa[tt]])
        cm.store_xacc(xo_d, tc)
    cm.finish_x()





def build_full_program():
    nc = bass.Bass("TRN2", target_bir_lowering=False)
    d = lambda name, shape, dt, kind: nc.dram_tensor(name, shape, dt, kind=kind).ap()
    x_in = d("x", [T, C], F32, "ExternalInput")
    xo_d = d("xo", [T, C], F32, "ExternalOutput")
    xA = d("xA", [T, C], F32, "Internal")
    xB = d("xB", [T, C], F32, "Internal")
    zscr_d = d("zscr", [T, C], BF16, "Internal")
    Wf = {"up": d("f_up", [HF, C], F32, "ExternalInput"),
          "dn": d("f_dn", [C, HF], F32, "ExternalInput"),
          "us": d("f_us", [HF], F32, "ExternalInput"),
          "ds": d("f_ds", [C], F32, "ExternalInput"),
          "os": d("f_os", [1], F32, "ExternalInput"),
          "upb": d("f_upb", [HF, C], BF16, "Internal"),
          "dnb": d("f_dnb", [C, HF], BF16, "Internal")}
    Wa0 = {"Wq": d("a0Wq", [C, 2048], F32, "ExternalInput"),
           "Wk": d("a0Wk", [C, 1024], F32, "ExternalInput"),
           "Wv": d("a0Wv", [C, 1024], F32, "ExternalInput"),
           "Wg": d("a0Wg", [C, 8], F32, "ExternalInput"),
           "Wc": d("a0Wc", [2048, C], F32, "ExternalInput"),
           "bc": d("a0bc", [C], F32, "ExternalInput"),
           "qb": d("a0qb", [C, 2048], BF16, "Internal"),
           "kb": d("a0kb", [C, 1024], BF16, "Internal"),
           "vb": d("a0vb", [C, 1024], BF16, "Internal"),
           "qscr": d("a0qscr", [2048, T], BF16, "Internal"),
           "kscr": d("a0kscr", [1024, T], BF16, "Internal"),
           "vscr": d("a0vscr", [T, 1024], BF16, "Internal"),
           "yscr": d("a0yscr", [2048, T], BF16, "Internal"),
           "logg": d("a0logg", [8, T], F32, "Internal")}
    Wm = {"W1": d("mW1", [7, C, C], F32, "ExternalInput"),
          "Wg": d("mWg", [7, C, C], F32, "ExternalInput"),
          "W2": d("mW2", [7, C, C], F32, "ExternalInput"),
          "Ws1": d("mWs1", [C, C], F32, "ExternalInput"),
          "Wsg": d("mWsg", [C, C], F32, "ExternalInput"),
          "Ws2": d("mWs2", [C, C], F32, "ExternalInput"),
          "gate_w": d("mgw", [C, 7], F32, "ExternalInput"),
          "gate_b": d("mgb", [7], F32, "ExternalInput"),
          "biases": d("mbias", [7], F32, "ExternalInput"),
          "W1b": d("mW1b", [8, C, C], BF16, "Internal"),
          "Wgb": d("mWgb", [8, C, C], BF16, "Internal"),
          "W2b": d("mW2b", [8, C, C], BF16, "Internal")}
    Wa1 = {"Wq": d("a1Wq", [C, C], F32, "ExternalInput"),
           "Wk": d("a1Wk", [C, C], F32, "ExternalInput"),
           "Wv": d("a1Wv", [C, C], F32, "ExternalInput"),
           "Wo": d("a1Wo", [C, C], F32, "ExternalInput"),
           "qb": d("a1qb", [C, C], BF16, "Internal"),
           "kb": d("a1kb", [C, C], BF16, "Internal"),
           "vb": d("a1vb", [C, C], BF16, "Internal"),
           "ob": d("a1ob", [C, C], BF16, "Internal")}

    bk = BK(nc)
    cm = Common(nc, bk)
    arena = Arena(bk, "ar", 112 * 1024)

    s_upb, s_dnb = emit_l0ffn_casts(bk, Wf)
    s_ca1 = emit_l1attn_casts(bk, Wa1)
    s_ca0 = None
    s_cm = None
    bx = frozenset(bk.sems[n] for n in
                   ["f_cup0", "f_cup1", "f_cup2", "f_cup3",
                    "f_cdn0", "f_cdn1", "f_cdn2", "f_cdn3",
                    "a1_cq", "a1_ck", "a1_cv", "a1_co"])

    def cut():
        bk.barrier(exclude=bx)
        arena.reset()

    bk.scope_mark("s0_ffn")
    emit_l0ffn(nc, bk, cm, Wf, zscr_d, x_in, xA, arena, s_upb, s_dnb, app=0)
    cut()
    bk.scope_mark("s1_attn0")
    emit_l0attn(nc, bk, cm, Wa0, zscr_d, xA, xB, arena, s_ca0, bx=bx)
    cut()
    bk.scope_mark("s2_ffn")
    emit_l0ffn(nc, bk, cm, Wf, zscr_d, xB, xA, arena, s_upb, s_dnb, app=1)
    cut()
    bk.scope_mark("s3_moe")
    emit_l1moe(nc, bk, cm, Wm, zscr_d, xA, xB, arena, s_cm, app=0)
    cut()
    bk.scope_mark("s4_attn1")
    emit_l1attn(nc, bk, cm, Wa1, zscr_d, xB, xA, arena, s_ca1)
    cut()
    bk.scope_mark("s5_moe")
    emit_l1moe(nc, bk, cm, Wm, zscr_d, xA, xo_d, arena, s_cm, app=1)
    bk.finalize()
    return nc




_PROGS = {}


def _full_prog():
    if "full" not in _PROGS:
        _PROGS["full"] = build_full_program()
    return _PROGS["full"]


def kernel(L0_attn_hidden_states, L0_attn_Wq, L0_attn_Wk, L0_attn_Wv, L0_attn_Wg,
           L0_attn_Wc, L0_attn_bc, L0_ffn_up_weight, L0_ffn_down_weight,
           L0_ffn_up_scale, L0_ffn_down_scale, L0_ffn_output_scale,
           L1_attn_Wq, L1_attn_Wk, L1_attn_Wv, L1_attn_Wo,
           L1_ffn_W1, L1_ffn_Wg, L1_ffn_W2, L1_ffn_Ws1, L1_ffn_Wsg, L1_ffn_Ws2,
           L1_ffn_gate_w, L1_ffn_gate_b, L1_ffn_biases):
    f32 = lambda a: np.ascontiguousarray(np.asarray(a), dtype=np.float32)
    x = f32(L0_attn_hidden_states)
    wmap = {"f_up": f32(L0_ffn_up_weight), "f_dn": f32(L0_ffn_down_weight),
            "f_us": f32(L0_ffn_up_scale), "f_ds": f32(L0_ffn_down_scale),
            "f_os": f32(L0_ffn_output_scale),
            "a0Wq": f32(L0_attn_Wq), "a0Wk": f32(L0_attn_Wk), "a0Wv": f32(L0_attn_Wv),
            "a0Wg": f32(L0_attn_Wg), "a0Wc": f32(L0_attn_Wc), "a0bc": f32(L0_attn_bc),
            "mW1": f32(L1_ffn_W1), "mWg": f32(L1_ffn_Wg), "mW2": f32(L1_ffn_W2),
            "mWs1": f32(L1_ffn_Ws1), "mWsg": f32(L1_ffn_Wsg), "mWs2": f32(L1_ffn_Ws2),
            "mgw": f32(L1_ffn_gate_w), "mgb": f32(L1_ffn_gate_b),
            "mbias": f32(L1_ffn_biases),
            "a1Wq": f32(L1_attn_Wq), "a1Wk": f32(L1_attn_Wk), "a1Wv": f32(L1_attn_Wv),
            "a1Wo": f32(L1_attn_Wo)}
    in_maps = [{**wmap, "x": np.ascontiguousarray(x[b])} for b in range(8)]
    nc = _full_prog()
    res = run_bass_kernel_spmd(nc, in_maps, core_ids=list(range(8)))
    return np.stack([np.asarray(res.results[b]["xo"]) for b in range(8)], axis=0).astype(np.float32)


# version 05
# speedup vs baseline: 1.1265x; 1.0108x over previous
import math
from contextlib import ExitStack
import numpy as np
import concourse.bass as bass
import concourse.mybir as mybir
from concourse.bass_utils import run_bass_kernel_spmd




F32 = mybir.dt.float32
BF16 = mybir.dt.bfloat16
AF = mybir.ActivationFunctionType
ALU = mybir.AluOpType
AX = mybir.AxisListType

ENGINES = ("tensor", "vector", "scalar", "gpsimd", "sync")
U8 = mybir.dt.uint8


def _dtsize(dt):
    return mybir.dt.size(dt)


class Arena:

    def __init__(self, bk, name, nbytes, addr=None):
        self.bk = bk
        self.name = name
        if addr is None:
            self.base_t = bk.sb(name, [128, nbytes], U8)
            self.addr = bk.nc.lookup_mloc(self.base_t).addr
        else:
            self.addr = addr
        self.nbytes = nbytes
        self.cur = 0
        self.n = 0

    def child(self, name):
        off = (self.cur + 31) // 32 * 32
        return Arena(self.bk, name, self.nbytes - off, addr=self.addr + off)

    def reset(self):
        self.cur = 0

    def alloc(self, name, shape, dtype):
        per = 1
        for d in shape[1:]:
            per *= d
        per *= _dtsize(dtype)
        off = (self.cur + 31) // 32 * 32
        assert off + per <= self.nbytes, (
            f"arena {self.name} overflow at {name}: {off}+{per} > {self.nbytes}")
        self.cur = off + per
        self.n += 1
        t = self.bk.nc.alloc_sbuf_tensor_at(
            f"{self.name}_{self.n}_{name}", shape, dtype, offset=self.addr + off)
        return t


class Slot:
    __slots__ = ("name", "writer", "readers")

    def __init__(self, name=""):
        self.name = name
        self.writer = None
        self.readers = []


class Sem:
    __slots__ = ("name", "handle", "count")

    def __init__(self, name, handle):
        self.name = name
        self.handle = handle
        self.count = 0


class BK:
    def __init__(self, nc):
        self.nc = nc
        self.stack = ExitStack()
        self.streams = {e: [] for e in ENGINES}
        self.watermark = {e: {} for e in ENGINES}
        self.sems = {}
        self.eng_sem = {}
        self.phase = 0
        self._pe_pending_reads = []
        self._pe_pending_writes = []
        self.new_phase()

    def sem(self, name):
        s = self.sems.get(name)
        if s is None:
            h = self.stack.enter_context(self.nc.semaphore(name))
            s = Sem(name, h)
            self.sems[name] = s
        return s

    def new_phase(self):
        self.phase += 1
        for e in ENGINES:
            self.eng_sem[e] = self.sem(f"ph{self.phase}_{e}")

    def sb(self, name, shape, dtype):
        t = self.stack.enter_context(self.nc.sbuf_tensor(name, shape, dtype))
        return t

    def ps(self, name, shape, dtype=F32):
        t = self.stack.enter_context(self.nc.psum_tensor(name, shape, dtype))
        return t

    def _collect(self, engine, reads, writes):
        need = {}

        def add(dep):
            if dep is None:
                return
            s, c, peng = dep
            if engine == "tensor" and peng == "tensor":
                return
            if need.get(s, 0) < c:
                need[s] = c

        for sl in reads:
            add(sl.writer)
        for sl in writes:
            add(sl.writer)
            for r in sl.readers:
                add(r)
        wm = self.watermark[engine]
        waits = []
        for s, c in need.items():
            if wm.get(s, 0) < c:
                waits.append((s, c))
                wm[s] = c
        return waits

    def emit(self, engine, fn, reads=(), writes=(), dma=None, tag=""):
        waits = self._collect(engine, reads, writes)
        if dma is not None:
            msem, inc = dma, 16
        else:
            msem, inc = self.eng_sem[engine], 1
        msem.count += inc
        marker = (msem, msem.count, engine)

        def op(eng, fn=fn, waits=tuple(waits), msem=msem, inc=inc, tag=tag):
            for s, c in waits:
                eng.wait_ge(s.handle, c)
            inst = fn(eng)
            inst.then_inc(msem.handle, inc)

        self.streams[engine].append(op)
        for sl in writes:
            sl.writer = marker
            sl.readers = []
        for sl in reads:
            sl.readers.append(marker)
        return marker

    def mm(self, fn, reads=(), psum=None, first=True, last=True, tag=""):
        eff_writes = (psum,) if (first and psum is not None) else ()
        waits = self._collect("tensor", reads, eff_writes)
        msem = self.eng_sem["tensor"]
        msem.count += 1
        marker = (msem, msem.count, "tensor")

        def op(eng, fn=fn, waits=tuple(waits), msem=msem, tag=tag):
            for s, c in waits:
                eng.wait_ge(s.handle, c)
            inst = fn(eng)
            inst.then_inc(msem.handle, 1)

        self.streams["tensor"].append(op)
        for sl in reads:
            sl.readers.append(marker)
        if psum is not None:
            if first:
                psum.readers = []
            if last:
                psum.writer = marker
        return marker

    def barrier(self, exclude=()):
        pairs = [(s, s.count) for s in self.sems.values()
                 if s.count > 0 and s not in exclude]
        bsem = self.eng_sem["sync"]
        bsem.count += 1
        bcount = bsem.count

        def op_sync(eng, pairs=tuple(pairs), bsem=bsem):
            for s, c in pairs:
                eng.wait_ge(s.handle, c)
            eng.nop().then_inc(bsem.handle, 1)

        self.streams["sync"].append(op_sync)
        wm = self.watermark["sync"]
        for s, c in pairs:
            wm[s] = max(wm.get(s, 0), c)
        for e in ENGINES:
            if e == "sync":
                continue

            def op(eng, bsem=bsem, bcount=bcount):
                eng.wait_ge(bsem.handle, bcount)

            self.streams[e].append(op)
            wme = self.watermark[e]
            wme[bsem] = max(wme.get(bsem, 0), bcount)
            for s, c in pairs:
                wme[s] = max(wme.get(s, 0), c)

    def seal(self, sem, slots):
        for sl in slots:
            if sl.writer is not None and sl.writer[0] is sem:
                sl.writer = (sem, sem.count, sl.writer[2])
            sl.readers = [(s, sem.count if s is sem else c, e2)
                          for (s, c, e2) in sl.readers]

    def wait_all(self, engine, sems):
        pairs = [(s, s.count) for s in sems]

        def op(eng, pairs=tuple(pairs)):
            for s, c in pairs:
                eng.wait_ge(s.handle, c)

        self.streams[engine].append(op)

    def barrier_wait(self, engine, slots):
        waits = self._collect(engine, slots, ())

        def op(eng, waits=tuple(waits)):
            for s, c in waits:
                eng.wait_ge(s.handle, c)

        self.streams[engine].append(op)

    def scope_mark(self, name):
        for e in ENGINES:
            self.streams[e].append(name)

    def finalize(self):
        nc = self.nc

        def replay(eng, ops):
            cur = None
            for op in ops:
                if isinstance(op, str):
                    if cur is not None:
                        nc.leave_named_scope(cur[0], cur[1], False)
                    cur = (op, nc.enter_named_scope(op, False)[0])
                else:
                    op(eng)
            if cur is not None:
                nc.leave_named_scope(cur[0], cur[1], False)

        with nc.Block() as blk:
            for ename in ENGINES:
                ops = self.streams[ename]
                getattr(blk, ename)(lambda eng, ops=ops: replay(eng, ops))
        self.stack.close()
        return nc



T = 2048
C = 2048
NT = 16
NCH = 16
HF = 8192
LNEPS = 1e-5
TCW = 1024
NTCOL = T // TCW


def configure(t=2048, c=2048, hf=8192, tcw=1024):
    global T, C, NT, NCH, HF, TCW, NTCOL
    T, C, HF, TCW = t, c, hf, tcw
    NT, NCH, NTCOL = T // 128, C // 128, T // TCW


def S(n=""):
    return Slot(n)


class Common:

    def __init__(self, nc, bk):
        self.nc, self.bk = nc, bk
        sb = bk.sb
        self.eps = sb("eps", [128, 1], F32)
        self.s_eps = S()
        bk.emit("vector", lambda e: e.memset(self.eps[:], LNEPS), writes=[self.s_eps])
        self.onesr = sb("onesr", [1, 128], BF16)
        self.s_onesr = S()
        bk.emit("vector", lambda e: e.memset(self.onesr[:], 1.0), writes=[self.s_onesr])
        self.onesrf = sb("onesrf", [1, 128], F32)
        self.s_onesrf = S()
        bk.emit("vector", lambda e: e.memset(self.onesrf[:], 1.0), writes=[self.s_onesrf])
        self.onecol = sb("onecol", [128, 1], BF16)
        self.s_onecol = S()
        bk.emit("vector", lambda e: e.memset(self.onecol[:], 1.0), writes=[self.s_onecol])
        self.identf = sb("identf", [128, 128], F32)
        self.s_identf = S()
        bk.emit("vector", lambda e: e.memset(self.identf[:], 1.0), writes=[self.s_identf])
        bk.emit("gpsimd", lambda e: e.affine_select(
            out=self.identf[:], in_=self.identf[:], pattern=[[1, 128]],
            compare_op=ALU.is_equal, fill=0.0, base=0, channel_multiplier=-1),
            reads=[self.s_identf], writes=[self.s_identf])
        self.lx = [sb(f"lnx{j}", [128, C], F32) for j in range(2)]
        self.s_lx = [S() for _ in range(2)]
        self.zt = [sb(f"lnz{j}", [128, C], BF16) for j in range(2)]
        self.s_zt = [S() for _ in range(2)]
        self.stat = [[sb(f"lnst{j}_{k}", [128, 1], F32) for k in range(5)] for j in range(2)]
        self.s_stat = [[S() for _ in range(5)] for _ in range(2)]
        self.d_lx = [bk.sem(f"c_lx{j}") for j in range(2)]
        self.d_zs = [bk.sem(f"c_zs{j}") for j in range(2)]
        self.xa = [sb(f"xa{i}", [128, C], F32) for i in range(TCW // 128)]
        self.s_xa = [S() for _ in range(TCW // 128)]
        self.d_xa = [bk.sem(f"c_xa{j}") for j in range(2)]
        self.d_xo = [bk.sem(f"c_xo{j}") for j in range(2)]
        self.pp = [bk.ps(f"cpp{i}", [128, 512]) for i in range(8)]
        self.s_pp = [S() for _ in range(8)]

    def col_table(self, name, src_1d, n, dsem, arena):
        bk = self.bk
        assert n <= 128
        row = arena.alloc(name + "_r", [n, 128], F32)
        out = arena.alloc(name, [128, n], F32)
        pst = self.pp[7][:, 0:n]
        s_row, s_out, s_ps = S(), S(), self.s_pp[7]
        bk.emit("sync", lambda e: e.dma_start(
            out=row[:], in_=src_1d.rearrange("(a b) -> a b", b=128)),
            writes=[s_row], dma=dsem)
        bk.seal(dsem, [s_row])
        bk.mm(lambda e: e.transpose(pst, row[:], self.identf[0:n, 0:n]),
              reads=[s_row, self.s_identf], psum=s_ps)
        bk.emit("vector", lambda e: e.tensor_copy(out=out[:], in_=pst),
                reads=[s_ps], writes=[s_out])
        return out, s_out

    def ln_pass(self, x_d, zscr_d, s_zscr):
        bk = self.bk
        for i in range(NT):
            j = i % 2
            st, s_st = self.stat[j], self.s_stat[j]
            x, s_x = self.lx[j], self.s_lx[j]
            bk.emit("sync", lambda e, i=i, x=x: e.dma_start(
                out=x[:], in_=x_d[i * 128:(i + 1) * 128, :]),
                writes=[s_x], dma=self.d_lx[j])
            bk.emit("vector", lambda e, x=x, st=st: e.reduce_sum(out=st[0][:], in_=x[:], axis=AX.X),
                    reads=[s_x], writes=[s_st[0]])
            bk.emit("scalar", lambda e, x=x, st=st, j=j: e.activation(
                out=self.zt[j][:], in_=x[:], func=AF.Square, accum_out=st[1][:]),
                reads=[s_x], writes=[self.s_zt[j], s_st[1]])
            bk.emit("vector", lambda e, st=st: e.tensor_scalar(out=st[2][:], in0=st[0][:],
                    scalar1=-1.0 / C, scalar2=None, op0=ALU.mult),
                    reads=[s_st[0]], writes=[s_st[2]])
            bk.emit("vector", lambda e, st=st: e.tensor_tensor(out=st[3][:], in0=st[2][:],
                    in1=st[2][:], op=ALU.mult), reads=[s_st[2]], writes=[s_st[3]])
            bk.emit("vector", lambda e, st=st: e.scalar_tensor_tensor(out=st[3][:], in0=st[1][:],
                    scalar=1.0 / C, in1=st[3][:], op0=ALU.mult, op1=ALU.subtract),
                    reads=[s_st[1], s_st[3]], writes=[s_st[3]])
            bk.emit("scalar", lambda e, st=st: e.activation(out=st[3][:], in_=st[3][:],
                    func=AF.Sqrt, bias=self.eps[:, 0:1]),
                    reads=[s_st[3], self.s_eps], writes=[s_st[3]])
            bk.emit("vector", lambda e, st=st: e.reciprocal(out=st[4][:], in_=st[3][:]),
                    reads=[s_st[3]], writes=[s_st[4]])
            bk.emit("vector", lambda e, x=x, st=st, j=j: e.tensor_scalar(out=self.zt[j][:],
                    in0=x[:], scalar1=st[2][:, 0:1], scalar2=st[4][:, 0:1],
                    op0=ALU.add, op1=ALU.mult),
                    reads=[s_x, s_st[2], s_st[4]], writes=[self.s_zt[j]])
            bk.emit("scalar", lambda e, i=i, j=j: e.dma_start(
                out=zscr_d[i * 128:(i + 1) * 128, :], in_=self.zt[j][:]),
                reads=[self.s_zt[j]], writes=[s_zscr[j]], dma=self.d_zs[j])
        for j in range(2):
            bk.seal(self.d_zs[j], [s_zscr[j]])

    def load_xacc(self, x_d, tc):
        bk = self.bk
        sem = self.d_xa[tc % 2]
        for k in range(TCW // 128):
            i = tc * (TCW // 128) + k
            bk.emit("sync", lambda e, i=i, k=k: e.dma_start(
                out=self.xa[k][:], in_=x_d[i * 128:(i + 1) * 128, :]),
                writes=[self.s_xa[k]], dma=sem)
        bk.seal(sem, self.s_xa)

    def store_xacc(self, xo_d, tc):
        bk = self.bk
        for k in range(TCW // 128):
            i = tc * (TCW // 128) + k
            bk.emit("sync", lambda e, i=i, k=k: e.dma_start(
                out=xo_d[i * 128:(i + 1) * 128, :], in_=self.xa[k][:]),
                reads=[self.s_xa[k]], dma=self.d_xo[tc % 2])
        bk.seal(self.d_xo[tc % 2], self.s_xa)

    def finish_x(self):
        self.bk.wait_all("sync", self.d_xo)



def emit_l0ffn_casts(bk, W):
    s_upb = [S() for _ in range(4)]
    s_dnb = [S() for _ in range(4)]
    for p in range(4):
        w = C // 4
        bk.emit("gpsimd", lambda e, p=p, w=w: e.dma_start(
            out=W["upb"][:, p * w:(p + 1) * w], in_=W["up"][:, p * w:(p + 1) * w]),
            writes=[s_upb[p]], dma=bk.sem(f"f_cup{p}"))
    for p in range(4):
        w = HF // 4
        bk.emit("gpsimd", lambda e, p=p, w=w: e.dma_start(
            out=W["dnb"][:, p * w:(p + 1) * w], in_=W["dn"][:, p * w:(p + 1) * w]),
            writes=[s_dnb[p]], dma=bk.sem(f"f_cdn{p}"))
    return s_upb, s_dnb


def emit_l0ffn(nc, bk, cm, W, zscr_d, x_d, xo_d, arena, s_upb, s_dnb, app=0):
    sb, ps = bk.sb, bk.ps
    s_zscr = [S("zscr0"), S("zscr1")]

    cm.ln_pass(x_d, zscr_d, s_zscr)

    usc, s_usc = cm.col_table("usc", W["us"], HF // 128, bk.sem("f_usc"), arena)

    dsrow = arena.alloc("dsrow", [1, C], F32)
    dsrow_b = arena.alloc("dsrow_b", [1, C], BF16)
    osct = arena.alloc("osct", [1, 1], F32)
    dsb = arena.alloc("dsb", [128, C], BF16)
    ps_b = cm.pp[6]
    s_dsrow, s_dsrow_b, s_osc, s_dsb, s_psb = S(), S(), S(), S(), cm.s_pp[6]
    d_ds = bk.sem("f_ds")
    bk.emit("sync", lambda e: e.dma_start(out=dsrow[:],
            in_=W["ds"].rearrange("(b a) -> b a", b=1)), writes=[s_dsrow], dma=d_ds)
    bk.emit("sync", lambda e: e.dma_start(out=osct[:],
            in_=W["os"].rearrange("(b a) -> b a", b=1)), writes=[s_osc], dma=d_ds)
    bk.seal(d_ds, [s_dsrow, s_osc])
    bk.emit("vector", lambda e: e.tensor_scalar(out=dsrow[:], in0=dsrow[:],
            scalar1=osct[0:1, 0:1], scalar2=0.5, op0=ALU.mult, op1=ALU.mult),
            reads=[s_dsrow, s_osc], writes=[s_dsrow])
    bk.emit("vector", lambda e: e.tensor_copy(out=dsrow_b[:], in_=dsrow[:]),
            reads=[s_dsrow], writes=[s_dsrow_b])
    for q in range(C // 512 if C >= 512 else 1):
        bk.mm(lambda e, q=q: e.matmul(ps_b[:], cm.onesr[:], dsrow_b[:, q * 512:(q + 1) * 512] if C >= 512 else dsrow_b[:],
              start=True, stop=True), reads=[cm.s_onesr, s_dsrow_b], psum=s_psb)
        bk.emit("vector", lambda e, q=q: e.tensor_copy(out=dsb[:, q * 512:(q + 1) * 512],
                in_=ps_b[:]), reads=[s_psb], writes=[s_dsb])

    upT = [arena.alloc(f"fupT{c}", [128, 512], BF16) for c in range(NCH)]
    s_upT = [S() for _ in range(NCH)]
    dnT = [arena.alloc(f"fdnT{h}", [128, C], BF16) for h in range(4)]
    s_dnT = [S() for _ in range(4)]
    zT = [arena.alloc(f"fzT{c}", [128, TCW], BF16) for c in range(NCH)]
    s_zT = [S() for _ in range(NCH)]
    hidT = [[arena.alloc(f"fhid{p}_{h}", [128, TCW], BF16) for h in range(4)] for p in range(2)]
    s_hid = [[S() for _ in range(4)] for _ in range(2)]
    psh = cm.pp[0:4]
    s_psh = cm.s_pp[0:4]
    pso = cm.pp[4:6]
    s_pso = cm.s_pp[4:6]
    d_up = [bk.sem(f"f_up{j}") for j in range(2)]
    d_dn = [bk.sem(f"f_dn{j}") for j in range(2)]
    d_zt = bk.sem("f_zt")

    NHP = HF // 512
    for tc in range(NTCOL):
        cm.load_xacc(x_d, tc)
        for c in range(NCH):
            bk.emit("scalar", lambda e, c=c, tc=tc: e.dma_start_transpose(
                out=zT[c][:], in_=zscr_d[tc * TCW:(tc + 1) * TCW, c * 128:(c + 1) * 128]),
                reads=s_zscr, writes=[s_zT[c]], dma=d_zt)
        bk.seal(d_zt, s_zT)
        for hp in range(NHP):
            par = hp % 2
            sem = d_up[par]
            for c in range(NCH):
                cp = s_upb[(c * 128) // (C // 4)]
                bk.emit("scalar", lambda e, c=c, hp=hp: e.dma_start_transpose(
                    out=upT[c][:], in_=W["upb"][hp * 512:(hp + 1) * 512, c * 128:(c + 1) * 128]),
                    reads=[cp], writes=[s_upT[c]], dma=sem)
            bk.seal(sem, s_upT)
            for c in range(NCH):
                bk.emit("scalar", lambda e, c=c: e.sign(out=upT[c][:], in_=upT[c][:]),
                        reads=[s_upT[c]], writes=[s_upT[c]])
            sem = d_dn[par]
            for hc in range(4):
                col = hp * 4 + hc
                cp = s_dnb[(col * 128) // (HF // 4)]
                bk.emit("scalar", lambda e, hc=hc, col=col: e.dma_start_transpose(
                    out=dnT[hc][:], in_=W["dnb"][:, col * 128:(col + 1) * 128]),
                    reads=[cp], writes=[s_dnT[hc]], dma=sem)
            bk.seal(sem, s_dnT)
            for hc in range(4):
                bk.emit("scalar", lambda e, hc=hc: e.sign(out=dnT[hc][:], in_=dnT[hc][:]),
                        reads=[s_dnT[hc]], writes=[s_dnT[hc]])
                bk.emit("vector", lambda e, hc=hc: e.tensor_tensor(out=dnT[hc][:],
                        in0=dnT[hc][:], in1=dsb[:], op=ALU.mult),
                        reads=[s_dnT[hc], s_dsb], writes=[s_dnT[hc]])
            for hb in range(4):
                for th in range(TCW // 512):
                    pp = (hb * (TCW // 512) + th) % 4
                    for c in range(NCH):
                        bk.mm(lambda e, hb=hb, th=th, c=c, pp=pp: e.matmul(
                            psh[pp][:], upT[c][:, hb * 128:(hb + 1) * 128],
                            zT[c][:, th * 512:(th + 1) * 512],
                            start=(c == 0), stop=(c == NCH - 1)),
                            reads=[s_upT[c], s_zT[c]], psum=s_psh[pp],
                            first=(c == 0), last=(c == NCH - 1))
                    hcol = hp * 4 + hb
                    bk.emit("scalar", lambda e, hb=hb, th=th, par=par, hcol=hcol, pp=pp:
                            e.activation(out=hidT[par][hb][:, th * 512:(th + 1) * 512],
                            in_=psh[pp][:], func=AF.Relu, scale=usc[:, hcol:hcol + 1]),
                            reads=[s_psh[pp], s_usc], writes=[s_hid[par][hb]])
            for tt in range(TCW // 128):
                for ccol in range(C // 512):
                    o = (tt * 4 + ccol) % 2
                    for hc in range(4):
                        bk.mm(lambda e, hc=hc, tt=tt, ccol=ccol, par=par, o=o: e.matmul(
                            pso[o][:], hidT[par][hc][:, tt * 128:(tt + 1) * 128],
                            dnT[hc][:, ccol * 512:(ccol + 1) * 512],
                            start=(hc == 0), stop=(hc == 3)),
                            reads=[s_hid[par][hc], s_dnT[hc]], psum=s_pso[o],
                            first=(hc == 0), last=(hc == 3))
                    bk.emit("vector", lambda e, tt=tt, ccol=ccol, o=o: e.tensor_add(
                        out=cm.xa[tt][:, ccol * 512:(ccol + 1) * 512],
                        in0=cm.xa[tt][:, ccol * 512:(ccol + 1) * 512], in1=pso[o][:]),
                        reads=[s_pso[o], cm.s_xa[tt]], writes=[cm.s_xa[tt]])
        cm.store_xacc(xo_d, tc)
    cm.finish_x()





def S(n=""):
    return Slot(n)


def emit_l1moe_casts(bk, W):
    NE = 8
    s_cast = [S(f"cast{e}") for e in range(NE)]
    for e in range(NE):
        sem = bk.sem(f"m_cast{e}")
        srcs = ([W["W1"][e], W["Wg"][e], W["W2"][e]] if e < 7
                else [W["Ws1"], W["Wsg"], W["Ws2"]])
        for k, nm in enumerate(("W1b", "Wgb", "W2b")):
            bk.emit("gpsimd", lambda e2, e=e, k=k, nm=nm, srcs=srcs: e2.dma_start(
                out=W[nm][e], in_=srcs[k]), writes=[s_cast[e]], dma=sem)
        bk.seal(sem, [s_cast[e]])
    return s_cast


def emit_l1moe(nc, bk, cm, W, zscr_d, x_d, xo_d, arena, s_cast, app=0):
    sb, ps = bk.sb, bk.ps
    s_zscr = [S("mz0"), S("mz1")]
    NE = 8
    NTT = TCW // 128

    cm.ln_pass(x_d, zscr_d, s_zscr)

    gw = [arena.alloc(f"m_gw{c}", [128, 7], F32) for c in range(NCH)]
    gwb = [arena.alloc(f"m_gwb{c}", [128, 7], BF16) for c in range(NCH)]
    s_gw = [S() for _ in range(NCH)]
    s_gwb = [S() for _ in range(NCH)]
    d_gw = bk.sem("m_gw")
    gw3 = W["gate_w"].rearrange("(c p) g -> c p g", p=128)
    for c in range(NCH):
        bk.emit("sync", lambda e, c=c: e.dma_start(out=gw[c][:], in_=gw3[c]),
                writes=[s_gw[c]], dma=d_gw)
    bk.seal(d_gw, s_gw)
    for c in range(NCH):
        bk.emit("vector", lambda e, c=c: e.tensor_copy(out=gwb[c][:], in_=gw[c][:]),
                reads=[s_gw[c]], writes=[s_gwb[c]])
    gbrow = arena.alloc("m_gbrow", [1, 7], F32)
    birow = arena.alloc("m_birow", [1, 7], F32)
    gbb = arena.alloc("m_gbb", [128, 7], F32)
    bib = arena.alloc("m_bib", [128, 7], F32)
    ps_g = cm.pp[6][:, 0:8]
    s_gbrow, s_birow, s_gbb, s_bib, s_psg = S(), S(), S(), S(), cm.s_pp[6]
    d_gb = bk.sem("m_gb")
    bk.emit("sync", lambda e: e.dma_start(out=gbrow[:],
            in_=W["gate_b"].rearrange("(b a) -> b a", b=1)), writes=[s_gbrow], dma=d_gb)
    bk.emit("sync", lambda e: e.dma_start(out=birow[:],
            in_=W["biases"].rearrange("(b a) -> b a", b=1)), writes=[s_birow], dma=d_gb)
    bk.seal(d_gb, [s_gbrow, s_birow])
    bk.mm(lambda e: e.matmul(ps_g[:, 0:7], cm.onesrf[:], gbrow[:], start=True, stop=True),
          reads=[cm.s_onesrf, s_gbrow], psum=s_psg)
    bk.emit("vector", lambda e: e.tensor_copy(out=gbb[:], in_=ps_g[:, 0:7]),
            reads=[s_psg], writes=[s_gbb])
    bk.mm(lambda e: e.matmul(ps_g[:, 0:7], cm.onesrf[:], birow[:], start=True, stop=True),
          reads=[cm.s_onesrf, s_birow], psum=s_psg)
    bk.emit("vector", lambda e: e.tensor_copy(out=bib[:], in_=ps_g[:, 0:7]),
            reads=[s_psg], writes=[s_bib])

    zT = [arena.alloc(f"m_zT{c}", [128, TCW], BF16) for c in range(NCH)]
    s_zT = [S() for _ in range(NCH)]
    slab1a = arena.alloc("m_sl1", [128, NCH * 512], BF16)
    slabga = arena.alloc("m_slg", [128, NCH * 512], BF16)
    slab2a = arena.alloc("m_sl2", [128, 4 * C], BF16)
    slab1 = [slab1a[:, c * 512:(c + 1) * 512] for c in range(NCH)]
    slabg = [slabga[:, c * 512:(c + 1) * 512] for c in range(NCH)]
    slab2 = [slab2a[:, h * C:(h + 1) * C] for h in range(4)]
    s_slab1a, s_slabga, s_slab2a = S(), S(), S()
    s_slab1 = [s_slab1a] * NCH
    s_slabg = [s_slabga] * NCH
    s_slab2 = [s_slab2a] * 4
    s1t = [arena.alloc(f"m_s1t{h}", [128, TCW], BF16) for h in range(4)]
    hh = [arena.alloc(f"m_hh{h}", [128, TCW], BF16) for h in range(4)]
    s_s1t = [S() for _ in range(4)]
    s_hh = [S() for _ in range(4)]
    wcol = [arena.alloc(f"m_w{t}", [128, 8], F32) for t in range(NTT)]
    s_wcol = [S() for _ in range(NTT)]
    glog = arena.alloc("m_glog", [128, 8], F32)
    gtmp = arena.alloc("m_gtmp", [128, 8], F32)
    gsel = arena.alloc("m_gsel", [128, 8], F32)
    gt2 = arena.alloc("m_gt2", [128, 8], F32)
    gcol = [arena.alloc(f"m_gc{k}", [128, 1], F32) for k in range(3)]
    s_glog, s_gtmp, s_gsel, s_gt2 = S(), S(), S(), S()
    s_gcol = [S() for _ in range(3)]
    psh = cm.pp[0:4]
    s_psh = cm.s_pp[0:4]
    pso = cm.pp[4:6]
    s_pso = cm.s_pp[4:6]
    d_zt = bk.sem("m_zt")
    d_s1 = [bk.sem(f"m_dsl1{j}") for j in range(2)]
    d_sg = [bk.sem(f"m_dslg{j}") for j in range(2)]
    d_s2 = [bk.sem(f"m_dsl2{j}") for j in range(2)]
    NTH = TCW // 512

    def src1(nm, e_i):
        w = W[nm][e_i] if e_i < 7 else W["Ws" + nm[1]]
        return w.rearrange("(c p) h -> p c h", p=128)

    def src2(e_i):
        w = W["W2"][e_i] if e_i < 7 else W["Ws2"]
        return w.rearrange("(h p) c -> p h c", p=128)

    for tc in range(NTCOL):
        cm.load_xacc(x_d, tc)
        for c in range(NCH):
            bk.emit("scalar", lambda e, c=c, tc=tc: e.dma_start_transpose(
                out=zT[c][:], in_=zscr_d[tc * TCW:(tc + 1) * TCW, c * 128:(c + 1) * 128]),
                reads=s_zscr, writes=[s_zT[c]], dma=d_zt)
        bk.seal(d_zt, s_zT)
        bk.scope_mark(f"m_gate_tc{tc}_{app}")
        for tt in range(NTT):
            for c in range(NCH):
                bk.mm(lambda e, tt=tt, c=c: e.matmul(
                    ps_g[:, 0:7], zT[c][:, tt * 128:(tt + 1) * 128], gwb[c][:],
                    start=(c == 0), stop=(c == NCH - 1)),
                    reads=[s_zT[c], s_gwb[c]], psum=s_psg,
                    first=(c == 0), last=(c == NCH - 1))
            bk.emit("vector", lambda e: e.tensor_tensor(out=glog[:, 0:7], in0=ps_g[:, 0:7],
                    in1=gbb[:, 0:7], op=ALU.add), reads=[s_psg, s_gbb], writes=[s_glog])
            bk.emit("vector", lambda e: e.reduce_max(out=gcol[0][:], in_=glog[:, 0:7],
                    axis=AX.X, negate=True), reads=[s_glog], writes=[s_gcol[0]])
            bk.emit("scalar", lambda e: e.activation(out=glog[:, 0:7], in_=glog[:, 0:7],
                    func=AF.Exp, bias=gcol[0][:, 0:1]),
                    reads=[s_glog, s_gcol[0]], writes=[s_glog])
            bk.emit("vector", lambda e: e.reduce_sum(out=gcol[1][:], in_=glog[:, 0:7],
                    axis=AX.X), reads=[s_glog], writes=[s_gcol[1]])
            bk.emit("vector", lambda e: e.reciprocal(out=gcol[2][:], in_=gcol[1][:]),
                    reads=[s_gcol[1]], writes=[s_gcol[2]])
            bk.emit("vector", lambda e: e.tensor_scalar(out=glog[:, 0:7], in0=glog[:, 0:7],
                    scalar1=gcol[2][:, 0:1], scalar2=None, op0=ALU.mult),
                    reads=[s_glog, s_gcol[2]], writes=[s_glog])
            bk.emit("vector", lambda e: e.tensor_tensor(out=gtmp[:, 0:7], in0=glog[:, 0:7],
                    in1=bib[:, 0:7], op=ALU.add), reads=[s_glog, s_bib], writes=[s_gtmp])
            bk.emit("vector", lambda e: e.reduce_max(out=gcol[0][:], in_=gtmp[:, 0:7],
                    axis=AX.X), reads=[s_gtmp], writes=[s_gcol[0]])
            bk.emit("vector", lambda e: e.tensor_scalar(out=gsel[:, 0:7], in0=gtmp[:, 0:7],
                    scalar1=gcol[0][:, 0:1], scalar2=None, op0=ALU.is_ge),
                    reads=[s_gtmp, s_gcol[0]], writes=[s_gsel])
            bk.emit("vector", lambda e: e.scalar_tensor_tensor(out=gt2[:, 0:7],
                    in0=gsel[:, 0:7], scalar=-1e9, in1=gtmp[:, 0:7],
                    op0=ALU.mult, op1=ALU.add), reads=[s_gsel, s_gtmp], writes=[s_gt2])
            bk.emit("vector", lambda e: e.reduce_max(out=gcol[1][:], in_=gt2[:, 0:7],
                    axis=AX.X), reads=[s_gt2], writes=[s_gcol[1]])
            bk.emit("vector", lambda e: e.tensor_scalar(out=gsel[:, 0:7], in0=gt2[:, 0:7],
                    scalar1=gcol[1][:, 0:1], scalar2=None, op0=ALU.is_ge),
                    reads=[s_gt2, s_gcol[1]], writes=[s_gsel])
            bk.emit("vector", lambda e: e.scalar_tensor_tensor(out=gt2[:, 0:7],
                    in0=gsel[:, 0:7], scalar=-1e9, in1=gt2[:, 0:7],
                    op0=ALU.mult, op1=ALU.add), reads=[s_gsel, s_gt2], writes=[s_gt2])
            bk.emit("vector", lambda e: e.reduce_max(out=gcol[2][:], in_=gt2[:, 0:7],
                    axis=AX.X), reads=[s_gt2], writes=[s_gcol[2]])
            bk.emit("vector", lambda e: e.tensor_scalar(out=gsel[:, 0:7], in0=gtmp[:, 0:7],
                    scalar1=gcol[2][:, 0:1], scalar2=None, op0=ALU.is_ge),
                    reads=[s_gtmp, s_gcol[2]], writes=[s_gsel])
            bk.emit("vector", lambda e, tt=tt: e.tensor_tensor(out=wcol[tt][:, 0:7],
                    in0=glog[:, 0:7], in1=gsel[:, 0:7], op=ALU.mult),
                    reads=[s_glog, s_gsel], writes=[s_wcol[tt]])
            bk.emit("vector", lambda e, tt=tt: e.reduce_sum(out=gcol[0][:],
                    in_=wcol[tt][:, 0:7], axis=AX.X), reads=[s_wcol[tt]], writes=[s_gcol[0]])
            bk.emit("vector", lambda e: e.reciprocal(out=gcol[1][:], in_=gcol[0][:]),
                    reads=[s_gcol[0]], writes=[s_gcol[1]])
            bk.emit("vector", lambda e, tt=tt: e.tensor_scalar(out=wcol[tt][:, 0:7],
                    in0=wcol[tt][:, 0:7], scalar1=gcol[1][:, 0:1], scalar2=0.5,
                    op0=ALU.mult, op1=ALU.mult), reads=[s_wcol[tt], s_gcol[1]],
                    writes=[s_wcol[tt]])
        bk.scope_mark(f"m_exp_tc{tc}_{app}")
        for e_i in range(NE):
            for hp in range(C // 512):
                par = hp % 2
                sem1 = d_s1[par]
                bk.emit("gpsimd", lambda e, e_i=e_i, hp=hp: e.dma_start(
                    out=slab1a.rearrange("p (c h) -> p c h", h=512),
                    in_=src1("W1", e_i)[:, :, hp * 512:(hp + 1) * 512]),
                    writes=[s_slab1a], dma=sem1)
                bk.seal(sem1, [s_slab1a])
                semg = d_sg[par]
                bk.emit("gpsimd", lambda e, e_i=e_i, hp=hp: e.dma_start(
                    out=slabga.rearrange("p (c h) -> p c h", h=512),
                    in_=src1("Wg", e_i)[:, :, hp * 512:(hp + 1) * 512]),
                    writes=[s_slabga], dma=semg)
                bk.seal(semg, [s_slabga])
                sem2 = d_s2[par]
                bk.emit("gpsimd", lambda e, e_i=e_i, hp=hp: e.dma_start(
                    out=slab2a.rearrange("p (h c) -> p h c", c=C),
                    in_=src2(e_i)[:, hp * 4:(hp + 1) * 4, :]),
                    writes=[s_slab2a], dma=sem2)
                bk.seal(sem2, [s_slab2a])
                for hb in range(4):
                    for th in range(NTH):
                        pp = (hb * NTH + th) % 2
                        for c in range(NCH):
                            bk.mm(lambda e, hb=hb, th=th, c=c, pp=pp: e.matmul(
                                psh[pp][:], slab1[c][:, hb * 128:(hb + 1) * 128],
                                zT[c][:, th * 512:(th + 1) * 512],
                                start=(c == 0), stop=(c == NCH - 1)),
                                reads=[s_slab1[c], s_zT[c]], psum=s_psh[pp],
                                first=(c == 0), last=(c == NCH - 1))
                        bk.emit("scalar", lambda e, hb=hb, th=th, pp=pp: e.activation(
                            out=s1t[hb][:, th * 512:(th + 1) * 512], in_=psh[pp][:],
                            func=AF.Silu), reads=[s_psh[pp]], writes=[s_s1t[hb]])
                        pp2 = 2 + (hb * NTH + th) % 2
                        for c in range(NCH):
                            bk.mm(lambda e, hb=hb, th=th, c=c, pp2=pp2: e.matmul(
                                psh[pp2][:], slabg[c][:, hb * 128:(hb + 1) * 128],
                                zT[c][:, th * 512:(th + 1) * 512],
                                start=(c == 0), stop=(c == NCH - 1)),
                                reads=[s_slabg[c], s_zT[c]], psum=s_psh[pp2],
                                first=(c == 0), last=(c == NCH - 1))
                        bk.emit("vector", lambda e, hb=hb, th=th, pp2=pp2: e.tensor_tensor(
                            out=hh[hb][:, th * 512:(th + 1) * 512], in0=psh[pp2][:],
                            in1=s1t[hb][:, th * 512:(th + 1) * 512], op=ALU.mult),
                            reads=[s_psh[pp2], s_s1t[hb]], writes=[s_hh[hb]])
                for tt in range(NTT):
                    gt = tc * NTT + tt
                    for ccol in range(C // 512):
                        o = (tt * (C // 512) + ccol) % 2
                        for hc in range(4):
                            bk.mm(lambda e, hc=hc, tt=tt, ccol=ccol, o=o: e.matmul(
                                pso[o][:], hh[hc][:, tt * 128:(tt + 1) * 128],
                                slab2[hc][:, ccol * 512:(ccol + 1) * 512],
                                start=(hc == 0), stop=(hc == 3)),
                                reads=[s_hh[hc], s_slab2[hc]], psum=s_pso[o],
                                first=(hc == 0), last=(hc == 3))
                        if e_i == 7:
                            bk.emit("vector", lambda e, tt=tt, ccol=ccol, o=o:
                                    e.scalar_tensor_tensor(
                                    out=cm.xa[tt][:, ccol * 512:(ccol + 1) * 512],
                                    in0=pso[o][:], scalar=0.5,
                                    in1=cm.xa[tt][:, ccol * 512:(ccol + 1) * 512],
                                    op0=ALU.mult, op1=ALU.add),
                                    reads=[s_pso[o], cm.s_xa[tt]], writes=[cm.s_xa[tt]])
                        else:
                            bk.emit("vector", lambda e, tt=tt, ccol=ccol, o=o, e_i=e_i, gt=gt:
                                    e.scalar_tensor_tensor(
                                    out=cm.xa[tt][:, ccol * 512:(ccol + 1) * 512],
                                    in0=pso[o][:], scalar=wcol[tt][:, e_i:e_i + 1],
                                    in1=cm.xa[tt][:, ccol * 512:(ccol + 1) * 512],
                                    op0=ALU.mult, op1=ALU.add),
                                    reads=[s_pso[o], cm.s_xa[tt], s_wcol[tt]],
                                    writes=[cm.s_xa[tt]])
        cm.store_xacc(xo_d, tc)
    cm.finish_x()





def S(n=""):
    return Slot(n)


EW = 256


def emit_l1attn_casts(bk, W):
    s_cast = {}
    for nm in ("q", "k", "v", "o"):
        sem = bk.sem(f"a1_c{nm}")
        s_cast[nm] = S()
        bk.emit("gpsimd", lambda e, nm=nm: e.dma_start(
            out=W[nm + "b"], in_=W["W" + nm]), writes=[s_cast[nm]], dma=sem)
        bk.seal(sem, [s_cast[nm]])
    return s_cast


def emit_l1attn(nc, bk, cm, W, zscr_d, x_d, xo_d, arena, s_cast):
    sb, ps = bk.sb, bk.ps
    s_zscr = [S(), S()]
    NTT = TCW // 128
    NTH = TCW // 512
    NEP = C // EW
    NEB = EW // 128
    denom = float(C) ** 0.5 / float(T)

    cm.ln_pass(x_d, zscr_d, s_zscr)

    zT = [arena.alloc(f"a1_zT{c}", [128, TCW], BF16) for c in range(NCH)]
    s_zT = [S() for _ in range(NCH)]
    wq = [arena.alloc(f"a1_wq{c}", [128, EW], BF16) for c in range(NCH)]
    wk = [arena.alloc(f"a1_wk{c}", [128, EW], BF16) for c in range(NCH)]
    wv = [arena.alloc(f"a1_wv{c}", [128, EW], BF16) for c in range(NCH)]
    s_wq = [S() for _ in range(NCH)]
    s_wk = [S() for _ in range(NCH)]
    s_wv = [S() for _ in range(NCH)]
    wo = [arena.alloc(f"a1_wo{b}", [128, C], BF16) for b in range(NEB)]
    s_wo = [S() for _ in range(NEB)]
    qf = [arena.alloc(f"a1_qf{b}", [128, TCW], F32) for b in range(NEB)]
    kf = [arena.alloc(f"a1_kf{b}", [128, TCW], F32) for b in range(NEB)]
    vf = [arena.alloc(f"a1_vf{b}", [128, TCW], F32) for b in range(NEB)]
    kvt = [arena.alloc(f"a1_kvt{b}", [128, TCW], F32) for b in range(NEB)]
    vhb = [arena.alloc(f"a1_vhb{b}", [128, TCW], BF16) for b in range(NEB)]
    s_qf = [S() for _ in range(NEB)]
    s_kf = [S() for _ in range(NEB)]
    s_vf = [S() for _ in range(NEB)]
    s_kvt = [S() for _ in range(NEB)]
    s_vhb = [S() for _ in range(NEB)]
    zeros = arena.alloc("a1_zeros", [128, TCW], F32)
    s_zeros = S()
    state = sb("a1_state", [128, NCH], F32)
    s_state = [S() for _ in range(NCH)]
    bk.emit("vector", lambda e: e.memset(zeros[:], 0.0), writes=[s_zeros])
    st_init = bk.emit("vector", lambda e: e.memset(state[:], 0.0), writes=s_state)
    psq = cm.pp[0:3]
    s_psq = cm.s_pp[0:3]
    pso = cm.pp[4:6]
    s_pso = cm.s_pp[4:6]
    d_zt = bk.sem("a1_zt")
    d_w = [bk.sem(f"a1_dw{j}") for j in range(2)]
    d_wo = [bk.sem(f"a1_dwo{j}") for j in range(2)]

    wqbT = {"q": W["qb"], "k": W["kb"], "v": W["vb"]}

    for tc in range(NTCOL):
        cm.load_xacc(x_d, tc)
        for c in range(NCH):
            bk.emit("scalar", lambda e, c=c, tc=tc: e.dma_start_transpose(
                out=zT[c][:], in_=zscr_d[tc * TCW:(tc + 1) * TCW, c * 128:(c + 1) * 128]),
                reads=s_zscr, writes=[s_zT[c]], dma=d_zt)
        bk.seal(d_zt, s_zT)
        for ep in range(NEP):
            par = ep % 2
            semw = d_w[par]
            for c in range(NCH):
                for nm, buf, slot in (("q", wq, s_wq), ("k", wk, s_wk), ("v", wv, s_wv)):
                    bk.emit("scalar", lambda e, c=c, ep=ep, nm=nm, buf=buf: e.dma_start_transpose(
                        out=buf[c][:], in_=wqbT[nm][ep * EW:(ep + 1) * EW, c * 128:(c + 1) * 128]),
                        reads=[s_cast[nm]], writes=[slot[c]], dma=semw)
            bk.seal(semw, s_wq + s_wk + s_wv)
            semo = d_wo[par]
            for b in range(NEB):
                bk.emit("scalar", lambda e, b=b, ep=ep: e.dma_start_transpose(
                    out=wo[b][:], in_=W["ob"][:, (ep * NEB + b) * 128:(ep * NEB + b + 1) * 128]),
                    reads=[s_cast["o"]], writes=[s_wo[b]], dma=semo)
            bk.seal(semo, s_wo)
            for b in range(NEB):
                for proj, (buf, slot, fbuf, fslot) in enumerate(
                        ((wq, s_wq, qf, s_qf), (wk, s_wk, kf, s_kf), (wv, s_wv, vf, s_vf))):
                    pp = proj
                    for th in range(NTH):
                        for c in range(NCH):
                            bk.mm(lambda e, b=b, th=th, c=c, pp=pp, buf=buf: e.matmul(
                                psq[pp][:], buf[c][:, b * 128:(b + 1) * 128],
                                zT[c][:, th * 512:(th + 1) * 512],
                                start=(c == 0), stop=(c == NCH - 1)),
                                reads=[slot[c], s_zT[c]], psum=s_psq[pp],
                                first=(c == 0), last=(c == NCH - 1))
                        bk.emit("vector", lambda e, b=b, th=th, pp=pp, fbuf=fbuf: e.tensor_copy(
                            out=fbuf[b][:, th * 512:(th + 1) * 512], in_=psq[pp][:]),
                            reads=[s_psq[pp]], writes=[fslot[b]])
                gc = ep * NEB + b
                bk.emit("vector", lambda e, b=b: e.tensor_tensor(out=kf[b][:], in0=kf[b][:],
                        in1=vf[b][:], op=ALU.mult), reads=[s_kf[b], s_vf[b]], writes=[s_kf[b]])
                bk.emit("vector", lambda e, b=b, gc=gc: e.tensor_tensor_scan(
                        out=kvt[b][:], data0=kf[b][:], data1=zeros[:],
                        initial=state[:, gc:gc + 1], op0=ALU.add, op1=ALU.add),
                        reads=[s_kf[b], s_zeros, s_state[gc]], writes=[s_kvt[b]])
                bk.emit("vector", lambda e, b=b, gc=gc: e.tensor_copy(
                        out=state[:, gc:gc + 1], in_=kvt[b][:, TCW - 1:TCW]),
                        reads=[s_kvt[b]], writes=[s_state[gc]])
                bk.emit("vector", lambda e, b=b: e.scalar_tensor_tensor(
                        out=vhb[b][:], in0=kvt[b][:], scalar=denom, in1=qf[b][:],
                        op0=ALU.mult, op1=ALU.mult),
                        reads=[s_kvt[b], s_qf[b]], writes=[s_vhb[b]])
            for tt in range(NTT):
                for ccol in range(C // 512):
                    o = (tt * (C // 512) + ccol) % 2
                    for b in range(NEB):
                        bk.mm(lambda e, tt=tt, ccol=ccol, o=o, b=b: e.matmul(
                            pso[o][:], vhb[b][:, tt * 128:(tt + 1) * 128],
                            wo[b][:, ccol * 512:(ccol + 1) * 512],
                            start=(b == 0), stop=(b == NEB - 1)),
                            reads=[s_vhb[b], s_wo[b]], psum=s_pso[o],
                            first=(b == 0), last=(b == NEB - 1))
                    bk.emit("vector", lambda e, tt=tt, ccol=ccol, o=o: e.tensor_add(
                        out=cm.xa[tt][:, ccol * 512:(ccol + 1) * 512],
                        in0=cm.xa[tt][:, ccol * 512:(ccol + 1) * 512], in1=pso[o][:]),
                        reads=[s_pso[o], cm.s_xa[tt]], writes=[cm.s_xa[tt]])
        cm.store_xacc(xo_d, tc)
    cm.finish_x()





def S(n=""):
    return Slot(n)


GATE_OFFSET = 6.906768
NKV = 8
NQH = 16
LN_C = math.log(1.0 / math.sqrt(128.0))


def emit_l0attn_casts(bk, W):
    s_cast = {}
    for nm, src in (("qb", "Wq"), ("kb", "Wk"), ("vb", "Wv")):
        sem = bk.sem(f"a0_c{nm}")
        s_cast[nm] = S()
        bk.emit("gpsimd", lambda e, nm=nm, src=src: e.dma_start(
            out=W[nm], in_=W[src]), writes=[s_cast[nm]], dma=sem)
        bk.seal(sem, [s_cast[nm]])
    return s_cast


def emit_l0attn(nc, bk, cm, W, zscr_d, x_d, xo_d, arena, s_cast, bx=()):
    sb, ps = bk.sb, bk.ps
    s_zscr = [S(), S()]
    NTT = TCW // 128
    NTH = TCW // 512

    cm.ln_pass(x_d, zscr_d, s_zscr)

    pp = cm.pp[0:8]
    s_pp = cm.s_pp[0:8]
    ps_g = cm.pp[6][:, 0:8]
    s_psg = cm.s_pp[6]
    ps_n = cm.pp[7][0:1, :]
    s_psn = cm.s_pp[7]

    onecolf = arena.alloc("a0_one", [128, 1], F32)
    s_onecolf = S()
    bk.emit("vector", lambda e: e.memset(onecolf[:], 1.0), writes=[s_onecolf])
    lncol = arena.alloc("a0_lnc", [128, 1], F32)
    s_lncol = S()
    bk.emit("vector", lambda e: e.memset(lncol[:], LN_C), writes=[s_lncol])
    row1 = arena.alloc("a0_row1", [1, T], F32)
    s_row1 = S()
    logg_tm = [arena.alloc(f"a0_lg{t}", [128, 8], F32) for t in range(NT)]
    s_logg_tm = [S() for _ in range(NT)]
    sqes_tm = [arena.alloc(f"a0_se{t}", [128, 8], F32) for t in range(NT)]
    s_sqes_tm = [S() for _ in range(NT)]
    gu = arena.alloc("a0_gu", [128, 8], F32)
    ga = arena.alloc("a0_ga", [128, 8], F32)
    gl = arena.alloc("a0_gl", [128, 8], F32)
    s_gu, s_ga, s_gl = S(), S(), S()
    lgpre = arena.alloc("a0_lgpre", [8, T], F32)
    lgcum = arena.alloc("a0_lgcum", [8, T], F32)
    s_lgpre, s_lgcum = S(), S()
    ebuf = [arena.alloc(f"a0_ebuf{j}", [128, 512], BF16) for j in range(4)]
    s_ebuf = [S() for _ in range(4)]
    d_st = [bk.sem(f"a0_dst{j}") for j in range(4)]
    s_qscr, s_kscr, s_vscr = S(), S(), S()
    ecnt = [0]

    wgf = [arena.alloc(f"a0_wgf{c}", [128, 8], F32) for c in range(NCH)]
    wgb = [arena.alloc(f"a0_wgb{c}", [128, 8], BF16) for c in range(NCH)]
    s_wgf = [S() for _ in range(NCH)]
    s_wgb = [S() for _ in range(NCH)]
    d_wg = bk.sem("a0_wg")
    wg3 = W["Wg"].rearrange("(c p) g -> c p g", p=128)
    for c in range(NCH):
        bk.emit("sync", lambda e, c=c: e.dma_start(out=wgf[c][:, 0:8], in_=wg3[c]),
                writes=[s_wgf[c]], dma=d_wg)
    bk.seal(d_wg, s_wgf)
    for c in range(NCH):
        bk.emit("vector", lambda e, c=c: e.tensor_copy(out=wgb[c][:], in_=wgf[c][:]),
                reads=[s_wgf[c]], writes=[s_wgb[c]])

    arena = arena.child("a0_sub")

    def evac_store(u, dst_ap, wslot):
        j = ecnt[0] % 4
        ecnt[0] += 1
        bk.emit("vector", lambda e, u=u, j=j: e.tensor_copy(out=ebuf[j][:], in_=pp[u][:]),
                reads=[s_pp[u]], writes=[s_ebuf[j]])
        bk.emit("sync", lambda e, j=j, dst_ap=dst_ap: e.dma_start(out=dst_ap, in_=ebuf[j][:]),
                reads=[s_ebuf[j]], writes=[wslot], dma=d_st[j])

    zT = [arena.alloc(f"zT{c}", [128, TCW], BF16) for c in range(NCH)]
    s_zT = [S() for _ in range(NCH)]
    wqsl = [arena.alloc(f"wqsl{j}", [128, 2048], BF16) for j in range(2)]
    s_wqsl = [S() for _ in range(2)]
    wksl = [arena.alloc(f"wksl{j}", [128, 2048], BF16) for j in range(2)]
    s_wksl = [S() for _ in range(2)]
    wvsla = arena.alloc("wvsl", [128, NCH * 512], BF16)
    wvsl = [wvsla[:, c * 512:(c + 1) * 512] for c in range(NCH)]
    s_wvsla = S()
    s_wvsl = [s_wvsla] * NCH
    d_zt = bk.sem("a0_zt")
    d_wq = [bk.sem(f"a0_dwq{j}") for j in range(2)]
    d_wv = bk.sem("a0_dwv")
    wqb3 = W["Wq"].rearrange("(c p) e -> p c e", p=128)
    wkb3 = W["Wk"].rearrange("(c p) e -> p c e", p=128)
    wvb3 = W["Wv"].rearrange("(c p) e -> p c e", p=128)

    for tc in range(NTCOL):
        bk.scope_mark(f"a0_proj{tc}")
        for c in range(NCH):
            bk.emit("scalar", lambda e, c=c, tc=tc: e.dma_start_transpose(
                out=zT[c][:], in_=zscr_d[tc * TCW:(tc + 1) * TCW, c * 128:(c + 1) * 128]),
                reads=s_zscr, writes=[s_zT[c]], dma=d_zt)
        bk.seal(d_zt, s_zT)
        for qb in range(16):
            j = qb % 2
            bk.emit("gpsimd", lambda e, qb=qb, j=j: e.dma_start(
                out=wqsl[j].rearrange("p (c e) -> p c e", e=128),
                in_=wqb3[:, :, qb * 128:(qb + 1) * 128]),
                writes=[s_wqsl[j]], dma=d_wq[j])
            bk.seal(d_wq[j], [s_wqsl[j]])
            for th in range(NTH):
                u = (qb * NTH + th) % 2
                for c in range(NCH):
                    bk.mm(lambda e, th=th, c=c, u=u, j=j: e.matmul(
                        pp[u][:], wqsl[j][:, c * 128:(c + 1) * 128],
                        zT[c][:, th * 512:(th + 1) * 512],
                        start=(c == 0), stop=(c == NCH - 1)),
                        reads=[s_wqsl[j], s_zT[c]], psum=s_pp[u],
                        first=(c == 0), last=(c == NCH - 1))
                evac_store(u, W["qscr"][qb * 128:(qb + 1) * 128,
                           tc * TCW + th * 512:tc * TCW + (th + 1) * 512], s_qscr)
        for kb in range(8):
            j = kb % 2
            bk.emit("gpsimd", lambda e, kb=kb, j=j: e.dma_start(
                out=wksl[j].rearrange("p (c e) -> p c e", e=128),
                in_=wkb3[:, :, kb * 128:(kb + 1) * 128]),
                writes=[s_wksl[j]], dma=d_wq[j])
            bk.seal(d_wq[j], [s_wksl[j]])
            for th in range(NTH):
                u = (kb * NTH + th) % 2
                for c in range(NCH):
                    bk.mm(lambda e, th=th, c=c, u=u, j=j: e.matmul(
                        pp[u][:], wksl[j][:, c * 128:(c + 1) * 128],
                        zT[c][:, th * 512:(th + 1) * 512],
                        start=(c == 0), stop=(c == NCH - 1)),
                        reads=[s_wksl[j], s_zT[c]], psum=s_pp[u],
                        first=(c == 0), last=(c == NCH - 1))
                evac_store(u, W["kscr"][kb * 128:(kb + 1) * 128,
                           tc * TCW + th * 512:tc * TCW + (th + 1) * 512], s_kscr)
        for vh in range(2):
            bk.emit("gpsimd", lambda e, vh=vh: e.dma_start(
                out=wvsla.rearrange("p (c e) -> p c e", e=512),
                in_=wvb3[:, :, vh * 512:(vh + 1) * 512]),
                writes=[s_wvsla], dma=d_wv)
            bk.seal(d_wv, [s_wvsla])
            for tt in range(NTT):
                gt = tc * NTT + tt
                u = (vh * NTT + tt) % 2 + 2
                for c in range(NCH):
                    bk.mm(lambda e, tt=tt, c=c, u=u: e.matmul(
                        pp[u][:], zT[c][:, tt * 128:(tt + 1) * 128], wvsl[c][:],
                        start=(c == 0), stop=(c == NCH - 1)),
                        reads=[s_wvsl[c], s_zT[c]], psum=s_pp[u],
                        first=(c == 0), last=(c == NCH - 1))
                evac_store(u, W["vscr"][gt * 128:(gt + 1) * 128,
                           vh * 512:(vh + 1) * 512], s_vscr)
        for tt in range(NTT):
            gt = tc * NTT + tt
            for c in range(NCH):
                bk.mm(lambda e, tt=tt, c=c: e.matmul(
                    ps_g, zT[c][:, tt * 128:(tt + 1) * 128], wgb[c][:],
                    start=(c == 0), stop=(c == NCH - 1)),
                    reads=[s_wgb[c], s_zT[c]], psum=s_psg,
                    first=(c == 0), last=(c == NCH - 1))
            bk.emit("vector", lambda e: e.tensor_copy(out=gu[:], in_=ps_g),
                    reads=[s_psg], writes=[s_gu])
            bk.emit("vector", lambda e: e.tensor_scalar(out=gu[:], in0=gu[:],
                    scalar1=GATE_OFFSET, scalar2=None, op0=ALU.add),
                    reads=[s_gu], writes=[s_gu])
            bk.emit("scalar", lambda e: e.activation(out=ga[:], in_=gu[:],
                    func=AF.Abs), reads=[s_gu], writes=[s_ga])
            bk.emit("scalar", lambda e: e.activation(out=ga[:], in_=ga[:],
                    func=AF.Exp, scale=-1.0), reads=[s_ga], writes=[s_ga])
            bk.emit("scalar", lambda e: e.activation(out=ga[:], in_=ga[:],
                    func=AF.Ln, bias=onecolf[:, 0:1]),
                    reads=[s_ga, s_onecolf], writes=[s_ga])
            bk.emit("vector", lambda e: e.tensor_scalar(out=gl[:], in0=gu[:],
                    scalar1=-1.0, scalar2=0.0, op0=ALU.mult, op1=ALU.max),
                    reads=[s_gu], writes=[s_gl])
            bk.emit("vector", lambda e, gt=gt: e.scalar_tensor_tensor(
                    out=logg_tm[gt][:], in0=gl[:], scalar=-1.0, in1=ga[:],
                    op0=ALU.mult, op1=ALU.subtract),
                    reads=[s_gl, s_ga], writes=[s_logg_tm[gt]])

    bk.scope_mark("a0_logg")
    for t in range(NT):
        u = t % 2
        bk.mm(lambda e, t=t, u=u: e.transpose(pp[u][0:8, 0:128], logg_tm[t][:],
              cm.identf[:]), reads=[s_logg_tm[t], cm.s_identf], psum=s_pp[u])
        bk.emit("vector", lambda e, t=t, u=u: e.tensor_copy(
            out=lgpre[:, t * 128:(t + 1) * 128], in_=pp[u][0:8, 0:128]),
            reads=[s_pp[u]], writes=[s_lgpre])
    bk.emit("vector", lambda e: e.tensor_tensor_scan(
        out=lgcum[:], data0=lgpre[:], data1=lgpre[:], initial=0.0,
        op0=ALU.add, op1=ALU.bypass), reads=[s_lgpre], writes=[s_lgcum])
    d_lg = bk.sem("a0_dlg")
    s_loggscr = S()
    et8 = arena.alloc("a0_et8", [8, T], BF16)
    s_et8 = S()
    bk.emit("scalar", lambda e: e.activation(out=et8[:], in_=lgcum[:], func=AF.Exp),
            reads=[s_lgcum], writes=[s_et8])
    bk.emit("sync", lambda e: e.dma_start(out=W["eta"], in_=et8[:]),
            reads=[s_et8], writes=[s_loggscr], dma=d_lg)
    bk.seal(d_lg, [s_loggscr])
    for t in range(NT):
        u = t % 2
        bk.mm(lambda e, t=t, u=u: e.transpose(pp[u][0:128, 0:8],
              lgcum[:, t * 128:(t + 1) * 128], cm.identf[0:8, 0:8]),
              reads=[s_lgcum, cm.s_identf], psum=s_pp[u])
        bk.emit("scalar", lambda e, t=t, u=u: e.activation(out=sqes_tm[t][:],
                in_=pp[u][0:128, 0:8], func=AF.Exp, scale=-0.5, bias=lncol[:, 0:1]),
                reads=[s_pp[u], s_lncol], writes=[s_sqes_tm[t]])

    bk.barrier(exclude=bx)
    arena.reset()
    qh = arena.alloc("qh", [128, T], BF16)
    kh = arena.alloc("kh", [128, T], BF16)
    vh = [arena.alloc(f"vh{t}", [128, 128], BF16) for t in range(NT)]
    yh = arena.alloc("yh", [128, T], BF16)
    eb = arena.alloc("eb", [128, T], BF16)
    sq = [arena.alloc(f"sq{j}", [128, 512], BF16) for j in range(8)]
    at = [arena.alloc(f"at{j}", [128, 512], BF16) for j in range(8)]
    mskt = [arena.alloc(f"msk{j}", [128, 512], BF16) for j in range(4)]
    etm = arena.alloc("etm", [128, 16 * 512], BF16)
    nrow = arena.alloc("nrow", [1, 512], F32)
    nrowb = [arena.alloc(f"nrowb{j}", [1, 512], BF16) for j in range(2)]
    rbb = [arena.alloc(f"rbb{j}", [128, 512], BF16) for j in range(2)]
    s_rnscr = [S(), S()]
    d_rn = [bk.sem(f"a0_drn{j}") for j in range(2)]
    d_rb = [bk.sem(f"a0_drb{j}") for j in range(2)]
    s_qh, s_kh, s_yh, s_eb = S(), S(), S(), S()
    s_vh = [S() for _ in range(NT)]
    s_sq = [S() for _ in range(8)]
    s_at = [S() for _ in range(8)]
    s_mskt = [S() for _ in range(4)]
    s_etm = S()
    s_nrow = S()
    s_nrowb = [S(), S()]
    s_rbb = [S(), S()]
    d_qh = [bk.sem(f"a0_dqh{j}") for j in range(2)]
    d_kh = bk.sem("a0_dkh")
    d_vh = bk.sem("a0_dvh")
    d_er = bk.sem("a0_der")
    d_ys = [bk.sem(f"a0_dys{j}") for j in range(2)]
    s_yscr = S()

    for j in range(4):
        bk.emit("vector", lambda e, j=j: e.memset(mskt[j][:], 1.0), writes=[s_mskt[j]])
        bk.emit("gpsimd", lambda e, j=j: e.affine_select(
            out=mskt[j][:], in_=mskt[j][:], pattern=[[1, 512]],
            compare_op=ALU.is_ge, fill=0.0, base=-128 * j, channel_multiplier=-1),
            reads=[s_mskt[j]], writes=[s_mskt[j]])
    for h in range(NQH):
        bk.scope_mark(f"a0_head{h}")
        hk = h // 2
        if h % 2 == 0:
            bk.emit("sync", lambda e, hk=hk: e.dma_start(
                out=kh[:], in_=W["kscr"][hk * 128:(hk + 1) * 128, :]),
                reads=[s_kscr], writes=[s_kh], dma=d_kh)
            bk.seal(d_kh, [s_kh])
            for t in range(NT):
                bk.emit("sync", lambda e, t=t, hk=hk: e.dma_start(
                    out=vh[t][:], in_=W["vscr"][t * 128:(t + 1) * 128,
                                                hk * 128:(hk + 1) * 128]),
                    reads=[s_vscr], writes=[s_vh[t]], dma=d_vh)
            bk.seal(d_vh, s_vh)
            bk.emit("sync", lambda e, hk=hk: e.dma_start(
                out=eb[:], in_=W["eta"][hk:hk + 1, :].partition_broadcast(128)),
                reads=[s_loggscr], writes=[s_eb], dma=d_er)
            bk.seal(d_er, [s_eb])
            for tq in range(4):
                for j in range(4):
                    bk.emit("vector", lambda e, tq=tq, j=j: e.tensor_tensor(
                        out=etm[:, (tq * 4 + j) * 512:(tq * 4 + j + 1) * 512],
                        in0=eb[:, tq * 512:(tq + 1) * 512], in1=mskt[j][:],
                        op=ALU.mult), reads=[s_eb, s_mskt[j]], writes=[s_etm])
        bk.emit("sync", lambda e, h=h: e.dma_start(
            out=qh[:], in_=W["qscr"][h * 128:(h + 1) * 128, :]),
            reads=[s_qscr], writes=[s_qh], dma=d_qh[h % 2])
        bk.seal(d_qh[h % 2], [s_qh])
        spool = (0, 1, 3, 4)
        DIST = 6
        tail = [None]
        for tq in range(4):
            nsb = 4 * tq + 4
            avb = 2 if tq % 2 == 0 else 5
            nrb = 7 if tq % 2 == 0 else 6
            psn_t, s_psn_t = pp[nrb][0:1, :], s_pp[nrb]

            def emit_qk(sbk, tq=tq):
                u = spool[sbk % 4]
                b4 = sbk % 8
                bk.mm(lambda e, sbk=sbk, tq=tq, u=u: e.matmul(
                    pp[u][:], kh[:, sbk * 128:(sbk + 1) * 128],
                    qh[:, tq * 512:(tq + 1) * 512], start=True, stop=True),
                    reads=[s_kh, s_qh], psum=s_pp[u])
                bk.emit("scalar", lambda e, sbk=sbk, u=u, b4=b4, hk=hk: e.activation(
                    out=sq[b4][:], in_=pp[u][:], func=AF.Square,
                    scale=sqes_tm[sbk][:, hk:hk + 1]),
                    reads=[s_pp[u], s_sqes_tm[sbk]], writes=[s_sq[b4]])
                j = sbk - 4 * tq
                esrc = (etm[:, (tq * 4 + j) * 512:(tq * 4 + j + 1) * 512], s_etm) \
                    if j >= 0 else (eb[:, tq * 512:(tq + 1) * 512], s_eb)
                bk.emit("vector", lambda e, b4=b4, esrc=esrc: e.tensor_tensor(
                    out=at[b4][:], in0=sq[b4][:], in1=esrc[0],
                    op=ALU.mult), reads=[s_sq[b4], esrc[1]], writes=[s_at[b4]])

            def emit_nav(sbk, nsb=nsb):
                b4 = sbk % 8
                bk.mm(lambda e, sbk=sbk, b4=b4, nsb=nsb, psn_t=psn_t: e.matmul(
                    psn_t, cm.onecol[:], at[b4][:],
                    start=(sbk == 0), stop=(sbk == nsb - 1)),
                    reads=[cm.s_onecol, s_at[b4]], psum=s_psn_t,
                    first=(sbk == 0), last=(sbk == nsb - 1))
                bk.mm(lambda e, sbk=sbk, b4=b4, nsb=nsb, avb=avb: e.matmul(
                    pp[avb][:], vh[sbk][:], at[b4][:],
                    start=(sbk == 0), stop=(sbk == nsb - 1)),
                    reads=[s_vh[sbk], s_at[b4]], psum=s_pp[avb],
                    first=(sbk == 0), last=(sbk == nsb - 1))

            for sbk in range(nsb):
                emit_qk(sbk)
                if sbk == 2 and tail[0] is not None:
                    tail[0]()
                    tail[0] = None
                if sbk >= DIST:
                    emit_nav(sbk - DIST)
            if tail[0] is not None:
                tail[0]()
                tail[0] = None
            for sbk in range(max(0, nsb - DIST), nsb):
                emit_nav(sbk)
            bk.emit("vector", lambda e, psn_t=psn_t: e.tensor_scalar(out=nrow[:],
                    in0=psn_t, scalar1=1e-6, scalar2=None, op0=ALU.add),
                    reads=[s_psn_t], writes=[s_nrow])
            bk.emit("scalar", lambda e: e.activation(out=nrow[:], in_=nrow[:],
                    func=AF.Ln), reads=[s_nrow], writes=[s_nrow])
            bk.emit("scalar", lambda e: e.activation(out=nrow[:], in_=nrow[:],
                    func=AF.Exp, scale=-1.0), reads=[s_nrow], writes=[s_nrow])
            bk.emit("vector", lambda e, tq=tq: e.tensor_copy(out=nrowb[tq % 2][:],
                    in_=nrow[:]), reads=[s_nrow], writes=[s_nrowb[tq % 2]])

            def emit_bcast_ymult(tq=tq, avb=avb, nrb=nrb):
                bk.mm(lambda e, nrb=nrb, tq=tq: e.matmul(pp[nrb][:], cm.onesr[:],
                      nrowb[tq % 2][:], start=True, stop=True),
                      reads=[cm.s_onesr, s_nrowb[tq % 2]], psum=s_pp[nrb])
                bk.emit("vector", lambda e, nrb=nrb, tq=tq: e.tensor_copy(
                        out=rbb[tq % 2][:], in_=pp[nrb][:]),
                        reads=[s_pp[nrb]], writes=[s_rbb[tq % 2]])
                bk.emit("vector", lambda e, tq=tq, avb=avb: e.tensor_tensor(
                        out=yh[:, tq * 512:(tq + 1) * 512], in0=pp[avb][:],
                        in1=rbb[tq % 2][:], op=ALU.mult),
                        reads=[s_pp[avb], s_rbb[tq % 2]], writes=[s_yh])
            tail[0] = emit_bcast_ymult
        if tail[0] is not None:
            tail[0]()
            tail[0] = None
        bk.emit("sync", lambda e, h=h: e.dma_start(
            out=W["yscr"][h * 128:(h + 1) * 128, :], in_=yh[:]),
            reads=[s_yh], writes=[s_yscr], dma=d_ys[h % 2])
        bk.seal(d_ys[h % 2], [s_yscr])

    bk.barrier(exclude=bx)
    arena.reset()
    ysl = [arena.alloc(f"ysl{q}", [128, TCW], BF16) for q in range(16)]
    s_ysl = [S() for _ in range(16)]
    wcs = [arena.alloc(f"wcs{q}", [128, 512], BF16) for q in range(16)]
    s_wcs = [S() for _ in range(16)]
    bcb = arena.alloc("bcb", [128, C], F32)
    s_bcb = S()
    d_bc = bk.sem("a0_dbc")
    d_ysl = [bk.sem(f"a0_dysl{j}") for j in range(2)]
    d_wcs = [bk.sem(f"a0_dwcs{j}") for j in range(2)]
    bk.emit("sync", lambda e: e.dma_start(out=row1[:],
            in_=W["bc"].rearrange("(b a) -> b a", b=1)), writes=[s_row1], dma=d_bc)
    bk.seal(d_bc, [s_row1])
    for q4 in range(4):
        bk.mm(lambda e, q4=q4: e.matmul(pp[5][:], cm.onesrf[:],
              row1[:, q4 * 512:(q4 + 1) * 512], start=True, stop=True),
              reads=[cm.s_onesrf, s_row1], psum=s_pp[5])
        bk.emit("vector", lambda e, q4=q4: e.tensor_copy(
                out=bcb[:, q4 * 512:(q4 + 1) * 512], in_=pp[5][:]),
                reads=[s_pp[5]], writes=[s_bcb])
    wc3 = W["Wc"].rearrange("(q p) c -> q p c", p=128)
    for tc in range(NTCOL):
        bk.scope_mark(f"a0_wc{tc}")
        cm.load_xacc(x_d, tc)
        for tt in range(NTT):
            bk.emit("vector", lambda e, tt=tt: e.tensor_add(out=cm.xa[tt][:],
                    in0=cm.xa[tt][:], in1=bcb[:]),
                    reads=[s_bcb, cm.s_xa[tt]], writes=[cm.s_xa[tt]])
        sem = d_ysl[tc % 2]
        for q in range(16):
            bk.emit("sync", lambda e, q=q, tc=tc: e.dma_start(
                out=ysl[q][:], in_=W["yscr"][q * 128:(q + 1) * 128,
                                             tc * TCW:(tc + 1) * TCW]),
                reads=[s_yscr], writes=[s_ysl[q]], dma=sem)
        bk.seal(sem, s_ysl)
        for ccol in range(C // 512):
            semw = d_wcs[ccol % 2]
            for q in range(16):
                bk.emit("gpsimd", lambda e, q=q, ccol=ccol: e.dma_start(
                    out=wcs[q][:], in_=wc3[q][:, ccol * 512:(ccol + 1) * 512]),
                    writes=[s_wcs[q]], dma=semw)
            bk.seal(semw, s_wcs)
            for tt in range(NTT):
                o = tt % 2
                for q in range(16):
                    bk.mm(lambda e, q=q, tt=tt, o=o: e.matmul(
                        pp[o][:], ysl[q][:, tt * 128:(tt + 1) * 128], wcs[q][:],
                        start=(q == 0), stop=(q == 15)),
                        reads=[s_ysl[q], s_wcs[q]], psum=s_pp[o],
                        first=(q == 0), last=(q == 15))
                bk.emit("vector", lambda e, tt=tt, ccol=ccol, o=o: e.tensor_add(
                    out=cm.xa[tt][:, ccol * 512:(ccol + 1) * 512],
                    in0=cm.xa[tt][:, ccol * 512:(ccol + 1) * 512], in1=pp[o][:]),
                    reads=[s_pp[o], cm.s_xa[tt]], writes=[cm.s_xa[tt]])
        cm.store_xacc(xo_d, tc)
    cm.finish_x()





def build_full_program():
    nc = bass.Bass("TRN2", target_bir_lowering=False)
    d = lambda name, shape, dt, kind: nc.dram_tensor(name, shape, dt, kind=kind).ap()
    x_in = d("x", [T, C], F32, "ExternalInput")
    xo_d = d("xo", [T, C], F32, "ExternalOutput")
    xA = d("xA", [T, C], F32, "Internal")
    xB = d("xB", [T, C], F32, "Internal")
    zscr_d = d("zscr", [T, C], BF16, "Internal")
    Wf = {"up": d("f_up", [HF, C], F32, "ExternalInput"),
          "dn": d("f_dn", [C, HF], F32, "ExternalInput"),
          "us": d("f_us", [HF], F32, "ExternalInput"),
          "ds": d("f_ds", [C], F32, "ExternalInput"),
          "os": d("f_os", [1], F32, "ExternalInput"),
          "upb": d("f_upb", [HF, C], BF16, "Internal"),
          "dnb": d("f_dnb", [C, HF], BF16, "Internal")}
    Wa0 = {"Wq": d("a0Wq", [C, 2048], F32, "ExternalInput"),
           "Wk": d("a0Wk", [C, 1024], F32, "ExternalInput"),
           "Wv": d("a0Wv", [C, 1024], F32, "ExternalInput"),
           "Wg": d("a0Wg", [C, 8], F32, "ExternalInput"),
           "Wc": d("a0Wc", [2048, C], F32, "ExternalInput"),
           "bc": d("a0bc", [C], F32, "ExternalInput"),
           "qb": d("a0qb", [C, 2048], BF16, "Internal"),
           "kb": d("a0kb", [C, 1024], BF16, "Internal"),
           "vb": d("a0vb", [C, 1024], BF16, "Internal"),
           "qscr": d("a0qscr", [2048, T], BF16, "Internal"),
           "kscr": d("a0kscr", [1024, T], BF16, "Internal"),
           "vscr": d("a0vscr", [T, 1024], BF16, "Internal"),
           "yscr": d("a0yscr", [2048, T], BF16, "Internal"),
           "logg": d("a0logg", [8, T], F32, "Internal"),
           "eta": d("a0eta", [8, T], BF16, "Internal"),
           "rnscr": d("a0rnscr", [2, 512], BF16, "Internal")}
    Wm = {"W1": d("mW1", [7, C, C], F32, "ExternalInput"),
          "Wg": d("mWg", [7, C, C], F32, "ExternalInput"),
          "W2": d("mW2", [7, C, C], F32, "ExternalInput"),
          "Ws1": d("mWs1", [C, C], F32, "ExternalInput"),
          "Wsg": d("mWsg", [C, C], F32, "ExternalInput"),
          "Ws2": d("mWs2", [C, C], F32, "ExternalInput"),
          "gate_w": d("mgw", [C, 7], F32, "ExternalInput"),
          "gate_b": d("mgb", [7], F32, "ExternalInput"),
          "biases": d("mbias", [7], F32, "ExternalInput"),
          "W1b": d("mW1b", [8, C, C], BF16, "Internal"),
          "Wgb": d("mWgb", [8, C, C], BF16, "Internal"),
          "W2b": d("mW2b", [8, C, C], BF16, "Internal")}
    Wa1 = {"Wq": d("a1Wq", [C, C], F32, "ExternalInput"),
           "Wk": d("a1Wk", [C, C], F32, "ExternalInput"),
           "Wv": d("a1Wv", [C, C], F32, "ExternalInput"),
           "Wo": d("a1Wo", [C, C], F32, "ExternalInput"),
           "qb": d("a1qb", [C, C], BF16, "Internal"),
           "kb": d("a1kb", [C, C], BF16, "Internal"),
           "vb": d("a1vb", [C, C], BF16, "Internal"),
           "ob": d("a1ob", [C, C], BF16, "Internal")}

    bk = BK(nc)
    cm = Common(nc, bk)
    arena = Arena(bk, "ar", 112 * 1024)

    s_upb, s_dnb = emit_l0ffn_casts(bk, Wf)
    s_ca1 = emit_l1attn_casts(bk, Wa1)
    s_ca0 = None
    s_cm = None
    bx = frozenset(bk.sems[n] for n in
                   ["f_cup0", "f_cup1", "f_cup2", "f_cup3",
                    "f_cdn0", "f_cdn1", "f_cdn2", "f_cdn3",
                    "a1_cq", "a1_ck", "a1_cv", "a1_co"])

    def cut():
        bk.barrier(exclude=bx)
        arena.reset()

    bk.scope_mark("s0_ffn")
    emit_l0ffn(nc, bk, cm, Wf, zscr_d, x_in, xA, arena, s_upb, s_dnb, app=0)
    cut()
    bk.scope_mark("s1_attn0")
    emit_l0attn(nc, bk, cm, Wa0, zscr_d, xA, xB, arena, s_ca0, bx=bx)
    cut()
    bk.scope_mark("s2_ffn")
    emit_l0ffn(nc, bk, cm, Wf, zscr_d, xB, xA, arena, s_upb, s_dnb, app=1)
    cut()
    bk.scope_mark("s3_moe")
    emit_l1moe(nc, bk, cm, Wm, zscr_d, xA, xB, arena, s_cm, app=0)
    cut()
    bk.scope_mark("s4_attn1")
    emit_l1attn(nc, bk, cm, Wa1, zscr_d, xB, xA, arena, s_ca1)
    cut()
    bk.scope_mark("s5_moe")
    emit_l1moe(nc, bk, cm, Wm, zscr_d, xA, xo_d, arena, s_cm, app=1)
    bk.finalize()
    return nc




_PROGS = {}


def _full_prog():
    if "full" not in _PROGS:
        _PROGS["full"] = build_full_program()
    return _PROGS["full"]


def kernel(L0_attn_hidden_states, L0_attn_Wq, L0_attn_Wk, L0_attn_Wv, L0_attn_Wg,
           L0_attn_Wc, L0_attn_bc, L0_ffn_up_weight, L0_ffn_down_weight,
           L0_ffn_up_scale, L0_ffn_down_scale, L0_ffn_output_scale,
           L1_attn_Wq, L1_attn_Wk, L1_attn_Wv, L1_attn_Wo,
           L1_ffn_W1, L1_ffn_Wg, L1_ffn_W2, L1_ffn_Ws1, L1_ffn_Wsg, L1_ffn_Ws2,
           L1_ffn_gate_w, L1_ffn_gate_b, L1_ffn_biases):
    f32 = lambda a: np.ascontiguousarray(np.asarray(a), dtype=np.float32)
    x = f32(L0_attn_hidden_states)
    wmap = {"f_up": f32(L0_ffn_up_weight), "f_dn": f32(L0_ffn_down_weight),
            "f_us": f32(L0_ffn_up_scale), "f_ds": f32(L0_ffn_down_scale),
            "f_os": f32(L0_ffn_output_scale),
            "a0Wq": f32(L0_attn_Wq), "a0Wk": f32(L0_attn_Wk), "a0Wv": f32(L0_attn_Wv),
            "a0Wg": f32(L0_attn_Wg), "a0Wc": f32(L0_attn_Wc), "a0bc": f32(L0_attn_bc),
            "mW1": f32(L1_ffn_W1), "mWg": f32(L1_ffn_Wg), "mW2": f32(L1_ffn_W2),
            "mWs1": f32(L1_ffn_Ws1), "mWsg": f32(L1_ffn_Wsg), "mWs2": f32(L1_ffn_Ws2),
            "mgw": f32(L1_ffn_gate_w), "mgb": f32(L1_ffn_gate_b),
            "mbias": f32(L1_ffn_biases),
            "a1Wq": f32(L1_attn_Wq), "a1Wk": f32(L1_attn_Wk), "a1Wv": f32(L1_attn_Wv),
            "a1Wo": f32(L1_attn_Wo)}
    in_maps = [{**wmap, "x": np.ascontiguousarray(x[b])} for b in range(8)]
    nc = _full_prog()
    res = run_bass_kernel_spmd(nc, in_maps, core_ids=list(range(8)))
    return np.stack([np.asarray(res.results[b]["xo"]) for b in range(8)], axis=0).astype(np.float32)


# version 06
# speedup vs baseline: 1.1582x; 1.0282x over previous
import math
from contextlib import ExitStack
import numpy as np
import concourse.bass as bass
import concourse.mybir as mybir
from concourse.bass_utils import run_bass_kernel_spmd




F32 = mybir.dt.float32
BF16 = mybir.dt.bfloat16
AF = mybir.ActivationFunctionType
ALU = mybir.AluOpType
AX = mybir.AxisListType

ENGINES = ("tensor", "vector", "scalar", "gpsimd", "sync")
U8 = mybir.dt.uint8


def _dtsize(dt):
    return mybir.dt.size(dt)


class Arena:

    def __init__(self, bk, name, nbytes, addr=None):
        self.bk = bk
        self.name = name
        if addr is None:
            self.base_t = bk.sb(name, [128, nbytes], U8)
            self.addr = bk.nc.lookup_mloc(self.base_t).addr
        else:
            self.addr = addr
        self.nbytes = nbytes
        self.cur = 0
        self.n = 0

    def child(self, name):
        off = (self.cur + 31) // 32 * 32
        return Arena(self.bk, name, self.nbytes - off, addr=self.addr + off)

    def reset(self):
        self.cur = 0

    def alloc(self, name, shape, dtype):
        per = 1
        for d in shape[1:]:
            per *= d
        per *= _dtsize(dtype)
        off = (self.cur + 31) // 32 * 32
        assert off + per <= self.nbytes, (
            f"arena {self.name} overflow at {name}: {off}+{per} > {self.nbytes}")
        self.cur = off + per
        self.n += 1
        t = self.bk.nc.alloc_sbuf_tensor_at(
            f"{self.name}_{self.n}_{name}", shape, dtype, offset=self.addr + off)
        return t


class Slot:
    __slots__ = ("name", "writer", "readers")

    def __init__(self, name=""):
        self.name = name
        self.writer = None
        self.readers = []


class Sem:
    __slots__ = ("name", "handle", "count")

    def __init__(self, name, handle):
        self.name = name
        self.handle = handle
        self.count = 0


class BK:
    def __init__(self, nc):
        self.nc = nc
        self.stack = ExitStack()
        self.streams = {e: [] for e in ENGINES}
        self.watermark = {e: {} for e in ENGINES}
        self.sems = {}
        self.eng_sem = {}
        self.phase = 0
        self._pe_pending_reads = []
        self._pe_pending_writes = []
        self.new_phase()

    def sem(self, name):
        s = self.sems.get(name)
        if s is None:
            h = self.stack.enter_context(self.nc.semaphore(name))
            s = Sem(name, h)
            self.sems[name] = s
        return s

    def new_phase(self):
        self.phase += 1
        for e in ENGINES:
            self.eng_sem[e] = self.sem(f"ph{self.phase}_{e}")

    def sb(self, name, shape, dtype):
        t = self.stack.enter_context(self.nc.sbuf_tensor(name, shape, dtype))
        return t

    def ps(self, name, shape, dtype=F32):
        t = self.stack.enter_context(self.nc.psum_tensor(name, shape, dtype))
        return t

    def _collect(self, engine, reads, writes):
        need = {}

        def add(dep):
            if dep is None:
                return
            s, c, peng = dep
            if engine == "tensor" and peng == "tensor":
                return
            if need.get(s, 0) < c:
                need[s] = c

        for sl in reads:
            add(sl.writer)
        for sl in writes:
            add(sl.writer)
            for r in sl.readers:
                add(r)
        wm = self.watermark[engine]
        waits = []
        for s, c in need.items():
            if wm.get(s, 0) < c:
                waits.append((s, c))
                wm[s] = c
        return waits

    def emit(self, engine, fn, reads=(), writes=(), dma=None, tag=""):
        waits = self._collect(engine, reads, writes)
        if dma is not None:
            msem, inc = dma, 16
        else:
            msem, inc = self.eng_sem[engine], 1
        msem.count += inc
        marker = (msem, msem.count, engine)

        def op(eng, fn=fn, waits=tuple(waits), msem=msem, inc=inc, tag=tag):
            for s, c in waits:
                eng.wait_ge(s.handle, c)
            inst = fn(eng)
            inst.then_inc(msem.handle, inc)

        self.streams[engine].append(op)
        for sl in writes:
            sl.writer = marker
            sl.readers = []
        for sl in reads:
            sl.readers.append(marker)
        return marker

    def mm(self, fn, reads=(), psum=None, first=True, last=True, tag=""):
        eff_writes = (psum,) if (first and psum is not None) else ()
        waits = self._collect("tensor", reads, eff_writes)
        msem = self.eng_sem["tensor"]
        msem.count += 1
        marker = (msem, msem.count, "tensor")

        def op(eng, fn=fn, waits=tuple(waits), msem=msem, tag=tag):
            for s, c in waits:
                eng.wait_ge(s.handle, c)
            inst = fn(eng)
            inst.then_inc(msem.handle, 1)

        self.streams["tensor"].append(op)
        for sl in reads:
            sl.readers.append(marker)
        if psum is not None:
            if first:
                psum.readers = []
            if last:
                psum.writer = marker
        return marker

    def barrier(self, exclude=()):
        pairs = [(s, s.count) for s in self.sems.values()
                 if s.count > 0 and s not in exclude]
        bsem = self.eng_sem["sync"]
        bsem.count += 1
        bcount = bsem.count

        def op_sync(eng, pairs=tuple(pairs), bsem=bsem):
            for s, c in pairs:
                eng.wait_ge(s.handle, c)
            eng.nop().then_inc(bsem.handle, 1)

        self.streams["sync"].append(op_sync)
        wm = self.watermark["sync"]
        for s, c in pairs:
            wm[s] = max(wm.get(s, 0), c)
        for e in ENGINES:
            if e == "sync":
                continue

            def op(eng, bsem=bsem, bcount=bcount):
                eng.wait_ge(bsem.handle, bcount)

            self.streams[e].append(op)
            wme = self.watermark[e]
            wme[bsem] = max(wme.get(bsem, 0), bcount)
            for s, c in pairs:
                wme[s] = max(wme.get(s, 0), c)

    def seal(self, sem, slots):
        for sl in slots:
            if sl.writer is not None and sl.writer[0] is sem:
                sl.writer = (sem, sem.count, sl.writer[2])
            sl.readers = [(s, sem.count if s is sem else c, e2)
                          for (s, c, e2) in sl.readers]

    def wait_all(self, engine, sems):
        pairs = [(s, s.count) for s in sems]

        def op(eng, pairs=tuple(pairs)):
            for s, c in pairs:
                eng.wait_ge(s.handle, c)

        self.streams[engine].append(op)

    def barrier_wait(self, engine, slots):
        waits = self._collect(engine, slots, ())

        def op(eng, waits=tuple(waits)):
            for s, c in waits:
                eng.wait_ge(s.handle, c)

        self.streams[engine].append(op)

    def scope_mark(self, name):
        for e in ENGINES:
            self.streams[e].append(name)

    def finalize(self):
        nc = self.nc

        def replay(eng, ops):
            cur = None
            for op in ops:
                if isinstance(op, str):
                    if cur is not None:
                        nc.leave_named_scope(cur[0], cur[1], False)
                    cur = (op, nc.enter_named_scope(op, False)[0])
                else:
                    op(eng)
            if cur is not None:
                nc.leave_named_scope(cur[0], cur[1], False)

        with nc.Block() as blk:
            for ename in ENGINES:
                ops = self.streams[ename]
                getattr(blk, ename)(lambda eng, ops=ops: replay(eng, ops))
        self.stack.close()
        return nc



T = 2048
C = 2048
NT = 16
NCH = 16
HF = 8192
LNEPS = 1e-5
TCW = 1024
NTCOL = T // TCW


def configure(t=2048, c=2048, hf=8192, tcw=1024):
    global T, C, NT, NCH, HF, TCW, NTCOL
    T, C, HF, TCW = t, c, hf, tcw
    NT, NCH, NTCOL = T // 128, C // 128, T // TCW


def S(n=""):
    return Slot(n)


class Common:

    def __init__(self, nc, bk):
        self.nc, self.bk = nc, bk
        sb = bk.sb
        self.eps = sb("eps", [128, 1], F32)
        self.s_eps = S()
        bk.emit("vector", lambda e: e.memset(self.eps[:], LNEPS), writes=[self.s_eps])
        self.onesr = sb("onesr", [1, 128], BF16)
        self.s_onesr = S()
        bk.emit("vector", lambda e: e.memset(self.onesr[:], 1.0), writes=[self.s_onesr])
        self.onesrf = sb("onesrf", [1, 128], F32)
        self.s_onesrf = S()
        bk.emit("vector", lambda e: e.memset(self.onesrf[:], 1.0), writes=[self.s_onesrf])
        self.onecol = sb("onecol", [128, 1], BF16)
        self.s_onecol = S()
        bk.emit("vector", lambda e: e.memset(self.onecol[:], 1.0), writes=[self.s_onecol])
        self.identf = sb("identf", [128, 128], F32)
        self.s_identf = S()
        bk.emit("vector", lambda e: e.memset(self.identf[:], 1.0), writes=[self.s_identf])
        bk.emit("gpsimd", lambda e: e.affine_select(
            out=self.identf[:], in_=self.identf[:], pattern=[[1, 128]],
            compare_op=ALU.is_equal, fill=0.0, base=0, channel_multiplier=-1),
            reads=[self.s_identf], writes=[self.s_identf])
        self.lx = [sb(f"lnx{j}", [128, C], F32) for j in range(2)]
        self.s_lx = [S() for _ in range(2)]
        self.zt = [sb(f"lnz{j}", [128, C], BF16) for j in range(2)]
        self.s_zt = [S() for _ in range(2)]
        self.stat = [[sb(f"lnst{j}_{k}", [128, 1], F32) for k in range(5)] for j in range(2)]
        self.s_stat = [[S() for _ in range(5)] for _ in range(2)]
        self.d_lx = [bk.sem(f"c_lx{j}") for j in range(2)]
        self.d_zs = [bk.sem(f"c_zs{j}") for j in range(2)]
        self.xa = [sb(f"xa{i}", [128, C], F32) for i in range(TCW // 128)]
        self.s_xa = [S() for _ in range(TCW // 128)]
        self.d_xa = [bk.sem(f"c_xa{j}") for j in range(2)]
        self.d_xo = [bk.sem(f"c_xo{j}") for j in range(2)]
        self.pp = [bk.ps(f"cpp{i}", [128, 512]) for i in range(8)]
        self.s_pp = [S() for _ in range(8)]

    def col_table(self, name, src_1d, n, dsem, arena):
        bk = self.bk
        assert n <= 128
        alloc = (lambda nm, sh, dt: bk.sb(nm, sh, dt)) if arena is None else arena.alloc
        row = alloc(name + "_r", [n, 128], F32)
        out = alloc(name, [128, n], F32)
        pst = self.pp[7][:, 0:n]
        s_row, s_out, s_ps = S(), S(), self.s_pp[7]
        bk.emit("sync", lambda e: e.dma_start(
            out=row[:], in_=src_1d.rearrange("(a b) -> a b", b=128)),
            writes=[s_row], dma=dsem)
        bk.seal(dsem, [s_row])
        bk.mm(lambda e: e.transpose(pst, row[:], self.identf[0:n, 0:n]),
              reads=[s_row, self.s_identf], psum=s_ps)
        bk.emit("vector", lambda e: e.tensor_copy(out=out[:], in_=pst),
                reads=[s_ps], writes=[s_out])
        return out, s_out

    def ln_pass(self, x_d, zscr_d, s_zscr):
        bk = self.bk
        for i in range(NT):
            j = i % 2
            st, s_st = self.stat[j], self.s_stat[j]
            x, s_x = self.lx[j], self.s_lx[j]
            bk.emit("sync", lambda e, i=i, x=x: e.dma_start(
                out=x[:], in_=x_d[i * 128:(i + 1) * 128, :]),
                writes=[s_x], dma=self.d_lx[j])
            bk.emit("vector", lambda e, x=x, st=st: e.reduce_sum(out=st[0][:], in_=x[:], axis=AX.X),
                    reads=[s_x], writes=[s_st[0]])
            bk.emit("scalar", lambda e, x=x, st=st, j=j: e.activation(
                out=self.zt[j][:], in_=x[:], func=AF.Square, accum_out=st[1][:]),
                reads=[s_x], writes=[self.s_zt[j], s_st[1]])
            bk.emit("vector", lambda e, st=st: e.tensor_scalar(out=st[2][:], in0=st[0][:],
                    scalar1=-1.0 / C, scalar2=None, op0=ALU.mult),
                    reads=[s_st[0]], writes=[s_st[2]])
            bk.emit("vector", lambda e, st=st: e.tensor_tensor(out=st[3][:], in0=st[2][:],
                    in1=st[2][:], op=ALU.mult), reads=[s_st[2]], writes=[s_st[3]])
            bk.emit("vector", lambda e, st=st: e.scalar_tensor_tensor(out=st[3][:], in0=st[1][:],
                    scalar=1.0 / C, in1=st[3][:], op0=ALU.mult, op1=ALU.subtract),
                    reads=[s_st[1], s_st[3]], writes=[s_st[3]])
            bk.emit("scalar", lambda e, st=st: e.activation(out=st[3][:], in_=st[3][:],
                    func=AF.Sqrt, bias=self.eps[:, 0:1]),
                    reads=[s_st[3], self.s_eps], writes=[s_st[3]])
            bk.emit("vector", lambda e, st=st: e.reciprocal(out=st[4][:], in_=st[3][:]),
                    reads=[s_st[3]], writes=[s_st[4]])
            bk.emit("vector", lambda e, x=x, st=st, j=j: e.tensor_scalar(out=self.zt[j][:],
                    in0=x[:], scalar1=st[2][:, 0:1], scalar2=st[4][:, 0:1],
                    op0=ALU.add, op1=ALU.mult),
                    reads=[s_x, s_st[2], s_st[4]], writes=[self.s_zt[j]])
            bk.emit("scalar", lambda e, i=i, j=j: e.dma_start(
                out=zscr_d[i * 128:(i + 1) * 128, :], in_=self.zt[j][:]),
                reads=[self.s_zt[j]], writes=[s_zscr[j]], dma=self.d_zs[j])
        for j in range(2):
            bk.seal(self.d_zs[j], [s_zscr[j]])

    def load_xacc(self, x_d, tc):
        bk = self.bk
        sem = self.d_xa[tc % 2]
        for k in range(TCW // 128):
            i = tc * (TCW // 128) + k
            bk.emit("sync", lambda e, i=i, k=k: e.dma_start(
                out=self.xa[k][:], in_=x_d[i * 128:(i + 1) * 128, :]),
                writes=[self.s_xa[k]], dma=sem)
        bk.seal(sem, self.s_xa)

    def store_xacc(self, xo_d, tc):
        bk = self.bk
        for k in range(TCW // 128):
            i = tc * (TCW // 128) + k
            bk.emit("sync", lambda e, i=i, k=k: e.dma_start(
                out=xo_d[i * 128:(i + 1) * 128, :], in_=self.xa[k][:]),
                reads=[self.s_xa[k]], dma=self.d_xo[tc % 2])
        bk.seal(self.d_xo[tc % 2], self.s_xa)

    def finish_x(self):
        self.bk.wait_all("sync", self.d_xo)



def emit_l0ffn_prep(bk, cm, W):
    s_cup = [S() for _ in range(4)]
    s_cdn = [S() for _ in range(4)]
    s_upbs = [S() for _ in range(4)]
    s_dnbs = [S() for _ in range(4)]
    HQ = HF // 4
    for p in range(4):
        bk.emit("gpsimd", lambda e, p=p: e.dma_start(
            out=W["upb"][p * HQ:(p + 1) * HQ, :], in_=W["up"][p * HQ:(p + 1) * HQ, :]),
            writes=[s_cup[p]], dma=bk.sem(f"f_cup{p}"))
        bk.emit("gpsimd", lambda e, p=p: e.dma_start(
            out=W["dnb"][:, p * HQ:(p + 1) * HQ], in_=W["dn"][:, p * HQ:(p + 1) * HQ]),
            writes=[s_cdn[p]], dma=bk.sem(f"f_cdn{p}"))
    dscol = bk.sb("f_dscol", [128, NCH], F32)
    osct = bk.sb("f_osct", [128, 1], F32)
    s_dscol, s_osct = S(), S()
    d_os = bk.sem("f_dos")
    bk.emit("sync", lambda e: e.dma_start(out=osct[:],
            in_=W["os"].rearrange("(b a) -> b a", b=1).partition_broadcast(128)),
            writes=[s_osct], dma=d_os)
    bk.seal(d_os, [s_osct])
    dsc0, s_dsc0 = cm.col_table("f_dsc", W["ds"], NCH, bk.sem("f_dsc"), None)
    bk.emit("vector", lambda e: e.tensor_scalar(out=dscol[:], in0=dsc0[:],
            scalar1=osct[:, 0:1], scalar2=0.5, op0=ALU.mult, op1=ALU.mult),
            reads=[s_dsc0, s_osct], writes=[s_dscol])
    d_su = [bk.sem(f"f_su{p}") for p in range(4)]
    d_sd = [bk.sem(f"f_sd{p}") for p in range(4)]
    d_ld = bk.sem("f_pld")
    for p in range(4):
        for t in range(HQ // 128):
            j = t % 2
            stg = cm.xa[j][:].bitcast(BF16)[:, 0:C]
            sstg = cm.s_xa[j]
            r0 = p * HQ + t * 128
            bk.emit("sync", lambda e, r0=r0, stg=stg: e.dma_start(
                out=stg, in_=W["upb"][r0:r0 + 128, :]),
                reads=[s_cup[p]], writes=[sstg], dma=d_ld)
            bk.seal(d_ld, [sstg])
            bk.emit("scalar", lambda e, stg=stg: e.sign(out=stg, in_=stg),
                    reads=[sstg], writes=[sstg])
            bk.emit("sync", lambda e, r0=r0, stg=stg: e.dma_start(
                out=W["upbs"][r0:r0 + 128, :], in_=stg),
                reads=[sstg], writes=[s_upbs[p]], dma=d_su[p])
        bk.seal(d_su[p], [s_upbs[p]])
        for t in range(C // 128):
            j = 2 + t % 2
            stg = cm.xa[j][:].bitcast(BF16)[:, 0:HQ]
            sstg = cm.s_xa[j]
            r0 = t * 128
            bk.emit("sync", lambda e, r0=r0, p=p, stg=stg: e.dma_start(
                out=stg, in_=W["dnb"][r0:r0 + 128, p * HQ:(p + 1) * HQ]),
                reads=[s_cdn[p]], writes=[sstg], dma=d_ld)
            bk.seal(d_ld, [sstg])
            bk.emit("scalar", lambda e, stg=stg: e.sign(out=stg, in_=stg),
                    reads=[sstg], writes=[sstg])
            bk.emit("vector", lambda e, t=t, stg=stg: e.tensor_scalar(
                    out=stg, in0=stg, scalar1=dscol[:, t:t + 1], scalar2=None,
                    op0=ALU.mult), reads=[sstg, s_dscol], writes=[sstg])
            bk.emit("sync", lambda e, r0=r0, p=p, stg=stg: e.dma_start(
                out=W["dnbs"][r0:r0 + 128, p * HQ:(p + 1) * HQ], in_=stg),
                reads=[sstg], writes=[s_dnbs[p]], dma=d_sd[p])
        bk.seal(d_sd[p], [s_dnbs[p]])
    return s_upbs, s_dnbs


def emit_l0ffn(nc, bk, cm, W, zscr_d, x_d, xo_d, arena, s_upb, s_dnb, app=0):
    sb, ps = bk.sb, bk.ps
    s_zscr = [S("zscr0"), S("zscr1")]

    cm.ln_pass(x_d, zscr_d, s_zscr)

    usc, s_usc = cm.col_table("usc" + str(app), W["us"], HF // 128, bk.sem("f_usc"), arena)

    upT = [arena.alloc(f"fupT{c}", [128, 512], BF16) for c in range(NCH)]
    s_upT = [S() for _ in range(NCH)]
    dnT = [arena.alloc(f"fdnT{h}", [128, C], BF16) for h in range(4)]
    s_dnT = [S() for _ in range(4)]
    zT = [arena.alloc(f"fzT{c}", [128, TCW], BF16) for c in range(NCH)]
    s_zT = [S() for _ in range(NCH)]
    hidT = [[arena.alloc(f"fhid{p}_{h}", [128, TCW], BF16) for h in range(4)] for p in range(2)]
    s_hid = [[S() for _ in range(4)] for _ in range(2)]
    psh = cm.pp[0:4]
    s_psh = cm.s_pp[0:4]
    pso = cm.pp[4:6]
    s_pso = cm.s_pp[4:6]
    d_up = [bk.sem(f"f_up{j}") for j in range(2)]
    d_dn = [bk.sem(f"f_dn{j}") for j in range(2)]
    d_zt = bk.sem("f_zt")

    NHP = HF // 512
    for tc in range(NTCOL):
        cm.load_xacc(x_d, tc)
        for c in range(NCH):
            bk.emit("scalar", lambda e, c=c, tc=tc: e.dma_start_transpose(
                out=zT[c][:], in_=zscr_d[tc * TCW:(tc + 1) * TCW, c * 128:(c + 1) * 128]),
                reads=s_zscr, writes=[s_zT[c]], dma=d_zt)
        bk.seal(d_zt, s_zT)
        for hp in range(NHP):
            par = hp % 2
            sem = d_up[par]
            HQ = HF // 4
            cps = [s_upb[p] for p in range((hp * 512) // HQ,
                                           (hp * 512 + 511) // HQ + 1)]
            for c in range(NCH):
                bk.emit("scalar", lambda e, c=c, hp=hp: e.dma_start_transpose(
                    out=upT[c][:], in_=W["upbs"][hp * 512:(hp + 1) * 512, c * 128:(c + 1) * 128]),
                    reads=cps, writes=[s_upT[c]], dma=sem)
            bk.seal(sem, s_upT)
            sem = d_dn[par]
            for hc in range(4):
                col = hp * 4 + hc
                cp = s_dnb[(col * 128) // (HF // 4)]
                bk.emit("scalar", lambda e, hc=hc, col=col: e.dma_start_transpose(
                    out=dnT[hc][:], in_=W["dnbs"][:, col * 128:(col + 1) * 128]),
                    reads=[cp], writes=[s_dnT[hc]], dma=sem)
            bk.seal(sem, s_dnT)
            for hb in range(4):
                for th in range(TCW // 512):
                    pp = (hb * (TCW // 512) + th) % 4
                    for c in range(NCH):
                        bk.mm(lambda e, hb=hb, th=th, c=c, pp=pp: e.matmul(
                            psh[pp][:], upT[c][:, hb * 128:(hb + 1) * 128],
                            zT[c][:, th * 512:(th + 1) * 512],
                            start=(c == 0), stop=(c == NCH - 1)),
                            reads=[s_upT[c], s_zT[c]], psum=s_psh[pp],
                            first=(c == 0), last=(c == NCH - 1))
                    hcol = hp * 4 + hb
                    bk.emit("scalar", lambda e, hb=hb, th=th, par=par, hcol=hcol, pp=pp:
                            e.activation(out=hidT[par][hb][:, th * 512:(th + 1) * 512],
                            in_=psh[pp][:], func=AF.Relu, scale=usc[:, hcol:hcol + 1]),
                            reads=[s_psh[pp], s_usc], writes=[s_hid[par][hb]])
            for tt in range(TCW // 128):
                for ccol in range(C // 512):
                    o = (tt * 4 + ccol) % 2
                    for hc in range(4):
                        bk.mm(lambda e, hc=hc, tt=tt, ccol=ccol, par=par, o=o: e.matmul(
                            pso[o][:], hidT[par][hc][:, tt * 128:(tt + 1) * 128],
                            dnT[hc][:, ccol * 512:(ccol + 1) * 512],
                            start=(hc == 0), stop=(hc == 3)),
                            reads=[s_hid[par][hc], s_dnT[hc]], psum=s_pso[o],
                            first=(hc == 0), last=(hc == 3))
                    bk.emit("vector", lambda e, tt=tt, ccol=ccol, o=o: e.tensor_add(
                        out=cm.xa[tt][:, ccol * 512:(ccol + 1) * 512],
                        in0=cm.xa[tt][:, ccol * 512:(ccol + 1) * 512], in1=pso[o][:]),
                        reads=[s_pso[o], cm.s_xa[tt]], writes=[cm.s_xa[tt]])
        cm.store_xacc(xo_d, tc)
    cm.finish_x()





def S(n=""):
    return Slot(n)


def emit_l1moe_casts(bk, W):
    NE = 8
    s_cast = [S(f"cast{e}") for e in range(NE)]
    for e in range(NE):
        sem = bk.sem(f"m_cast{e}")
        srcs = ([W["W1"][e], W["Wg"][e], W["W2"][e]] if e < 7
                else [W["Ws1"], W["Wsg"], W["Ws2"]])
        for k, nm in enumerate(("W1b", "Wgb", "W2b")):
            bk.emit("gpsimd", lambda e2, e=e, k=k, nm=nm, srcs=srcs: e2.dma_start(
                out=W[nm][e], in_=srcs[k]), writes=[s_cast[e]], dma=sem)
        bk.seal(sem, [s_cast[e]])
    return s_cast


def emit_l1moe(nc, bk, cm, W, zscr_d, x_d, xo_d, arena, s_cast, app=0):
    sb, ps = bk.sb, bk.ps
    s_zscr = [S("mz0"), S("mz1")]
    NE = 8
    NTT = TCW // 128

    cm.ln_pass(x_d, zscr_d, s_zscr)

    gw = [arena.alloc(f"m_gw{c}", [128, 7], F32) for c in range(NCH)]
    gwb = [arena.alloc(f"m_gwb{c}", [128, 7], BF16) for c in range(NCH)]
    s_gw = [S() for _ in range(NCH)]
    s_gwb = [S() for _ in range(NCH)]
    d_gw = bk.sem("m_gw")
    gw3 = W["gate_w"].rearrange("(c p) g -> c p g", p=128)
    for c in range(NCH):
        bk.emit("sync", lambda e, c=c: e.dma_start(out=gw[c][:], in_=gw3[c]),
                writes=[s_gw[c]], dma=d_gw)
    bk.seal(d_gw, s_gw)
    for c in range(NCH):
        bk.emit("vector", lambda e, c=c: e.tensor_copy(out=gwb[c][:], in_=gw[c][:]),
                reads=[s_gw[c]], writes=[s_gwb[c]])
    gbrow = arena.alloc("m_gbrow", [1, 7], F32)
    birow = arena.alloc("m_birow", [1, 7], F32)
    gbb = arena.alloc("m_gbb", [128, 7], F32)
    bib = arena.alloc("m_bib", [128, 7], F32)
    ps_g = cm.pp[6][:, 0:8]
    s_gbrow, s_birow, s_gbb, s_bib, s_psg = S(), S(), S(), S(), cm.s_pp[6]
    d_gb = bk.sem("m_gb")
    bk.emit("sync", lambda e: e.dma_start(out=gbrow[:],
            in_=W["gate_b"].rearrange("(b a) -> b a", b=1)), writes=[s_gbrow], dma=d_gb)
    bk.emit("sync", lambda e: e.dma_start(out=birow[:],
            in_=W["biases"].rearrange("(b a) -> b a", b=1)), writes=[s_birow], dma=d_gb)
    bk.seal(d_gb, [s_gbrow, s_birow])
    bk.mm(lambda e: e.matmul(ps_g[:, 0:7], cm.onesrf[:], gbrow[:], start=True, stop=True),
          reads=[cm.s_onesrf, s_gbrow], psum=s_psg)
    bk.emit("vector", lambda e: e.tensor_copy(out=gbb[:], in_=ps_g[:, 0:7]),
            reads=[s_psg], writes=[s_gbb])
    bk.mm(lambda e: e.matmul(ps_g[:, 0:7], cm.onesrf[:], birow[:], start=True, stop=True),
          reads=[cm.s_onesrf, s_birow], psum=s_psg)
    bk.emit("vector", lambda e: e.tensor_copy(out=bib[:], in_=ps_g[:, 0:7]),
            reads=[s_psg], writes=[s_bib])

    zT = [arena.alloc(f"m_zT{c}", [128, TCW], BF16) for c in range(NCH)]
    s_zT = [S() for _ in range(NCH)]
    slab1a = arena.alloc("m_sl1", [128, NCH * 512], BF16)
    slabga = arena.alloc("m_slg", [128, NCH * 512], BF16)
    slab2a = arena.alloc("m_sl2", [128, 4 * C], BF16)
    slab1 = [slab1a[:, c * 512:(c + 1) * 512] for c in range(NCH)]
    slabg = [slabga[:, c * 512:(c + 1) * 512] for c in range(NCH)]
    slab2 = [slab2a[:, h * C:(h + 1) * C] for h in range(4)]
    s_slab1a, s_slabga, s_slab2a = S(), S(), S()
    s_slab1 = [s_slab1a] * NCH
    s_slabg = [s_slabga] * NCH
    s_slab2 = [s_slab2a] * 4
    s1t = [arena.alloc(f"m_s1t{h}", [128, TCW], BF16) for h in range(4)]
    hh = [arena.alloc(f"m_hh{h}", [128, TCW], BF16) for h in range(4)]
    s_s1t = [S() for _ in range(4)]
    s_hh = [S() for _ in range(4)]
    wcol = [arena.alloc(f"m_w{t}", [128, 8], F32) for t in range(NTT)]
    s_wcol = [S() for _ in range(NTT)]
    glog = arena.alloc("m_glog", [128, 8], F32)
    gtmp = arena.alloc("m_gtmp", [128, 8], F32)
    gsel = arena.alloc("m_gsel", [128, 8], F32)
    gt2 = arena.alloc("m_gt2", [128, 8], F32)
    gcol = [arena.alloc(f"m_gc{k}", [128, 1], F32) for k in range(3)]
    s_glog, s_gtmp, s_gsel, s_gt2 = S(), S(), S(), S()
    s_gcol = [S() for _ in range(3)]
    psh = cm.pp[0:4]
    s_psh = cm.s_pp[0:4]
    pso = cm.pp[4:6]
    s_pso = cm.s_pp[4:6]
    d_zt = bk.sem("m_zt")
    d_s1 = [bk.sem(f"m_dsl1{j}") for j in range(2)]
    d_sg = [bk.sem(f"m_dslg{j}") for j in range(2)]
    d_s2 = [bk.sem(f"m_dsl2{j}") for j in range(2)]
    NTH = TCW // 512

    def src1(nm, e_i):
        w = W[nm][e_i] if e_i < 7 else W["Ws" + nm[1]]
        return w.rearrange("(c p) h -> p c h", p=128)

    def src2(e_i):
        w = W["W2"][e_i] if e_i < 7 else W["Ws2"]
        return w.rearrange("(h p) c -> p h c", p=128)

    for tc in range(NTCOL):
        cm.load_xacc(x_d, tc)
        for c in range(NCH):
            bk.emit("scalar", lambda e, c=c, tc=tc: e.dma_start_transpose(
                out=zT[c][:], in_=zscr_d[tc * TCW:(tc + 1) * TCW, c * 128:(c + 1) * 128]),
                reads=s_zscr, writes=[s_zT[c]], dma=d_zt)
        bk.seal(d_zt, s_zT)
        bk.scope_mark(f"m_gate_tc{tc}_{app}")
        for tt in range(NTT):
            for c in range(NCH):
                bk.mm(lambda e, tt=tt, c=c: e.matmul(
                    ps_g[:, 0:7], zT[c][:, tt * 128:(tt + 1) * 128], gwb[c][:],
                    start=(c == 0), stop=(c == NCH - 1)),
                    reads=[s_zT[c], s_gwb[c]], psum=s_psg,
                    first=(c == 0), last=(c == NCH - 1))
            bk.emit("vector", lambda e: e.tensor_tensor(out=glog[:, 0:7], in0=ps_g[:, 0:7],
                    in1=gbb[:, 0:7], op=ALU.add), reads=[s_psg, s_gbb], writes=[s_glog])
            bk.emit("vector", lambda e: e.reduce_max(out=gcol[0][:], in_=glog[:, 0:7],
                    axis=AX.X, negate=True), reads=[s_glog], writes=[s_gcol[0]])
            bk.emit("scalar", lambda e: e.activation(out=glog[:, 0:7], in_=glog[:, 0:7],
                    func=AF.Exp, bias=gcol[0][:, 0:1]),
                    reads=[s_glog, s_gcol[0]], writes=[s_glog])
            bk.emit("vector", lambda e: e.reduce_sum(out=gcol[1][:], in_=glog[:, 0:7],
                    axis=AX.X), reads=[s_glog], writes=[s_gcol[1]])
            bk.emit("vector", lambda e: e.reciprocal(out=gcol[2][:], in_=gcol[1][:]),
                    reads=[s_gcol[1]], writes=[s_gcol[2]])
            bk.emit("vector", lambda e: e.tensor_scalar(out=glog[:, 0:7], in0=glog[:, 0:7],
                    scalar1=gcol[2][:, 0:1], scalar2=None, op0=ALU.mult),
                    reads=[s_glog, s_gcol[2]], writes=[s_glog])
            bk.emit("vector", lambda e: e.tensor_tensor(out=gtmp[:, 0:7], in0=glog[:, 0:7],
                    in1=bib[:, 0:7], op=ALU.add), reads=[s_glog, s_bib], writes=[s_gtmp])
            bk.emit("vector", lambda e: e.reduce_max(out=gcol[0][:], in_=gtmp[:, 0:7],
                    axis=AX.X), reads=[s_gtmp], writes=[s_gcol[0]])
            bk.emit("vector", lambda e: e.tensor_scalar(out=gsel[:, 0:7], in0=gtmp[:, 0:7],
                    scalar1=gcol[0][:, 0:1], scalar2=None, op0=ALU.is_ge),
                    reads=[s_gtmp, s_gcol[0]], writes=[s_gsel])
            bk.emit("vector", lambda e: e.scalar_tensor_tensor(out=gt2[:, 0:7],
                    in0=gsel[:, 0:7], scalar=-1e9, in1=gtmp[:, 0:7],
                    op0=ALU.mult, op1=ALU.add), reads=[s_gsel, s_gtmp], writes=[s_gt2])
            bk.emit("vector", lambda e: e.reduce_max(out=gcol[1][:], in_=gt2[:, 0:7],
                    axis=AX.X), reads=[s_gt2], writes=[s_gcol[1]])
            bk.emit("vector", lambda e: e.tensor_scalar(out=gsel[:, 0:7], in0=gt2[:, 0:7],
                    scalar1=gcol[1][:, 0:1], scalar2=None, op0=ALU.is_ge),
                    reads=[s_gt2, s_gcol[1]], writes=[s_gsel])
            bk.emit("vector", lambda e: e.scalar_tensor_tensor(out=gt2[:, 0:7],
                    in0=gsel[:, 0:7], scalar=-1e9, in1=gt2[:, 0:7],
                    op0=ALU.mult, op1=ALU.add), reads=[s_gsel, s_gt2], writes=[s_gt2])
            bk.emit("vector", lambda e: e.reduce_max(out=gcol[2][:], in_=gt2[:, 0:7],
                    axis=AX.X), reads=[s_gt2], writes=[s_gcol[2]])
            bk.emit("vector", lambda e: e.tensor_scalar(out=gsel[:, 0:7], in0=gtmp[:, 0:7],
                    scalar1=gcol[2][:, 0:1], scalar2=None, op0=ALU.is_ge),
                    reads=[s_gtmp, s_gcol[2]], writes=[s_gsel])
            bk.emit("vector", lambda e, tt=tt: e.tensor_tensor(out=wcol[tt][:, 0:7],
                    in0=glog[:, 0:7], in1=gsel[:, 0:7], op=ALU.mult),
                    reads=[s_glog, s_gsel], writes=[s_wcol[tt]])
            bk.emit("vector", lambda e, tt=tt: e.reduce_sum(out=gcol[0][:],
                    in_=wcol[tt][:, 0:7], axis=AX.X), reads=[s_wcol[tt]], writes=[s_gcol[0]])
            bk.emit("vector", lambda e: e.reciprocal(out=gcol[1][:], in_=gcol[0][:]),
                    reads=[s_gcol[0]], writes=[s_gcol[1]])
            bk.emit("vector", lambda e, tt=tt: e.tensor_scalar(out=wcol[tt][:, 0:7],
                    in0=wcol[tt][:, 0:7], scalar1=gcol[1][:, 0:1], scalar2=0.5,
                    op0=ALU.mult, op1=ALU.mult), reads=[s_wcol[tt], s_gcol[1]],
                    writes=[s_wcol[tt]])
        bk.scope_mark(f"m_exp_tc{tc}_{app}")
        for e_i in range(NE):
            for hp in range(C // 512):
                par = hp % 2
                sem1 = d_s1[par]
                bk.emit("gpsimd", lambda e, e_i=e_i, hp=hp: e.dma_start(
                    out=slab1a.rearrange("p (c h) -> p c h", h=512),
                    in_=src1("W1", e_i)[:, :, hp * 512:(hp + 1) * 512]),
                    writes=[s_slab1a], dma=sem1)
                bk.seal(sem1, [s_slab1a])
                semg = d_sg[par]
                bk.emit("gpsimd", lambda e, e_i=e_i, hp=hp: e.dma_start(
                    out=slabga.rearrange("p (c h) -> p c h", h=512),
                    in_=src1("Wg", e_i)[:, :, hp * 512:(hp + 1) * 512]),
                    writes=[s_slabga], dma=semg)
                bk.seal(semg, [s_slabga])
                sem2 = d_s2[par]
                bk.emit("gpsimd", lambda e, e_i=e_i, hp=hp: e.dma_start(
                    out=slab2a.rearrange("p (h c) -> p h c", c=C),
                    in_=src2(e_i)[:, hp * 4:(hp + 1) * 4, :]),
                    writes=[s_slab2a], dma=sem2)
                bk.seal(sem2, [s_slab2a])
                for hb in range(4):
                    for th in range(NTH):
                        pp = (hb * NTH + th) % 2
                        for c in range(NCH):
                            bk.mm(lambda e, hb=hb, th=th, c=c, pp=pp: e.matmul(
                                psh[pp][:], slab1[c][:, hb * 128:(hb + 1) * 128],
                                zT[c][:, th * 512:(th + 1) * 512],
                                start=(c == 0), stop=(c == NCH - 1)),
                                reads=[s_slab1[c], s_zT[c]], psum=s_psh[pp],
                                first=(c == 0), last=(c == NCH - 1))
                        bk.emit("scalar", lambda e, hb=hb, th=th, pp=pp: e.activation(
                            out=s1t[hb][:, th * 512:(th + 1) * 512], in_=psh[pp][:],
                            func=AF.Silu), reads=[s_psh[pp]], writes=[s_s1t[hb]])
                        pp2 = 2 + (hb * NTH + th) % 2
                        for c in range(NCH):
                            bk.mm(lambda e, hb=hb, th=th, c=c, pp2=pp2: e.matmul(
                                psh[pp2][:], slabg[c][:, hb * 128:(hb + 1) * 128],
                                zT[c][:, th * 512:(th + 1) * 512],
                                start=(c == 0), stop=(c == NCH - 1)),
                                reads=[s_slabg[c], s_zT[c]], psum=s_psh[pp2],
                                first=(c == 0), last=(c == NCH - 1))
                        bk.emit("vector", lambda e, hb=hb, th=th, pp2=pp2: e.tensor_tensor(
                            out=hh[hb][:, th * 512:(th + 1) * 512], in0=psh[pp2][:],
                            in1=s1t[hb][:, th * 512:(th + 1) * 512], op=ALU.mult),
                            reads=[s_psh[pp2], s_s1t[hb]], writes=[s_hh[hb]])
                for tt in range(NTT):
                    gt = tc * NTT + tt
                    for ccol in range(C // 512):
                        o = (tt * (C // 512) + ccol) % 2
                        for hc in range(4):
                            bk.mm(lambda e, hc=hc, tt=tt, ccol=ccol, o=o: e.matmul(
                                pso[o][:], hh[hc][:, tt * 128:(tt + 1) * 128],
                                slab2[hc][:, ccol * 512:(ccol + 1) * 512],
                                start=(hc == 0), stop=(hc == 3)),
                                reads=[s_hh[hc], s_slab2[hc]], psum=s_pso[o],
                                first=(hc == 0), last=(hc == 3))
                        if e_i == 7:
                            bk.emit("vector", lambda e, tt=tt, ccol=ccol, o=o:
                                    e.scalar_tensor_tensor(
                                    out=cm.xa[tt][:, ccol * 512:(ccol + 1) * 512],
                                    in0=pso[o][:], scalar=0.5,
                                    in1=cm.xa[tt][:, ccol * 512:(ccol + 1) * 512],
                                    op0=ALU.mult, op1=ALU.add),
                                    reads=[s_pso[o], cm.s_xa[tt]], writes=[cm.s_xa[tt]])
                        else:
                            bk.emit("vector", lambda e, tt=tt, ccol=ccol, o=o, e_i=e_i, gt=gt:
                                    e.scalar_tensor_tensor(
                                    out=cm.xa[tt][:, ccol * 512:(ccol + 1) * 512],
                                    in0=pso[o][:], scalar=wcol[tt][:, e_i:e_i + 1],
                                    in1=cm.xa[tt][:, ccol * 512:(ccol + 1) * 512],
                                    op0=ALU.mult, op1=ALU.add),
                                    reads=[s_pso[o], cm.s_xa[tt], s_wcol[tt]],
                                    writes=[cm.s_xa[tt]])
        cm.store_xacc(xo_d, tc)
    cm.finish_x()





def S(n=""):
    return Slot(n)


EW = 256


def emit_l1attn_casts(bk, W):
    s_cast = {}
    for nm in ("q", "k", "v", "o"):
        sem = bk.sem(f"a1_c{nm}")
        s_cast[nm] = S()
        bk.emit("gpsimd", lambda e, nm=nm: e.dma_start(
            out=W[nm + "b"], in_=W["W" + nm]), writes=[s_cast[nm]], dma=sem)
        bk.seal(sem, [s_cast[nm]])
    return s_cast


def emit_l1attn(nc, bk, cm, W, zscr_d, x_d, xo_d, arena, s_cast):
    sb, ps = bk.sb, bk.ps
    s_zscr = [S(), S()]
    NTT = TCW // 128
    NTH = TCW // 512
    NEP = C // EW
    NEB = EW // 128
    denom = float(C) ** 0.5 / float(T)

    cm.ln_pass(x_d, zscr_d, s_zscr)

    zT = [arena.alloc(f"a1_zT{c}", [128, TCW], BF16) for c in range(NCH)]
    s_zT = [S() for _ in range(NCH)]
    wq = [arena.alloc(f"a1_wq{c}", [128, EW], BF16) for c in range(NCH)]
    wk = [arena.alloc(f"a1_wk{c}", [128, EW], BF16) for c in range(NCH)]
    wv = [arena.alloc(f"a1_wv{c}", [128, EW], BF16) for c in range(NCH)]
    s_wq = [S() for _ in range(NCH)]
    s_wk = [S() for _ in range(NCH)]
    s_wv = [S() for _ in range(NCH)]
    wo = [arena.alloc(f"a1_wo{b}", [128, C], BF16) for b in range(NEB)]
    s_wo = [S() for _ in range(NEB)]
    qf = [arena.alloc(f"a1_qf{b}", [128, TCW], F32) for b in range(NEB)]
    kf = [arena.alloc(f"a1_kf{b}", [128, TCW], F32) for b in range(NEB)]
    vf = [arena.alloc(f"a1_vf{b}", [128, TCW], F32) for b in range(NEB)]
    kvt = [arena.alloc(f"a1_kvt{b}", [128, TCW], F32) for b in range(NEB)]
    vhb = [arena.alloc(f"a1_vhb{b}", [128, TCW], BF16) for b in range(NEB)]
    s_qf = [S() for _ in range(NEB)]
    s_kf = [S() for _ in range(NEB)]
    s_vf = [S() for _ in range(NEB)]
    s_kvt = [S() for _ in range(NEB)]
    s_vhb = [S() for _ in range(NEB)]
    zeros = arena.alloc("a1_zeros", [128, TCW], F32)
    s_zeros = S()
    state = sb("a1_state", [128, NCH], F32)
    s_state = [S() for _ in range(NCH)]
    bk.emit("vector", lambda e: e.memset(zeros[:], 0.0), writes=[s_zeros])
    st_init = bk.emit("vector", lambda e: e.memset(state[:], 0.0), writes=s_state)
    psq = cm.pp[0:3]
    s_psq = cm.s_pp[0:3]
    pso = cm.pp[4:6]
    s_pso = cm.s_pp[4:6]
    d_zt = bk.sem("a1_zt")
    d_w = [bk.sem(f"a1_dw{j}") for j in range(2)]
    d_wo = [bk.sem(f"a1_dwo{j}") for j in range(2)]

    wqbT = {"q": W["qb"], "k": W["kb"], "v": W["vb"]}

    for tc in range(NTCOL):
        cm.load_xacc(x_d, tc)
        for c in range(NCH):
            bk.emit("scalar", lambda e, c=c, tc=tc: e.dma_start_transpose(
                out=zT[c][:], in_=zscr_d[tc * TCW:(tc + 1) * TCW, c * 128:(c + 1) * 128]),
                reads=s_zscr, writes=[s_zT[c]], dma=d_zt)
        bk.seal(d_zt, s_zT)
        for ep in range(NEP):
            par = ep % 2
            semw = d_w[par]
            for c in range(NCH):
                for nm, buf, slot in (("q", wq, s_wq), ("k", wk, s_wk), ("v", wv, s_wv)):
                    bk.emit("scalar", lambda e, c=c, ep=ep, nm=nm, buf=buf: e.dma_start_transpose(
                        out=buf[c][:], in_=wqbT[nm][ep * EW:(ep + 1) * EW, c * 128:(c + 1) * 128]),
                        reads=[s_cast[nm]], writes=[slot[c]], dma=semw)
            bk.seal(semw, s_wq + s_wk + s_wv)
            semo = d_wo[par]
            for b in range(NEB):
                bk.emit("scalar", lambda e, b=b, ep=ep: e.dma_start_transpose(
                    out=wo[b][:], in_=W["ob"][:, (ep * NEB + b) * 128:(ep * NEB + b + 1) * 128]),
                    reads=[s_cast["o"]], writes=[s_wo[b]], dma=semo)
            bk.seal(semo, s_wo)
            for b in range(NEB):
                for proj, (buf, slot, fbuf, fslot) in enumerate(
                        ((wq, s_wq, qf, s_qf), (wk, s_wk, kf, s_kf), (wv, s_wv, vf, s_vf))):
                    pp = proj
                    for th in range(NTH):
                        for c in range(NCH):
                            bk.mm(lambda e, b=b, th=th, c=c, pp=pp, buf=buf: e.matmul(
                                psq[pp][:], buf[c][:, b * 128:(b + 1) * 128],
                                zT[c][:, th * 512:(th + 1) * 512],
                                start=(c == 0), stop=(c == NCH - 1)),
                                reads=[slot[c], s_zT[c]], psum=s_psq[pp],
                                first=(c == 0), last=(c == NCH - 1))
                        bk.emit("vector", lambda e, b=b, th=th, pp=pp, fbuf=fbuf: e.tensor_copy(
                            out=fbuf[b][:, th * 512:(th + 1) * 512], in_=psq[pp][:]),
                            reads=[s_psq[pp]], writes=[fslot[b]])
                gc = ep * NEB + b
                bk.emit("vector", lambda e, b=b: e.tensor_tensor(out=kf[b][:], in0=kf[b][:],
                        in1=vf[b][:], op=ALU.mult), reads=[s_kf[b], s_vf[b]], writes=[s_kf[b]])
                bk.emit("vector", lambda e, b=b, gc=gc: e.tensor_tensor_scan(
                        out=kvt[b][:], data0=kf[b][:], data1=zeros[:],
                        initial=state[:, gc:gc + 1], op0=ALU.add, op1=ALU.add),
                        reads=[s_kf[b], s_zeros, s_state[gc]], writes=[s_kvt[b]])
                bk.emit("vector", lambda e, b=b, gc=gc: e.tensor_copy(
                        out=state[:, gc:gc + 1], in_=kvt[b][:, TCW - 1:TCW]),
                        reads=[s_kvt[b]], writes=[s_state[gc]])
                bk.emit("vector", lambda e, b=b: e.scalar_tensor_tensor(
                        out=vhb[b][:], in0=kvt[b][:], scalar=denom, in1=qf[b][:],
                        op0=ALU.mult, op1=ALU.mult),
                        reads=[s_kvt[b], s_qf[b]], writes=[s_vhb[b]])
            for tt in range(NTT):
                for ccol in range(C // 512):
                    o = (tt * (C // 512) + ccol) % 2
                    for b in range(NEB):
                        bk.mm(lambda e, tt=tt, ccol=ccol, o=o, b=b: e.matmul(
                            pso[o][:], vhb[b][:, tt * 128:(tt + 1) * 128],
                            wo[b][:, ccol * 512:(ccol + 1) * 512],
                            start=(b == 0), stop=(b == NEB - 1)),
                            reads=[s_vhb[b], s_wo[b]], psum=s_pso[o],
                            first=(b == 0), last=(b == NEB - 1))
                    bk.emit("vector", lambda e, tt=tt, ccol=ccol, o=o: e.tensor_add(
                        out=cm.xa[tt][:, ccol * 512:(ccol + 1) * 512],
                        in0=cm.xa[tt][:, ccol * 512:(ccol + 1) * 512], in1=pso[o][:]),
                        reads=[s_pso[o], cm.s_xa[tt]], writes=[cm.s_xa[tt]])
        cm.store_xacc(xo_d, tc)
    cm.finish_x()





def S(n=""):
    return Slot(n)


GATE_OFFSET = 6.906768
NKV = 8
NQH = 16
LN_C = math.log(1.0 / math.sqrt(128.0))


def emit_l0attn_casts(bk, W):
    s_cast = {}
    for nm, src in (("qb", "Wq"), ("kb", "Wk"), ("vb", "Wv")):
        sem = bk.sem(f"a0_c{nm}")
        s_cast[nm] = S()
        bk.emit("gpsimd", lambda e, nm=nm, src=src: e.dma_start(
            out=W[nm], in_=W[src]), writes=[s_cast[nm]], dma=sem)
        bk.seal(sem, [s_cast[nm]])
    return s_cast


def emit_l0attn(nc, bk, cm, W, zscr_d, x_d, xo_d, arena, s_cast, bx=()):
    sb, ps = bk.sb, bk.ps
    s_zscr = [S(), S()]
    NTT = TCW // 128
    NTH = TCW // 512

    cm.ln_pass(x_d, zscr_d, s_zscr)

    pp = cm.pp[0:8]
    s_pp = cm.s_pp[0:8]
    ps_g = cm.pp[6][:, 0:8]
    s_psg = cm.s_pp[6]
    ps_n = cm.pp[7][0:1, :]
    s_psn = cm.s_pp[7]

    onecolf = arena.alloc("a0_one", [128, 1], F32)
    s_onecolf = S()
    bk.emit("vector", lambda e: e.memset(onecolf[:], 1.0), writes=[s_onecolf])
    lncol = arena.alloc("a0_lnc", [128, 1], F32)
    s_lncol = S()
    bk.emit("vector", lambda e: e.memset(lncol[:], LN_C), writes=[s_lncol])
    row1 = arena.alloc("a0_row1", [1, T], F32)
    s_row1 = S()
    logg_tm = [arena.alloc(f"a0_lg{t}", [128, 8], F32) for t in range(NT)]
    s_logg_tm = [S() for _ in range(NT)]
    sqes_tm = [arena.alloc(f"a0_se{t}", [128, 8], F32) for t in range(NT)]
    s_sqes_tm = [S() for _ in range(NT)]
    gu = arena.alloc("a0_gu", [128, 8], F32)
    ga = arena.alloc("a0_ga", [128, 8], F32)
    gl = arena.alloc("a0_gl", [128, 8], F32)
    s_gu, s_ga, s_gl = S(), S(), S()
    lgpre = arena.alloc("a0_lgpre", [8, T], F32)
    lgcum = arena.alloc("a0_lgcum", [8, T], F32)
    s_lgpre, s_lgcum = S(), S()
    ebuf = [arena.alloc(f"a0_ebuf{j}", [128, 512], BF16) for j in range(4)]
    s_ebuf = [S() for _ in range(4)]
    d_st = [bk.sem(f"a0_dst{j}") for j in range(4)]
    s_qscr, s_kscr, s_vscr = S(), S(), S()
    ecnt = [0]

    wgf = [arena.alloc(f"a0_wgf{c}", [128, 8], F32) for c in range(NCH)]
    wgb = [arena.alloc(f"a0_wgb{c}", [128, 8], BF16) for c in range(NCH)]
    s_wgf = [S() for _ in range(NCH)]
    s_wgb = [S() for _ in range(NCH)]
    d_wg = bk.sem("a0_wg")
    wg3 = W["Wg"].rearrange("(c p) g -> c p g", p=128)
    for c in range(NCH):
        bk.emit("sync", lambda e, c=c: e.dma_start(out=wgf[c][:, 0:8], in_=wg3[c]),
                writes=[s_wgf[c]], dma=d_wg)
    bk.seal(d_wg, s_wgf)
    for c in range(NCH):
        bk.emit("vector", lambda e, c=c: e.tensor_copy(out=wgb[c][:], in_=wgf[c][:]),
                reads=[s_wgf[c]], writes=[s_wgb[c]])

    arena = arena.child("a0_sub")

    def evac_store(u, dst_ap, wslot):
        j = ecnt[0] % 4
        ecnt[0] += 1
        bk.emit("vector", lambda e, u=u, j=j: e.tensor_copy(out=ebuf[j][:], in_=pp[u][:]),
                reads=[s_pp[u]], writes=[s_ebuf[j]])
        bk.emit("sync", lambda e, j=j, dst_ap=dst_ap: e.dma_start(out=dst_ap, in_=ebuf[j][:]),
                reads=[s_ebuf[j]], writes=[wslot], dma=d_st[j])

    zT = [arena.alloc(f"zT{c}", [128, TCW], BF16) for c in range(NCH)]
    s_zT = [S() for _ in range(NCH)]
    wqsl = [arena.alloc(f"wqsl{j}", [128, 2048], BF16) for j in range(2)]
    s_wqsl = [S() for _ in range(2)]
    wksl = [arena.alloc(f"wksl{j}", [128, 2048], BF16) for j in range(2)]
    s_wksl = [S() for _ in range(2)]
    wvsla = arena.alloc("wvsl", [128, NCH * 512], BF16)
    wvsl = [wvsla[:, c * 512:(c + 1) * 512] for c in range(NCH)]
    s_wvsla = S()
    s_wvsl = [s_wvsla] * NCH
    d_zt = bk.sem("a0_zt")
    d_wq = [bk.sem(f"a0_dwq{j}") for j in range(2)]
    d_wv = bk.sem("a0_dwv")
    wqb3 = W["Wq"].rearrange("(c p) e -> p c e", p=128)
    wkb3 = W["Wk"].rearrange("(c p) e -> p c e", p=128)
    wvb3 = W["Wv"].rearrange("(c p) e -> p c e", p=128)

    for tc in range(NTCOL):
        bk.scope_mark(f"a0_proj{tc}")
        for c in range(NCH):
            bk.emit("scalar", lambda e, c=c, tc=tc: e.dma_start_transpose(
                out=zT[c][:], in_=zscr_d[tc * TCW:(tc + 1) * TCW, c * 128:(c + 1) * 128]),
                reads=s_zscr, writes=[s_zT[c]], dma=d_zt)
        bk.seal(d_zt, s_zT)
        for qb in range(16):
            j = qb % 2
            bk.emit("gpsimd", lambda e, qb=qb, j=j: e.dma_start(
                out=wqsl[j].rearrange("p (c e) -> p c e", e=128),
                in_=wqb3[:, :, qb * 128:(qb + 1) * 128]),
                writes=[s_wqsl[j]], dma=d_wq[j])
            bk.seal(d_wq[j], [s_wqsl[j]])
            for th in range(NTH):
                u = (qb * NTH + th) % 2
                for c in range(NCH):
                    bk.mm(lambda e, th=th, c=c, u=u, j=j: e.matmul(
                        pp[u][:], wqsl[j][:, c * 128:(c + 1) * 128],
                        zT[c][:, th * 512:(th + 1) * 512],
                        start=(c == 0), stop=(c == NCH - 1)),
                        reads=[s_wqsl[j], s_zT[c]], psum=s_pp[u],
                        first=(c == 0), last=(c == NCH - 1))
                evac_store(u, W["qscr"][qb * 128:(qb + 1) * 128,
                           tc * TCW + th * 512:tc * TCW + (th + 1) * 512], s_qscr)
        for kb in range(8):
            j = kb % 2
            bk.emit("gpsimd", lambda e, kb=kb, j=j: e.dma_start(
                out=wksl[j].rearrange("p (c e) -> p c e", e=128),
                in_=wkb3[:, :, kb * 128:(kb + 1) * 128]),
                writes=[s_wksl[j]], dma=d_wq[j])
            bk.seal(d_wq[j], [s_wksl[j]])
            for th in range(NTH):
                u = (kb * NTH + th) % 2
                for c in range(NCH):
                    bk.mm(lambda e, th=th, c=c, u=u, j=j: e.matmul(
                        pp[u][:], wksl[j][:, c * 128:(c + 1) * 128],
                        zT[c][:, th * 512:(th + 1) * 512],
                        start=(c == 0), stop=(c == NCH - 1)),
                        reads=[s_wksl[j], s_zT[c]], psum=s_pp[u],
                        first=(c == 0), last=(c == NCH - 1))
                evac_store(u, W["kscr"][kb * 128:(kb + 1) * 128,
                           tc * TCW + th * 512:tc * TCW + (th + 1) * 512], s_kscr)
        for vh in range(2):
            bk.emit("gpsimd", lambda e, vh=vh: e.dma_start(
                out=wvsla.rearrange("p (c e) -> p c e", e=512),
                in_=wvb3[:, :, vh * 512:(vh + 1) * 512]),
                writes=[s_wvsla], dma=d_wv)
            bk.seal(d_wv, [s_wvsla])
            for tt in range(NTT):
                gt = tc * NTT + tt
                u = (vh * NTT + tt) % 2 + 2
                for c in range(NCH):
                    bk.mm(lambda e, tt=tt, c=c, u=u: e.matmul(
                        pp[u][:], zT[c][:, tt * 128:(tt + 1) * 128], wvsl[c][:],
                        start=(c == 0), stop=(c == NCH - 1)),
                        reads=[s_wvsl[c], s_zT[c]], psum=s_pp[u],
                        first=(c == 0), last=(c == NCH - 1))
                evac_store(u, W["vscr"][gt * 128:(gt + 1) * 128,
                           vh * 512:(vh + 1) * 512], s_vscr)
        for tt in range(NTT):
            gt = tc * NTT + tt
            for c in range(NCH):
                bk.mm(lambda e, tt=tt, c=c: e.matmul(
                    ps_g, zT[c][:, tt * 128:(tt + 1) * 128], wgb[c][:],
                    start=(c == 0), stop=(c == NCH - 1)),
                    reads=[s_wgb[c], s_zT[c]], psum=s_psg,
                    first=(c == 0), last=(c == NCH - 1))
            bk.emit("vector", lambda e: e.tensor_copy(out=gu[:], in_=ps_g),
                    reads=[s_psg], writes=[s_gu])
            bk.emit("vector", lambda e: e.tensor_scalar(out=gu[:], in0=gu[:],
                    scalar1=GATE_OFFSET, scalar2=None, op0=ALU.add),
                    reads=[s_gu], writes=[s_gu])
            bk.emit("scalar", lambda e: e.activation(out=ga[:], in_=gu[:],
                    func=AF.Abs), reads=[s_gu], writes=[s_ga])
            bk.emit("scalar", lambda e: e.activation(out=ga[:], in_=ga[:],
                    func=AF.Exp, scale=-1.0), reads=[s_ga], writes=[s_ga])
            bk.emit("scalar", lambda e: e.activation(out=ga[:], in_=ga[:],
                    func=AF.Ln, bias=onecolf[:, 0:1]),
                    reads=[s_ga, s_onecolf], writes=[s_ga])
            bk.emit("vector", lambda e: e.tensor_scalar(out=gl[:], in0=gu[:],
                    scalar1=-1.0, scalar2=0.0, op0=ALU.mult, op1=ALU.max),
                    reads=[s_gu], writes=[s_gl])
            bk.emit("vector", lambda e, gt=gt: e.scalar_tensor_tensor(
                    out=logg_tm[gt][:], in0=gl[:], scalar=-1.0, in1=ga[:],
                    op0=ALU.mult, op1=ALU.subtract),
                    reads=[s_gl, s_ga], writes=[s_logg_tm[gt]])

    bk.scope_mark("a0_logg")
    for t in range(NT):
        u = t % 2
        bk.mm(lambda e, t=t, u=u: e.transpose(pp[u][0:8, 0:128], logg_tm[t][:],
              cm.identf[:]), reads=[s_logg_tm[t], cm.s_identf], psum=s_pp[u])
        bk.emit("vector", lambda e, t=t, u=u: e.tensor_copy(
            out=lgpre[:, t * 128:(t + 1) * 128], in_=pp[u][0:8, 0:128]),
            reads=[s_pp[u]], writes=[s_lgpre])
    bk.emit("vector", lambda e: e.tensor_tensor_scan(
        out=lgcum[:], data0=lgpre[:], data1=lgpre[:], initial=0.0,
        op0=ALU.add, op1=ALU.bypass), reads=[s_lgpre], writes=[s_lgcum])
    d_lg = bk.sem("a0_dlg")
    s_loggscr = S()
    et8 = arena.alloc("a0_et8", [8, T], BF16)
    s_et8 = S()
    bk.emit("scalar", lambda e: e.activation(out=et8[:], in_=lgcum[:], func=AF.Exp),
            reads=[s_lgcum], writes=[s_et8])
    bk.emit("sync", lambda e: e.dma_start(out=W["eta"], in_=et8[:]),
            reads=[s_et8], writes=[s_loggscr], dma=d_lg)
    bk.seal(d_lg, [s_loggscr])
    for t in range(NT):
        u = t % 2
        bk.mm(lambda e, t=t, u=u: e.transpose(pp[u][0:128, 0:8],
              lgcum[:, t * 128:(t + 1) * 128], cm.identf[0:8, 0:8]),
              reads=[s_lgcum, cm.s_identf], psum=s_pp[u])
        bk.emit("scalar", lambda e, t=t, u=u: e.activation(out=sqes_tm[t][:],
                in_=pp[u][0:128, 0:8], func=AF.Exp, scale=-0.5, bias=lncol[:, 0:1]),
                reads=[s_pp[u], s_lncol], writes=[s_sqes_tm[t]])

    bk.barrier(exclude=bx)
    arena.reset()
    qh = arena.alloc("qh", [128, T], BF16)
    kh = arena.alloc("kh", [128, T], BF16)
    vh = [arena.alloc(f"vh{t}", [128, 128], BF16) for t in range(NT)]
    yh = arena.alloc("yh", [128, T], BF16)
    eb = arena.alloc("eb", [128, T], BF16)
    sq = [arena.alloc(f"sq{j}", [128, 512], BF16) for j in range(8)]
    at = [arena.alloc(f"at{j}", [128, 512], BF16) for j in range(8)]
    mskt = [arena.alloc(f"msk{j}", [128, 512], BF16) for j in range(4)]
    etm = arena.alloc("etm", [128, 16 * 512], BF16)
    nrow = arena.alloc("nrow", [1, 512], F32)
    nrowb = [arena.alloc(f"nrowb{j}", [1, 512], BF16) for j in range(2)]
    rbb = [arena.alloc(f"rbb{j}", [128, 512], BF16) for j in range(2)]
    s_rnscr = [S(), S()]
    d_rn = [bk.sem(f"a0_drn{j}") for j in range(2)]
    d_rb = [bk.sem(f"a0_drb{j}") for j in range(2)]
    s_qh, s_kh, s_yh, s_eb = S(), S(), S(), S()
    s_vh = [S() for _ in range(NT)]
    s_sq = [S() for _ in range(8)]
    s_at = [S() for _ in range(8)]
    s_mskt = [S() for _ in range(4)]
    s_etm = S()
    s_nrow = S()
    s_nrowb = [S(), S()]
    s_rbb = [S(), S()]
    d_qh = [bk.sem(f"a0_dqh{j}") for j in range(2)]
    d_kh = bk.sem("a0_dkh")
    d_vh = bk.sem("a0_dvh")
    d_er = bk.sem("a0_der")
    d_ys = [bk.sem(f"a0_dys{j}") for j in range(2)]
    s_yscr = S()

    for j in range(4):
        bk.emit("vector", lambda e, j=j: e.memset(mskt[j][:], 1.0), writes=[s_mskt[j]])
        bk.emit("gpsimd", lambda e, j=j: e.affine_select(
            out=mskt[j][:], in_=mskt[j][:], pattern=[[1, 512]],
            compare_op=ALU.is_ge, fill=0.0, base=-128 * j, channel_multiplier=-1),
            reads=[s_mskt[j]], writes=[s_mskt[j]])
    for h in range(NQH):
        bk.scope_mark(f"a0_head{h}")
        hk = h // 2
        if h % 2 == 0:
            bk.emit("sync", lambda e, hk=hk: e.dma_start(
                out=kh[:], in_=W["kscr"][hk * 128:(hk + 1) * 128, :]),
                reads=[s_kscr], writes=[s_kh], dma=d_kh)
            bk.seal(d_kh, [s_kh])
            for t in range(NT):
                bk.emit("sync", lambda e, t=t, hk=hk: e.dma_start(
                    out=vh[t][:], in_=W["vscr"][t * 128:(t + 1) * 128,
                                                hk * 128:(hk + 1) * 128]),
                    reads=[s_vscr], writes=[s_vh[t]], dma=d_vh)
            bk.seal(d_vh, s_vh)
            bk.emit("sync", lambda e, hk=hk: e.dma_start(
                out=eb[:], in_=W["eta"][hk:hk + 1, :].partition_broadcast(128)),
                reads=[s_loggscr], writes=[s_eb], dma=d_er)
            bk.seal(d_er, [s_eb])
            for tq in range(4):
                for j in range(4):
                    bk.emit("vector", lambda e, tq=tq, j=j: e.tensor_tensor(
                        out=etm[:, (tq * 4 + j) * 512:(tq * 4 + j + 1) * 512],
                        in0=eb[:, tq * 512:(tq + 1) * 512], in1=mskt[j][:],
                        op=ALU.mult), reads=[s_eb, s_mskt[j]], writes=[s_etm])
        bk.emit("sync", lambda e, h=h: e.dma_start(
            out=qh[:], in_=W["qscr"][h * 128:(h + 1) * 128, :]),
            reads=[s_qscr], writes=[s_qh], dma=d_qh[h % 2])
        bk.seal(d_qh[h % 2], [s_qh])
        spool = (0, 1, 3, 4)
        DIST = 6
        tail = [None]
        for tq in range(4):
            nsb = 4 * tq + 4
            avb = 2 if tq % 2 == 0 else 5
            nrb = 7 if tq % 2 == 0 else 6
            psn_t, s_psn_t = pp[nrb][0:1, :], s_pp[nrb]

            def emit_qk(sbk, tq=tq):
                u = spool[sbk % 4]
                b4 = sbk % 8
                bk.mm(lambda e, sbk=sbk, tq=tq, u=u: e.matmul(
                    pp[u][:], kh[:, sbk * 128:(sbk + 1) * 128],
                    qh[:, tq * 512:(tq + 1) * 512], start=True, stop=True),
                    reads=[s_kh, s_qh], psum=s_pp[u])
                bk.emit("scalar", lambda e, sbk=sbk, u=u, b4=b4, hk=hk: e.activation(
                    out=sq[b4][:], in_=pp[u][:], func=AF.Square,
                    scale=sqes_tm[sbk][:, hk:hk + 1]),
                    reads=[s_pp[u], s_sqes_tm[sbk]], writes=[s_sq[b4]])
                j = sbk - 4 * tq
                esrc = (etm[:, (tq * 4 + j) * 512:(tq * 4 + j + 1) * 512], s_etm) \
                    if j >= 0 else (eb[:, tq * 512:(tq + 1) * 512], s_eb)
                bk.emit("vector", lambda e, b4=b4, esrc=esrc: e.tensor_tensor(
                    out=at[b4][:], in0=sq[b4][:], in1=esrc[0],
                    op=ALU.mult), reads=[s_sq[b4], esrc[1]], writes=[s_at[b4]])

            def emit_nav(sbk, nsb=nsb):
                b4 = sbk % 8
                bk.mm(lambda e, sbk=sbk, b4=b4, nsb=nsb, psn_t=psn_t: e.matmul(
                    psn_t, cm.onecol[:], at[b4][:],
                    start=(sbk == 0), stop=(sbk == nsb - 1)),
                    reads=[cm.s_onecol, s_at[b4]], psum=s_psn_t,
                    first=(sbk == 0), last=(sbk == nsb - 1))
                bk.mm(lambda e, sbk=sbk, b4=b4, nsb=nsb, avb=avb: e.matmul(
                    pp[avb][:], vh[sbk][:], at[b4][:],
                    start=(sbk == 0), stop=(sbk == nsb - 1)),
                    reads=[s_vh[sbk], s_at[b4]], psum=s_pp[avb],
                    first=(sbk == 0), last=(sbk == nsb - 1))

            for sbk in range(nsb):
                emit_qk(sbk)
                if sbk == 2 and tail[0] is not None:
                    tail[0]()
                    tail[0] = None
                if sbk >= DIST:
                    emit_nav(sbk - DIST)
            if tail[0] is not None:
                tail[0]()
                tail[0] = None
            for sbk in range(max(0, nsb - DIST), nsb):
                emit_nav(sbk)
            bk.emit("vector", lambda e, psn_t=psn_t: e.tensor_scalar(out=nrow[:],
                    in0=psn_t, scalar1=1e-6, scalar2=None, op0=ALU.add),
                    reads=[s_psn_t], writes=[s_nrow])
            bk.emit("scalar", lambda e: e.activation(out=nrow[:], in_=nrow[:],
                    func=AF.Ln), reads=[s_nrow], writes=[s_nrow])
            bk.emit("scalar", lambda e: e.activation(out=nrow[:], in_=nrow[:],
                    func=AF.Exp, scale=-1.0), reads=[s_nrow], writes=[s_nrow])
            bk.emit("vector", lambda e, tq=tq: e.tensor_copy(out=nrowb[tq % 2][:],
                    in_=nrow[:]), reads=[s_nrow], writes=[s_nrowb[tq % 2]])

            def emit_bcast_ymult(tq=tq, avb=avb, nrb=nrb):
                bk.mm(lambda e, nrb=nrb, tq=tq: e.matmul(pp[nrb][:], cm.onesr[:],
                      nrowb[tq % 2][:], start=True, stop=True),
                      reads=[cm.s_onesr, s_nrowb[tq % 2]], psum=s_pp[nrb])
                bk.emit("vector", lambda e, nrb=nrb, tq=tq: e.tensor_copy(
                        out=rbb[tq % 2][:], in_=pp[nrb][:]),
                        reads=[s_pp[nrb]], writes=[s_rbb[tq % 2]])
                bk.emit("vector", lambda e, tq=tq, avb=avb: e.tensor_tensor(
                        out=yh[:, tq * 512:(tq + 1) * 512], in0=pp[avb][:],
                        in1=rbb[tq % 2][:], op=ALU.mult),
                        reads=[s_pp[avb], s_rbb[tq % 2]], writes=[s_yh])
            tail[0] = emit_bcast_ymult
        if tail[0] is not None:
            tail[0]()
            tail[0] = None
        bk.emit("sync", lambda e, h=h: e.dma_start(
            out=W["yscr"][h * 128:(h + 1) * 128, :], in_=yh[:]),
            reads=[s_yh], writes=[s_yscr], dma=d_ys[h % 2])
        bk.seal(d_ys[h % 2], [s_yscr])

    bk.barrier(exclude=bx)
    arena.reset()
    ysl = [arena.alloc(f"ysl{q}", [128, TCW], BF16) for q in range(16)]
    s_ysl = [S() for _ in range(16)]
    wcs = [arena.alloc(f"wcs{q}", [128, 512], BF16) for q in range(16)]
    s_wcs = [S() for _ in range(16)]
    bcb = arena.alloc("bcb", [128, C], F32)
    s_bcb = S()
    d_bc = bk.sem("a0_dbc")
    d_ysl = [bk.sem(f"a0_dysl{j}") for j in range(2)]
    d_wcs = [bk.sem(f"a0_dwcs{j}") for j in range(2)]
    bk.emit("sync", lambda e: e.dma_start(out=row1[:],
            in_=W["bc"].rearrange("(b a) -> b a", b=1)), writes=[s_row1], dma=d_bc)
    bk.seal(d_bc, [s_row1])
    for q4 in range(4):
        bk.mm(lambda e, q4=q4: e.matmul(pp[5][:], cm.onesrf[:],
              row1[:, q4 * 512:(q4 + 1) * 512], start=True, stop=True),
              reads=[cm.s_onesrf, s_row1], psum=s_pp[5])
        bk.emit("vector", lambda e, q4=q4: e.tensor_copy(
                out=bcb[:, q4 * 512:(q4 + 1) * 512], in_=pp[5][:]),
                reads=[s_pp[5]], writes=[s_bcb])
    wc3 = W["Wc"].rearrange("(q p) c -> q p c", p=128)
    for tc in range(NTCOL):
        bk.scope_mark(f"a0_wc{tc}")
        cm.load_xacc(x_d, tc)
        for tt in range(NTT):
            bk.emit("vector", lambda e, tt=tt: e.tensor_add(out=cm.xa[tt][:],
                    in0=cm.xa[tt][:], in1=bcb[:]),
                    reads=[s_bcb, cm.s_xa[tt]], writes=[cm.s_xa[tt]])
        sem = d_ysl[tc % 2]
        for q in range(16):
            bk.emit("sync", lambda e, q=q, tc=tc: e.dma_start(
                out=ysl[q][:], in_=W["yscr"][q * 128:(q + 1) * 128,
                                             tc * TCW:(tc + 1) * TCW]),
                reads=[s_yscr], writes=[s_ysl[q]], dma=sem)
        bk.seal(sem, s_ysl)
        for ccol in range(C // 512):
            semw = d_wcs[ccol % 2]
            for q in range(16):
                bk.emit("gpsimd", lambda e, q=q, ccol=ccol: e.dma_start(
                    out=wcs[q][:], in_=wc3[q][:, ccol * 512:(ccol + 1) * 512]),
                    writes=[s_wcs[q]], dma=semw)
            bk.seal(semw, s_wcs)
            for tt in range(NTT):
                o = tt % 2
                for q in range(16):
                    bk.mm(lambda e, q=q, tt=tt, o=o: e.matmul(
                        pp[o][:], ysl[q][:, tt * 128:(tt + 1) * 128], wcs[q][:],
                        start=(q == 0), stop=(q == 15)),
                        reads=[s_ysl[q], s_wcs[q]], psum=s_pp[o],
                        first=(q == 0), last=(q == 15))
                bk.emit("vector", lambda e, tt=tt, ccol=ccol, o=o: e.tensor_add(
                    out=cm.xa[tt][:, ccol * 512:(ccol + 1) * 512],
                    in0=cm.xa[tt][:, ccol * 512:(ccol + 1) * 512], in1=pp[o][:]),
                    reads=[s_pp[o], cm.s_xa[tt]], writes=[cm.s_xa[tt]])
        cm.store_xacc(xo_d, tc)
    cm.finish_x()





def build_full_program():
    nc = bass.Bass("TRN2", target_bir_lowering=False)
    d = lambda name, shape, dt, kind: nc.dram_tensor(name, shape, dt, kind=kind).ap()
    x_in = d("x", [T, C], F32, "ExternalInput")
    xo_d = d("xo", [T, C], F32, "ExternalOutput")
    xA = d("xA", [T, C], F32, "Internal")
    xB = d("xB", [T, C], F32, "Internal")
    zscr_d = d("zscr", [T, C], BF16, "Internal")
    Wf = {"up": d("f_up", [HF, C], F32, "ExternalInput"),
          "dn": d("f_dn", [C, HF], F32, "ExternalInput"),
          "us": d("f_us", [HF], F32, "ExternalInput"),
          "ds": d("f_ds", [C], F32, "ExternalInput"),
          "os": d("f_os", [1], F32, "ExternalInput"),
          "upb": d("f_upb", [HF, C], BF16, "Internal"),
          "dnb": d("f_dnb", [C, HF], BF16, "Internal"),
          "upbs": d("f_upbs", [HF, C], BF16, "Internal"),
          "dnbs": d("f_dnbs", [C, HF], BF16, "Internal")}
    Wa0 = {"Wq": d("a0Wq", [C, 2048], F32, "ExternalInput"),
           "Wk": d("a0Wk", [C, 1024], F32, "ExternalInput"),
           "Wv": d("a0Wv", [C, 1024], F32, "ExternalInput"),
           "Wg": d("a0Wg", [C, 8], F32, "ExternalInput"),
           "Wc": d("a0Wc", [2048, C], F32, "ExternalInput"),
           "bc": d("a0bc", [C], F32, "ExternalInput"),
           "qb": d("a0qb", [C, 2048], BF16, "Internal"),
           "kb": d("a0kb", [C, 1024], BF16, "Internal"),
           "vb": d("a0vb", [C, 1024], BF16, "Internal"),
           "qscr": d("a0qscr", [2048, T], BF16, "Internal"),
           "kscr": d("a0kscr", [1024, T], BF16, "Internal"),
           "vscr": d("a0vscr", [T, 1024], BF16, "Internal"),
           "yscr": d("a0yscr", [2048, T], BF16, "Internal"),
           "logg": d("a0logg", [8, T], F32, "Internal"),
           "eta": d("a0eta", [8, T], BF16, "Internal"),
           "rnscr": d("a0rnscr", [2, 512], BF16, "Internal")}
    Wm = {"W1": d("mW1", [7, C, C], F32, "ExternalInput"),
          "Wg": d("mWg", [7, C, C], F32, "ExternalInput"),
          "W2": d("mW2", [7, C, C], F32, "ExternalInput"),
          "Ws1": d("mWs1", [C, C], F32, "ExternalInput"),
          "Wsg": d("mWsg", [C, C], F32, "ExternalInput"),
          "Ws2": d("mWs2", [C, C], F32, "ExternalInput"),
          "gate_w": d("mgw", [C, 7], F32, "ExternalInput"),
          "gate_b": d("mgb", [7], F32, "ExternalInput"),
          "biases": d("mbias", [7], F32, "ExternalInput"),
          "W1b": d("mW1b", [8, C, C], BF16, "Internal"),
          "Wgb": d("mWgb", [8, C, C], BF16, "Internal"),
          "W2b": d("mW2b", [8, C, C], BF16, "Internal")}
    Wa1 = {"Wq": d("a1Wq", [C, C], F32, "ExternalInput"),
           "Wk": d("a1Wk", [C, C], F32, "ExternalInput"),
           "Wv": d("a1Wv", [C, C], F32, "ExternalInput"),
           "Wo": d("a1Wo", [C, C], F32, "ExternalInput"),
           "qb": d("a1qb", [C, C], BF16, "Internal"),
           "kb": d("a1kb", [C, C], BF16, "Internal"),
           "vb": d("a1vb", [C, C], BF16, "Internal"),
           "ob": d("a1ob", [C, C], BF16, "Internal")}

    bk = BK(nc)
    cm = Common(nc, bk)
    arena = Arena(bk, "ar", 112 * 1024)

    s_upb, s_dnb = emit_l0ffn_prep(bk, cm, Wf)
    s_ca1 = emit_l1attn_casts(bk, Wa1)
    s_ca0 = None
    s_cm = None
    bx = frozenset(bk.sems[n] for n in
                   ["f_cup0", "f_cup1", "f_cup2", "f_cup3",
                    "f_cdn0", "f_cdn1", "f_cdn2", "f_cdn3",
                    "f_su0", "f_su1", "f_su2", "f_su3",
                    "f_sd0", "f_sd1", "f_sd2", "f_sd3", "f_pld",
                    "a1_cq", "a1_ck", "a1_cv", "a1_co"])

    def cut():
        bk.barrier(exclude=bx)
        arena.reset()

    bk.scope_mark("s0_ffn")
    emit_l0ffn(nc, bk, cm, Wf, zscr_d, x_in, xA, arena, s_upb, s_dnb, app=0)
    cut()
    bk.scope_mark("s1_attn0")
    emit_l0attn(nc, bk, cm, Wa0, zscr_d, xA, xB, arena, s_ca0, bx=bx)
    cut()
    bk.scope_mark("s2_ffn")
    emit_l0ffn(nc, bk, cm, Wf, zscr_d, xB, xA, arena, s_upb, s_dnb, app=1)
    cut()
    bk.scope_mark("s3_moe")
    emit_l1moe(nc, bk, cm, Wm, zscr_d, xA, xB, arena, s_cm, app=0)
    cut()
    bk.scope_mark("s4_attn1")
    emit_l1attn(nc, bk, cm, Wa1, zscr_d, xB, xA, arena, s_ca1)
    cut()
    bk.scope_mark("s5_moe")
    emit_l1moe(nc, bk, cm, Wm, zscr_d, xA, xo_d, arena, s_cm, app=1)
    bk.finalize()
    return nc




_PROGS = {}


def _full_prog():
    if "full" not in _PROGS:
        _PROGS["full"] = build_full_program()
    return _PROGS["full"]


def kernel(L0_attn_hidden_states, L0_attn_Wq, L0_attn_Wk, L0_attn_Wv, L0_attn_Wg,
           L0_attn_Wc, L0_attn_bc, L0_ffn_up_weight, L0_ffn_down_weight,
           L0_ffn_up_scale, L0_ffn_down_scale, L0_ffn_output_scale,
           L1_attn_Wq, L1_attn_Wk, L1_attn_Wv, L1_attn_Wo,
           L1_ffn_W1, L1_ffn_Wg, L1_ffn_W2, L1_ffn_Ws1, L1_ffn_Wsg, L1_ffn_Ws2,
           L1_ffn_gate_w, L1_ffn_gate_b, L1_ffn_biases):
    f32 = lambda a: np.ascontiguousarray(np.asarray(a), dtype=np.float32)
    x = f32(L0_attn_hidden_states)
    wmap = {"f_up": f32(L0_ffn_up_weight), "f_dn": f32(L0_ffn_down_weight),
            "f_us": f32(L0_ffn_up_scale), "f_ds": f32(L0_ffn_down_scale),
            "f_os": f32(L0_ffn_output_scale),
            "a0Wq": f32(L0_attn_Wq), "a0Wk": f32(L0_attn_Wk), "a0Wv": f32(L0_attn_Wv),
            "a0Wg": f32(L0_attn_Wg), "a0Wc": f32(L0_attn_Wc), "a0bc": f32(L0_attn_bc),
            "mW1": f32(L1_ffn_W1), "mWg": f32(L1_ffn_Wg), "mW2": f32(L1_ffn_W2),
            "mWs1": f32(L1_ffn_Ws1), "mWsg": f32(L1_ffn_Wsg), "mWs2": f32(L1_ffn_Ws2),
            "mgw": f32(L1_ffn_gate_w), "mgb": f32(L1_ffn_gate_b),
            "mbias": f32(L1_ffn_biases),
            "a1Wq": f32(L1_attn_Wq), "a1Wk": f32(L1_attn_Wk), "a1Wv": f32(L1_attn_Wv),
            "a1Wo": f32(L1_attn_Wo)}
    in_maps = [{**wmap, "x": np.ascontiguousarray(x[b])} for b in range(8)]
    nc = _full_prog()
    res = run_bass_kernel_spmd(nc, in_maps, core_ids=list(range(8)))
    return np.stack([np.asarray(res.results[b]["xo"]) for b in range(8)], axis=0).astype(np.float32)
